# version 10
# baseline (speedup 1.0000x reference)
import math
import numpy as np
import ml_dtypes

import concourse.bass as bass
import concourse.mybir as mybir
from concourse.bass_utils import run_bass_kernel_spmd

F32 = mybir.dt.float32
F16 = mybir.dt.float16
BF16 = mybir.dt.bfloat16
AF = mybir.ActivationFunctionType
ALU = mybir.AluOpType
AX = mybir.AxisListType

S, D, NH, HD, FF, NC = 8192, 2048, 16, 128, 8192, 8
T = S // NC
TH = T + 128
KC = D // 128
FC = FF // 128
QB = T // 128
TC = TH // 128
THETA = 10000.0
NEG = -1.0e30
ATANH_HALF = 0.5493061443340549
NWM = 6 * D
MS = NWM // 128

DBG_PHASE = 0


def _sw(A, name=None):
    Kd, Nd = A.shape
    return np.ascontiguousarray(
        A.reshape(Kd // 128, 128, Nd // 128, 128).transpose(2, 1, 0, 3)
        .reshape(Nd // 128, 128, Kd))


def _host_prepare(x, W_qkv, W_out, b_out, w1, g1, w2, g2, w3, g3, c, wmod):
    f32 = np.float32
    X = np.ascontiguousarray(np.asarray(x, dtype=f32).reshape(S, D))
    Wq = np.asarray(W_qkv, dtype=f32)
    n = np.arange(3 * D)
    kind, r = n // D, n % D
    hh, hd = r // HD, r % HD
    Wq_sw = _sw(Wq[:, hh * (HD * 3) + hd * 3 + kind])
    Wo_sw = _sw(np.asarray(W_out, dtype=f32))
    w1_sw = _sw(np.ascontiguousarray(np.asarray(w1, dtype=f32).T))
    w2_sw = _sw(np.ascontiguousarray(np.asarray(w2, dtype=f32).T))
    w3_sw = _sw(np.ascontiguousarray(np.asarray(w3, dtype=f32).T))
    wmod_sw = _sw(np.asarray(wmod, dtype=f32))
    vecs = dict(
        b_out2=np.asarray(b_out, dtype=f32).reshape(KC, 128).copy(),
        g1_2=np.asarray(g1, dtype=f32).reshape(KC, 128).copy(),
        g2_2=np.asarray(g2, dtype=f32).reshape(KC, 128).copy(),
        g3_2=np.asarray(g3, dtype=f32).reshape(FC, 128).copy(),
        c2=np.asarray(c, dtype=f32).reshape(KC, 128).copy())
    cosg, sing = _rope_tables()
    in_maps = []
    for cix in range(NC):
        start = cix * T
        xs = np.zeros((TH, D), dtype=f32)
        lo = max(0, start - 128)
        xs[128 - (start - lo):] = X[lo:start + T]
        pk = np.clip(np.arange(start - 128, start + T), 0, S - 1)
        cK = np.ascontiguousarray(cosg[pk].T)
        sK = np.ascontiguousarray(sing[pk].T)
        cQ = np.ascontiguousarray(cosg[start:start + T].T)
        sQ = np.ascontiguousarray(sing[start:start + T].T)
        masks = np.full((QB, 128, 256), NEG, dtype=f32)
        rr = np.arange(128)[:, None]
        cc = np.arange(256)[None, :]
        for qb in range(QB):
            i = start + 128 * qb + rr
            j = start - 128 + 128 * qb + cc
            ok = (j >= 0) & (j <= i) & (j >= i - 128)
            masks[qb][ok] = 0.0
        in_maps.append(dict(
            x_sh=np.ascontiguousarray(xs.reshape(TC, 128, D)),
            wq=Wq_sw, wo=Wo_sw, w1=w1_sw, w2=w2_sw, w3=w3_sw, wmod=wmod_sw,
            cosK=cK, sinK=sK, cosQ=cQ, sinQ=sQ,
            masks=masks, **vecs))
    return in_maps


_TBL = {}


def _rope_tables():
    if "c" in _TBL:
        return _TBL["c"], _TBL["s"]
    try:
        import jax
        import jax.numpy as jnp
        with jax.default_device(jax.devices("cpu")[0]):
            inv_freq = 1.0 / THETA ** (jnp.arange(0, HD, 2, dtype=jnp.float32)
                                       / HD)
            tt = jnp.arange(S, dtype=jnp.float32)
            ang = jnp.repeat(tt[:, None] * inv_freq[None, :], 2, axis=-1)
            c, s_ = np.asarray(jnp.cos(ang)), np.asarray(jnp.sin(ang))
    except Exception:
        inv = (np.float32(1.0) / np.power(
            np.float32(THETA),
            np.arange(0, HD, 2).astype(np.float32) / np.float32(HD)))
        ang = np.repeat(np.arange(S, dtype=np.float32)[:, None] * inv[None, :],
                        2, axis=-1)
        c, s_ = np.cos(ang), np.sin(ang)
    _TBL["c"], _TBL["s"] = c, s_
    return c, s_


ENGINES = ("sync", "tensor", "vector", "scalar", "gpsimd")
V, P, A, E, G = "vector", "tensor", "scalar", "sync", "gpsimd"


class K:

    def __init__(self):
        self.q = {e: [] for e in ENGINES}
        self.semv = {}
        self.sems = {}
        self.hi = {e: {} for e in ENGINES}
        self.pend = {e: set() for e in ENGINES}

    def plan(self, name, n=1):
        self.semv[name] = self.semv.get(name, 0) + n
        return self.semv[name]

    def val(self, name):
        return self.semv.get(name, 0)

    def waitt(self, eng, tok):
        if tok is None or tok == 0:
            return
        self.wait(eng, tok[0], tok[1])

    def wait(self, eng, name, value):
        if value is None or value <= 0:
            return
        if self.hi[eng].get(name, -1) >= value:
            return
        self.hi[eng][name] = value
        self.q[eng].append(lambda e, n=name, v=value: e.wait_ge(self.sems[n], v))

    def op(self, eng, fn, reads=(), writes=(), inc=None, incn=1):
        tags = set(reads) | set(writes)
        if tags & self.pend[eng]:
            self.q[eng].append(lambda e: e.drain())
            self.pend[eng].clear()
        tgt = self.plan(inc, incn) if inc else None

        def f(e, fn=fn, inc=inc, incn=incn):
            ins = fn(e)
            if inc is not None:
                ins.then_inc(self.sems[inc], incn)
        self.q[eng].append(f)
        self.pend[eng].update(writes)
        return tgt


def _alloc(nc):
    t = {}

    def din(name, shape, dtype=F32):
        t[name] = nc.dram_tensor(name, shape, dtype, kind="ExternalInput").ap()

    din("x_sh", [TC, 128, D])
    din("wq", [48, 128, D]); din("wo", [KC, 128, D])
    din("w1", [FC, 128, D]); din("w2", [FC, 128, D]); din("w3", [KC, 128, FF])
    din("wmod", [MS, 128, D])
    din("b_out2", [KC, 128]); din("g1_2", [KC, 128]); din("g2_2", [KC, 128])
    din("g3_2", [FC, 128]); din("c2", [KC, 128])
    din("cosK", [128, TH]); din("sinK", [128, TH])
    din("cosQ", [128, T]); din("sinQ", [128, T])
    din("masks", [QB, 128, 256])
    t["out"] = nc.dram_tensor("out", [QB, 128, D], F32, kind="ExternalOutput").ap()
    if DBG_PHASE:
        t["dbg"] = nc.dram_tensor("dbg", [128, 18432], F32, kind="ExternalOutput").ap()
    t["x_park"] = nc.dram_tensor("x_park", [KC, 128, T], F32, kind="Internal").ap()
    t["xm_park"] = nc.dram_tensor("xm_park", [KC, 128, T], F32, kind="Internal").ap()
    t["v_dram"] = nc.dram_tensor("v_dram", [NH, 128, TH], BF16, kind="Internal").ap()
    t["hmid_dr"] = nc.dram_tensor("hmid_dr", [FC, 128, T], F16, kind="Internal").ap()

    t["ident_d"] = nc.inline_tensor(np.eye(128, dtype=np.float32), "ident_d").ap()
    t["identb_d"] = nc.inline_tensor(np.eye(128, dtype=ml_dtypes.bfloat16),
                                     "identb_d").ap()
    Rnp = np.zeros((128, 128), dtype=ml_dtypes.bfloat16)
    for i in range(64):
        Rnp[2 * i + 1, 2 * i] = -1.0
        Rnp[2 * i, 2 * i + 1] = 1.0
    t["R_d"] = nc.inline_tensor(Rnp, "R_d").ap()
    t["ones_d"] = nc.inline_tensor(np.ones((1, 128), dtype=np.float32), "ones_d").ap()
    seln = np.zeros((8, 8 * 128), dtype=np.float32)
    for j in range(8):
        seln[j, j * 128:(j + 1) * 128] = 1.0
    t["sel8_d"] = nc.inline_tensor(seln, "sel8_d").ap()

    arena = nc.alloc_sbuf_tensor("arena", [128, 206848], mybir.dt.uint8)
    A0 = nc.lookup_mloc(arena).addr
    SB = {}

    def at(name, rel, shape, dtype=F32):
        SB[name] = nc.alloc_sbuf_tensor_at(name, shape, dtype, offset=A0 + rel)

    KB = 1024
    at("identf", 0, [128, 128])
    at("Rb", 512, [128, 128], BF16)
    at("onesr", 768, [1, 128])
    at("onesb", 1280, [128, 1], BF16)
    at("scTb", 1312, [128, 16], BF16)
    at("sel8", 1344, [8, 8 * 128])
    at("bout", 5440, [128, KC])
    at("g1t", 5504, [128, KC])
    at("g2t", 5568, [128, KC])
    at("g3t", 5632, [128, FC])
    at("modv", 5888, [128, 8 * KC])
    at("svec", 6400, [128, 24])
    at("tok8", 6496, [128, 64])
    at("c2t", 6752, [16, 128])
    at("cpad", 7264, [128, 128])
    at("acc1", 7776, [128, 256])
    at("acc2", 8800, [128, 256])
    at("identb", 9824, [128, 128], BF16)
    at("onesf", 10080, [128, 1])
    at("rows", 10 * KB, [1, 2304])
    at("xm1", 10 * KB, [128, 1024])
    RT = 19456
    at("cosKt", RT, [128, TH]); at("sinKt", RT + 4608, [128, TH])
    at("cosQt", RT + 9216, [128, T]); at("sinQt", RT + 13312, [128, T])
    at("maskt", RT + 17408, [128, QB * 256])
    at("wgrp", RT, [128, 2048])
    at("remg", RT + 8192, [128, 2048])
    at("pg", RT + 16384, [128, 2048])
    at("hin0", RT, [128, T], F16)
    at("hin1", RT + 2048, [128, T], F16)
    at("u3g", RT + 4096, [128, 1024])
    at("rem3", RT + 8192, [128, 1024])
    at("p3", RT + 12288, [128, 1024])
    at("rs3B", RT + 16384, [128, 1024])
    at("i3B", RT + 20480, [128, 1024])
    at("sqb", RT + 20480, [128, 1024], BF16)
    STb = 45568
    at("st0", STb, [128, 2048]); at("st1", STb + 8192, [128, 2048])
    at("stq0", STb + 16384, [128, 2048], BF16)
    at("stq1", STb + 20480, [128, 2048], BF16)
    at("p1s0", STb + 24576, [128, 512]); at("p1s1", STb + 26624, [128, 512])
    at("scW", STb + 16384, [128, 1024]); at("ivW", STb + 20480, [128, 1024])
    at("m2B", STb + 24576, [128, 1024])
    at("r2B", 19456 + 16384, [128, 1024])
    at("f10", STb, [128, 1024]); at("f11", STb + 4096, [128, 1024])
    at("f20", STb + 8192, [128, 1024]); at("f21", STb + 12288, [128, 1024])
    at("fa1", STb + 16384, [128, 1024], BF16); at("fb1", STb + 18432, [128, 1024], BF16)
    at("fq1", STb + 20480, [128, 1024], BF16)
    at("fa2", STb + 22528, [128, 1024], BF16); at("fb2", STb + 24576, [128, 1024], BF16)
    at("fq2", STb + 26624, [128, 1024], BF16)
    at("w30", STb, [128, 1024]); at("w31", STb + 4096, [128, 1024])
    at("os0", STb + 8192, [128, 1024]); at("os1", STb + 12288, [128, 1024])
    at("wa3", STb + 16384, [128, 1024], BF16); at("wb3", STb + 18432, [128, 1024], BF16)
    at("wq30", STb + 20480, [128, 1024], BF16); at("wq31", STb + 22528, [128, 1024], BF16)
    at("xm0", STb + 24576, [128, 1024])
    BG = 74752
    at("xfm", BG, [128, KC * TH])
    at("qbf", BG, [128, NH * T], BF16)
    at("kbf", BG + 32768, [128, NH * TH], BF16)
    at("hbf", BG + 73728, [128, KC * TH], BF16)
    at("vbuf0", BG + 110592, [128, TH], BF16)
    at("vbuf1", BG + 112896, [128, TH], BF16)
    at("vfm0", BG + 110592, [128, TH], BF16)
    at("vfm1", BG + 112896, [128, TH], BF16)
    at("vtm0", BG + 115200, [128, TC * 128], BF16)
    at("vtm1", BG + 117504, [128, TC * 128], BF16)
    at("scf", BG + 119808, [128, 256])
    at("ppf", BG + 120832, [128, 256])
    at("pnf", BG + 121856, [128, 256])
    at("ptb", BG + 122880, [128, 256], BF16)
    at("scfB", BG + 123392, [128, 256])
    at("ppfB", BG + 124416, [128, 256])
    at("pnfB", BG + 125440, [128, 256])
    at("ptbB", BG + 126464, [128, 256], BF16)
    at("lnt0", BG + 110592, [128, 1152])
    at("lnt1", BG + 115200, [128, 1152])
    at("xbp", BG + 119808, [128, 2 * TH], BF16)
    at("mB", BG + 119808, [128, TH])
    at("rB", BG + 124416, [128, TH])
    at("xres0", BG + 32768, [128, 1024])
    at("xres1", BG + 36864, [128, 1024])
    at("xmid", BG + 65536, [128, KC * T])
    at("xmbf", BG + 32768, [128, KC * T], BF16)
    at("sq2", BG, [128, KC * T], BF16)
    at("h2", BG, [128, KC * T])
    at("xq1", BG + 65536, [128, KC * T], BF16)
    at("xq2", BG + 98304, [128, KC * T], BF16)
    at("sil0", BG, [128, 1024]); at("sil1", BG + 4096, [128, 1024])
    at("hm0", BG + 8192, [128, 1024], F16); at("hm1", BG + 10240, [128, 1024], F16)
    at("amx3", BG + 12288, [128, 1024])
    at("i1B", BG + 16384, [128, 1024]); at("i2B", BG + 20480, [128, 1024])
    at("xq3", BG, [128, FC * T], BF16)
    at("sqp", BG + 124416, [128, 2 * TH], BF16)

    PSl = [nc.alloc_psum_tensor("ps%d" % i, [128, 512], F32) for i in range(6)]
    t["SB"] = SB
    t["PS"] = PSl
    t["PT"] = [nc.alloc_psum_tensor("pt%d" % i, [128, 1024], BF16) for i in range(2)]
    return t


def _build(nc, k, t):
    SB, PS = t["SB"], t["PS"]

    def s(name):
        return SB[name][:]

    def dma_s(dst, src, st="sc"):
        return st, k.op(E, lambda e, d=dst, sr=src: e.dma_start(out=d, in_=sr),
                        inc=st, incn=16)

    def dma_g(dst, src, st="gp"):
        return st, k.op(G, lambda e, d=dst, sr=src: e.dma_start(out=d, in_=sr),
                        inc=st, incn=16)

    def mm(out, lhsT, rhs, start, stop, inc=None):
        return k.op(P, (lambda e, o=out, l=lhsT, r=rhs, st=start, sp=stop:
                        e.matmul(o, l, r, start=st, stop=sp)), inc=inc)

    class TP:

        def __init__(self, banks):
            self.banks = banks
            self.pend_ve = [0, 0]
            self.i = 0

        def do(self, src_ap, dst_ap, eng=V):
            b = self.i % 2
            bank = self.banks[b]
            k.wait(P, "ve", self.pend_ve[b])
            tp_pe = k.op(P, (lambda e, o=bank[:, 0:128], i_=src_ap:
                             e.transpose(o, i_, s("identf"))), inc="pe")
            k.wait(eng, "pe", tp_pe)
            self.pend_ve[b] = k.op(
                eng, (lambda e, d=dst_ap, bb=bank:
                      e.tensor_copy(out=d, in_=bb[:, 0:128])), inc="ve")
            self.i += 1

    def bcast(rows_ap, dst_ap, n, bank, name="bc"):
        done = []
        for off in range(0, n, 512):
            ln = min(512, n - off)
            pe_v = mm(bank[:, 0:ln], s("onesr"), rows_ap[:, off:off + ln],
                      start=True, stop=True, inc="pe")
            k.wait(V, "pe", pe_v)
            ve_v = k.op(V, (lambda e, d=dst_ap[:, off:off + ln], b=bank, l=ln:
                            e.tensor_copy(out=d, in_=b[:, 0:l])),
                        writes=(name,), inc="ve")
            k.wait(P, "ve", ve_v)
            done.append(ve_v)
        return done[-1]

    ldc = {}
    ldc["identf"] = dma_s(s("identf"), t["ident_d"])
    ldc["Rb"] = dma_s(s("Rb"), t["R_d"])
    ldc["identb"] = dma_s(s("identb"), t["identb_d"])
    ldc["onesr"] = dma_s(s("onesr"), t["ones_d"])
    ldc["sel8"] = dma_s(s("sel8"), t["sel8_d"])
    ldc["c2t"] = dma_s(s("c2t"), t["c2"])
    for c in range(KC):
        dma_s(SB["bout"][:, c:c + 1], t["b_out2"][c].unsqueeze(1))
        dma_s(SB["g1t"][:, c:c + 1], t["g1_2"][c].unsqueeze(1))
        ldc["g2_%d" % c] = dma_s(SB["g2t"][:, c:c + 1],
                                 t["g2_2"][c].unsqueeze(1))
    for c in range(FC):
        ldc["g3_%d" % c] = dma_s(SB["g3t"][:, c:c + 1],
                                 t["g3_2"][c].unsqueeze(1))
    ldc["cosK"] = dma_s(s("cosKt"), t["cosK"])
    ldc["sinK"] = dma_s(s("sinKt"), t["sinK"])
    ldc["cosQ"] = dma_s(s("cosQt"), t["cosQ"])
    ldc["sinQ"] = dma_s(s("sinQt"), t["sinQ"])
    for qb in range(QB):
        ldc["mask"] = dma_s(SB["maskt"][:, qb * 256:(qb + 1) * 256], t["masks"][qb])
    sc_tok = ("sc", k.val("sc"))

    sv = SB["svec"]
    k.op(V, lambda e: e.memset(sv[:, 0:1], 1.0), writes=("svec",))
    k.op(V, lambda e: e.memset(sv[:, 1:2], -0.5), writes=("svec",))
    k.op(V, lambda e: e.memset(sv[:, 4:5], 0.0), writes=("svec",))
    k.op(V, lambda e: e.memset(sv[:, 13:14], -63.5), writes=("svec",))
    nc.const_aps.register(mybir.dt.float32, -63.5, sv[:, 13:14])
    nc.const_aps.register(mybir.dt.float32, 0.0, sv[:, 4:5])
    nc.const_aps.register(mybir.dt.float32, 1.0, sv[:, 0:1])
    for col, val in ((2, 1e-5), (3, 1e-8), (11, -ATANH_HALF), (12, ATANH_HALF)):
        k.op(V, (lambda e, c_=col, v_=val: e.tensor_scalar(
            out=sv[:, c_:c_ + 1], in0=sv[:, 0:1], scalar1=v_, scalar2=None,
            op0=ALU.mult)), reads=("svec",), writes=("svec",))
    k.op(V, lambda e: e.memset(SB["onesf"][:], 1.0), writes=("onesf",))
    k.op(V, lambda e: e.memset(SB["cpad"][:], 0.0), writes=("cpad",))
    k.op(V, lambda e: e.memset(s("onesb"), 1.0), writes=("onesb",), inc="ve")
    v_consts = k.val("ve")

    xst = [SB["st0"], SB["st1"]]
    xtp = TP([PS[0], PS[1]])
    xld = []
    xtrans_done = []
    for tt in range(TC):
        if tt >= 2:
            k.wait(E, "pe", xtrans_done[tt - 2])
        xld.append(dma_s(xst[tt % 2][:], t["x_sh"][tt], "sx%d" % (tt % 2)))
        k.waitt(P, xld[tt])
        if tt == 0:
            k.waitt(P, sc_tok)
        for c in range(KC):
            xtp.do(xst[tt % 2][:, c * 128:(c + 1) * 128],
                   SB["xfm"][:, c * TH + tt * 128: c * TH + tt * 128 + 128])
        xtrans_done.append(k.val("pe"))
    x_in_v = k.val("ve")

    k.waitt(A, sc_tok)
    k.wait(A, "ve", v_consts)
    k.op(A, lambda e: e.activation(out=SB["cpad"][0:16, :], in_=s("c2t"),
                                   func=AF.Silu), writes=("cpad",), inc="ae")
    k.wait(P, "ae", k.val("ae"))
    pe_v = k.op(P, lambda e: e.transpose(PS[5][:, 0:128], s("cpad"),
                                         s("identf")), inc="pe")
    k.wait(V, "pe", pe_v)
    k.op(V, lambda e: e.tensor_copy(out=s("scTb"), in_=PS[5][:, 0:16]),
         writes=("scTb",))
    k.op(V, lambda e: e.tensor_copy(out=SB["cpad"][:, 64:80],
                                    in_=PS[5][:, 0:16]),
         writes=("cpad",), inc="ve")
    v_sct = k.val("ve")

    wmst = [SB["st0"], SB["st1"]]
    scTf = SB["cpad"][:, 64:80]
    k.wait(E, "pe", xtrans_done[-1])
    k.wait(P, "ve", v_sct)
    mod_pe0 = {}
    for ms in range(MS):
        if ms >= 2:
            k.wait(E, "pe", mod_pe0[ms - 2])
        w_ld = dma_s(wmst[ms % 2][:], t["wmod"][ms], "sx%d" % (ms % 2))
        k.waitt(P, w_ld)
        for kc in range(KC):
            mm(PS[4][:, 0:1], wmst[ms % 2][:, kc * 128:(kc + 1) * 128],
               scTf[:, kc:kc + 1], start=(kc == 0), stop=(kc == KC - 1),
               inc="pe" if kc == KC - 1 else None)
        mod_pe0[ms] = k.val("pe")
        pe_v = k.val("pe")
        k.wait(V, "pe", pe_v)
        vi, ch = ms // KC, ms % KC
        ve_v = k.op(V, (lambda e, v_=vi, c_=ch:
                        e.tensor_copy(out=SB["modv"][:, v_ * KC + c_:
                                                     v_ * KC + c_ + 1],
                                      in_=PS[4][:, 0:1])),
                    writes=("modv",), inc="ve")
        k.wait(P, "ve", ve_v)
    mo = SB["modv"]

    def slot(i):
        return mo[:, i * KC:(i + 1) * KC]
    k.op(V, lambda e: e.tensor_scalar(out=slot(1), in0=slot(1), scalar1=1.0,
                                      scalar2=None, op0=ALU.add),
         reads=("modv",), writes=("modv",))
    k.op(V, lambda e: e.tensor_scalar(out=slot(4), in0=slot(4), scalar1=1.0,
                                      scalar2=None, op0=ALU.add),
         reads=("modv",), writes=("modv",))
    for gi in (2, 5):
        k.wait(A, "ve", k.val("ve"))
        ae_v = k.op(A, (lambda e, i=gi: e.activation(out=slot(6), in_=slot(i),
                                                     func=AF.Sign)),
                    inc="ae")
        k.wait(V, "ae", ae_v)
        k.op(V, lambda e: e.scalar_tensor_tensor(out=slot(7), in0=slot(6),
                                                 scalar=-1.0, in1=slot(6),
                                                 op0=ALU.mult, op1=ALU.mult),
             writes=("modv7",))
        k.op(V, lambda e: e.tensor_scalar(out=slot(7), in0=slot(7), scalar1=1.0,
                                          scalar2=None, op0=ALU.add),
             reads=("modv7",), writes=("modv7",))
        k.op(V, lambda e: e.tensor_tensor(out=slot(7), in0=slot(7), in1=slot(6),
                                          op=ALU.add),
             reads=("modv7",), writes=("modv7",))
        k.op(V, lambda e, i=gi: e.scalar_tensor_tensor(out=slot(6), in0=slot(i),
                                                       scalar=-1.0, in1=slot(i),
                                                       op0=ALU.mult, op1=ALU.max),
             writes=("modv6",))
        k.op(V, lambda e: e.tensor_scalar(out=slot(6), in0=slot(6), scalar1=0.25,
                                          scalar2=None, op0=ALU.max),
             reads=("modv6",), writes=("modv6",))
        k.op(V, lambda e, i=gi: e.tensor_tensor(out=slot(i), in0=slot(6),
                                                in1=slot(7), op=ALU.mult),
             reads=("modv6", "modv7"), writes=("modv",), inc="ve")
    mod_v = k.val("ve")

    npieces = [(0, 512), (512, 512), (1024, 128)]
    sum_banks = [PS[0], PS[1], PS[2]]
    sq_banks = [PS[3], PS[4], PS[5]]
    k.wait(V, "ve", x_in_v)
    grp_pe = 0
    for g in range(8):
        k.wait(V, "pe", grp_pe)
        cast_v = k.op(V, (lambda e, gg=g: e.tensor_copy(
            out=s("xbp"), in_=SB["xfm"][:, gg * 2 * TH:(gg + 1) * 2 * TH])),
            writes=("xbp",), inc="ve")
        k.wait(A, "ve", cast_v)
        k.wait(A, "pe", grp_pe)
        sq_v = k.op(A, lambda e: e.activation(out=s("sqp"), in_=s("xbp"),
                                              func=AF.Square),
                    writes=("sqp",), inc="ae")
        k.wait(P, "ve", cast_v)
        k.wait(P, "ae", sq_v)
        for ci in range(2):
            for bi, (off, ln) in enumerate(npieces):
                st = (g == 0 and ci == 0)
                sp = (g == 7 and ci == 1)
                mm(sum_banks[bi][0:1, 0:ln], s("onesb"),
                   SB["xbp"][:, ci * TH + off:ci * TH + off + ln], st, sp)
                mm(sq_banks[bi][0:1, 0:ln], s("onesb"),
                   SB["sqp"][:, ci * TH + off:ci * TH + off + ln], st, sp,
                   inc="pe" if (ci == 1 and bi == 2) else None)
        grp_pe = k.val("pe")
    ln1_pe = k.val("pe")
    rows = SB["rows"]
    rowC = SB["lnt0"][0:1, :]
    k.wait(V, "pe", ln1_pe)
    for bi, (off, ln) in enumerate(npieces):
        k.op(V, (lambda e, b=sum_banks[bi], o=off, l=ln:
                 e.tensor_scalar(out=rows[:, o:o + l], in0=b[0:1, 0:l],
                                 scalar1=1.0 / D, scalar2=None, op0=ALU.mult)),
             writes=("rows",))
        k.op(V, (lambda e, b=sq_banks[bi], o=off, l=ln:
                 e.tensor_scalar(out=rows[:, TH + o:TH + o + l], in0=b[0:1, 0:l],
                                 scalar1=1.0 / D, scalar2=None, op0=ALU.mult)),
             writes=("rows",))
    k.op(V, lambda e: e.tensor_tensor(out=rowC, in0=rows[:, 0:TH],
                                      in1=rows[:, 0:TH], op=ALU.mult),
         reads=("rows",), writes=("rowC",))
    k.op(V, lambda e: e.tensor_tensor(out=rows[:, TH:2 * TH],
                                      in0=rows[:, TH:2 * TH], in1=rowC,
                                      op=ALU.subtract),
         reads=("rows", "rowC"), writes=("rows",), inc="ve")
    lnv_v = k.val("ve")
    k.wait(A, "ve", lnv_v)
    k.op(A, lambda e: e.activation(out=rowC, in_=rows[:, TH:2 * TH],
                                   func=AF.Ln, bias=sv[0:1, 2:3],
                                   scale=sv[0:1, 0:1]),
         writes=("rowC",))
    k.op(A, lambda e: e.activation(out=rows[:, TH:2 * TH], in_=rowC,
                                   func=AF.Exp, bias=sv[0:1, 4:5],
                                   scale=sv[0:1, 1:2]),
         reads=("rowC",), writes=("rows",), inc="ae")
    rstd_a = k.val("ae")
    k.wait(V, "ae", rstd_a)
    k.wait(P, "ae", rstd_a)
    bcast(rows[:, 0:TH], s("mB"), TH, PS[0], "mB")
    bcast(rows[:, TH:2 * TH], s("rB"), TH, PS[1], "rB")
    bc_v = k.val("ve")
    k.wait(G, "ve", max(mod_v, bc_v))
    g_done = {}
    for c in range(KC):
        scr = s("lnt1") if c % 2 == 0 else SB["lnt0"][:]
        tg = "lnt1" if c % 2 == 0 else "lnt0"
        k.op(V, (lambda e, cc=c, sc_=scr: e.tensor_tensor(
            out=sc_, in0=SB["xfm"][:, cc * TH:(cc + 1) * TH], in1=s("mB"),
            op=ALU.subtract)), reads=("mB", "rowC", tg), writes=(tg,))
        k.op(V, (lambda e, sc_=scr: e.tensor_tensor(out=sc_, in0=sc_,
                                                    in1=s("rB"), op=ALU.mult)),
             reads=(tg, "rB"), writes=(tg,), inc="ve")
        g_done[c] = k.op(V, (lambda e, cc=c, sc_=scr: e.tensor_scalar(
            out=SB["hbf"][:, cc * TH:(cc + 1) * TH], in0=sc_,
            scalar1=SB["modv"][:, KC + cc:KC + cc + 1],
            scalar2=SB["modv"][:, cc:cc + 1],
            op0=ALU.mult, op1=ALU.add)),
            reads=("modv", tg), writes=("hbf",), inc="po")
    h_v, h_po = k.val("ve"), k.val("po")
    return dict(ldc=ldc, h_v=h_v, h_po=h_po, mod_v=mod_v, x_in_v=x_in_v,
                TPcls=TP, bcast=bcast, mm=mm, dma_s=dma_s, dma_g=dma_g, s=s)


def _build_p23(nc, k, t, pp):
    SB, PS, PT = t["SB"], t["PS"], t["PT"]
    s, mm, dma_s, dma_g = pp["s"], pp["mm"], pp["dma_s"], pp["dma_g"]
    h_v, h_po = pp["h_v"], pp["h_po"]

    npieces = [(0, 512), (512, 512), (1024, 128)]
    wst = [SB["st0"], SB["st1"]]
    wsq = [SB["stq0"], SB["stq1"]]
    vbuf = [SB["vbuf0"], SB["vbuf1"]]

    p1_pieces = []
    for wname, base in (("w1", 0), ("w2", 256), ("w3", 512)):
        wd = t[wname]
        nslab = wd.shape[0]
        per = wd.shape[2] // 512
        for sl in range(nslab):
            for pc in range(per):
                i = base + sl * per + pc
                p1_pieces.append((wd[sl][:, pc * 512:(pc + 1) * 512], i))
    assert len(p1_pieces) == 768
    p1s = [SB["p1s0"], SB["p1s1"]]
    p1pair = SB["p1s0"][:].rearrange("p (a b) -> p a b", a=1) if False else None
    p1_iter = [0]
    p1_cnt = {V: 0, A: 0}
    p1_fold = {}

    def pump_pass1(n):
        for _ in range(n):
            j = p1_iter[0]
            if j >= 384:
                return
            p1_iter[0] += 1
            if j >= 1:
                peng = "p1v" if (j - 1) % 2 == 0 else "p1a"
                k.wait(E, peng, k.val(peng))
            dma_s(p1s[0][:], p1_pieces[2 * j][0], "sp")
            ld2 = dma_s(p1s[1][:], p1_pieces[2 * j + 1][0], "sp")
            eng = V if j % 2 == 0 else A
            k.waitt(eng, ld2)
            acc = SB["acc1"] if eng == V else SB["acc2"]
            pi = p1_cnt[eng]
            p1_cnt[eng] += 1
            col = pi % 16
            if pi >= 16:
                k.wait(eng, "p1f", p1_fold[(eng, pi // 16 - 1)])
            if eng == V:
                k.op(V, (lambda e, a=acc, cl=col:
                         e.reduce_sum(out=a[:, cl:cl + 1],
                                      in_=SB["p1s0"][:].rearrange(
                                          "p (a b) -> p a b", a=2,
                                          b=512).flatten_outer_dims()
                                      if False else p1s[0][:, 0:512],
                                      axis=AX.X, apply_absolute_value=True)),
                     reads=("acc",), writes=("acc", "p1b"))
                k.op(V, (lambda e, a=acc, cl=col:
                         e.reduce_sum(out=a[:, 16 + cl:16 + cl + 1],
                                      in_=p1s[1][:], axis=AX.X,
                                      apply_absolute_value=True)),
                     reads=("acc",), writes=("acc", "p1b"), inc="p1v")
            else:
                k.op(A, (lambda e, a=acc, cl=col: e.activation(
                    out=p1s[0][:], in_=p1s[0][:], func=AF.Abs,
                    accum_out=a[:, cl:cl + 1])),
                    reads=("acc",), writes=("acc", "p1b"))
                k.op(A, (lambda e, a=acc, cl=col: e.activation(
                    out=p1s[1][:], in_=p1s[1][:], func=AF.Abs,
                    accum_out=a[:, 16 + cl:16 + cl + 1])),
                    reads=("acc",), writes=("acc", "p1b"), inc="p1a")
            if pi % 16 == 15:
                fcol = 32 + pi // 16
                if eng == A:
                    k.wait(V, "p1a", k.val("p1a"))
                k.op(V, (lambda e, a=acc, fc=fcol:
                         e.reduce_sum(out=a[:, fc:fc + 1], in_=a[:, 0:32],
                                      axis=AX.X)),
                     reads=("acc",), writes=("acc",), inc="p1f")
                p1_fold[(eng, pi // 16)] = k.val("p1f")

    k.wait(G, "ve", h_v)
    k.wait(G, "po", h_po)
    for kc in range(KC):
        park_qg = dma_g(t["x_park"][kc],
                        SB["xfm"][:, kc * TH + 128:(kc + 1) * TH], "gp")
    k.wait(P, "ve", h_v)
    k.wait(P, "po", h_po)
    k.waitt(V, park_qg)
    pp["park_qg"] = park_qg
    ogld = [None] * 48
    ogcast = [None] * 48
    ev_after = {}
    po_after = {}
    qg_after = {}
    for og in range(48):
        if og >= 2:
            k.wait(E, "po", ogcast[og - 2])
        if og < 2:
            k.wait(E, "po", k.val("po"))
        ogld[og] = dma_s(wst[og % 2][:], t["wq"][og], "sx%d" % (og % 2))
        k.waitt(G, ogld[og])
        if og >= 2:
            k.wait(G, "pe", pe_after_og)
        else:
            k.wait(G, "pe", k.val("pe"))
        ogcast[og] = k.op(G, (lambda e, o=og: e.tensor_copy(
            out=wsq[o % 2][:], in_=wst[o % 2][:])), inc="po")
        k.wait(P, "po", ogcast[og])
        banks = [PS[0], PS[1], PS[2]] if og % 2 == 0 else [PS[3], PS[4], PS[5]]
        if og >= 2:
            k.wait(P, "ve", ev_after[og - 2])
        for kc in range(KC):
            for bi, (off, ln) in enumerate(npieces):
                mm(banks[bi][:, 0:ln], wsq[og % 2][:, kc * 128:(kc + 1) * 128],
                   SB["hbf"][:, kc * TH + off:kc * TH + off + ln],
                   start=(kc == 0), stop=(kc == KC - 1),
                   inc="pe" if (kc == KC - 1 and bi == 2) else None)
        pe_after_og = k.val("pe")
        k.wait(V, "pe", pe_after_og)
        if og < 16:
            hidx = og
            dsts = [(SB["qbf"][:, hidx * T:hidx * T + 384], banks[0][:, 128:512]),
                    (SB["qbf"][:, hidx * T + 384:hidx * T + 896], banks[1][:, 0:512]),
                    (SB["qbf"][:, hidx * T + 896:hidx * T + 1024], banks[2][:, 0:128])]
            for di, (d, sr) in enumerate(dsts):
                k.op(V, (lambda e, dd=d, ss=sr: e.tensor_copy(out=dd, in_=ss)),
                     writes=("qbf",), inc="ve" if di == 2 else None)
        elif og < 32:
            hidx = og - 16
            for bi, (off, ln) in enumerate(npieces):
                k.op(V, (lambda e, h_=hidx, o=off, l=ln, b=banks[bi]:
                         e.tensor_copy(out=SB["kbf"][:, h_ * TH + o:h_ * TH + o + l],
                                       in_=b[:, 0:l])), writes=("kbf",),
                     inc="ve" if bi == 2 else None)
        else:
            hidx = og - 32
            vb = vbuf[hidx % 2]
            if hidx >= 2:
                k.waitt(V, qg_after[hidx - 2])
            for bi, (off, ln) in enumerate(npieces):
                k.op(V, (lambda e, o=off, l=ln, b=banks[bi], v_=vb:
                         e.tensor_copy(out=v_[:, o:o + l], in_=b[:, 0:l])),
                     writes=("vbuf",), inc="ve" if bi == 2 else None)
            k.wait(G, "ve", k.val("ve"))
            qg_after[hidx] = dma_g(t["v_dram"][hidx], vb[:],
                                   "ga%d" % (hidx % 2))
        ev_after[og] = k.val("ve")
        pump_pass1(8)
    qkv_done_pe = k.val("pe")
    v_all_stored = qg_after[15]

    k.wait(P, "ve", ev_after[47])
    k.wait(V, "ga0", k.val("ga0"))
    k.wait(V, "ga1", k.val("ga1"))
    for h in range(NH):
        banks = [PS[0], PS[1], PS[2]] if h % 2 == 0 else [PS[3], PS[4], PS[5]]
        for (x_sl, cosT, sinT, n_tok) in (
                (SB["qbf"][:, h * T:(h + 1) * T], s("cosQt"), s("sinQt"), T),
                (SB["kbf"][:, h * TH:(h + 1) * TH], s("cosKt"), s("sinKt"), TH)):
            pieces = [(0, 512), (512, 512)] if n_tok == T else npieces
            k.wait(P, "ve", k.val("ve"))
            pe_rot = None
            for bi, (off, ln) in enumerate(pieces):
                pe_rot = mm(banks[bi][:, 0:ln], s("Rb"), x_sl[:, off:off + ln],
                            start=True, stop=True,
                            inc="pe" if bi == len(pieces) - 1 else None)
            k.wait(G, "pe", pe_rot)
            k.wait(V, "pe", pe_rot)
            for bi, (off, ln) in enumerate(pieces):
                k.op(V, (lambda e, xs=x_sl, ct=cosT, o=off, l=ln:
                         e.tensor_tensor(out=SB["lnt0"][:, o:o + l],
                                         in0=xs[:, o:o + l], in1=ct[:, o:o + l],
                                         op=ALU.mult)),
                     reads=("qk",), writes=("lnt0",))
                k.op(V, (lambda e, b=banks[bi], st_=sinT, o=off, l=ln:
                         e.tensor_tensor(out=SB["lnt1"][:, o:o + l],
                                         in0=b[:, 0:l], in1=st_[:, o:o + l],
                                         op=ALU.mult)),
                     writes=("lnt1",))
                k.op(V, (lambda e, xs=x_sl, o=off, l=ln:
                         e.tensor_tensor(out=xs[:, o:o + l],
                                         in0=SB["lnt0"][:, o:o + l],
                                         in1=SB["lnt1"][:, o:o + l],
                                         op=ALU.add)),
                     reads=("lnt0", "lnt1"), writes=("qk",), inc="ve")
    rope_v = k.val("ve")
    pump_pass1(384)

    vfm = [SB["vfm0"], SB["vfm1"]]
    vtm = [SB["vtm0"], SB["vtm1"]]
    vin_ld = [None] * NH
    vtm_ready = [None] * NH
    k.wait(G, "ga0", k.val("ga0"))
    k.wait(G, "ga1", k.val("ga1"))
    k.wait(P, "ve", rope_v)

    vtp_last = [0, 0]

    def v_prep(h):
        if h >= 2:
            k.wait(G, "pe", unit_pv_pe[h - 2])
        else:
            k.wait(G, "ve", rope_v)
        vin_ld[h] = dma_g(vfm[h % 2][:], t["v_dram"][h], "gb%d" % (h % 2))
        k.waitt(P, vin_ld[h])
        for tc in range(TC):
            b = PT[tc % 2]
            k.wait(P, "ve", vtp_last[tc % 2])
            pe_v = k.op(P, (lambda e, h_=h, tt=tc, bb=b: e.transpose(
                bb[:, 0:128], vfm[h_ % 2][:, tt * 128:(tt + 1) * 128],
                s("identb"))), inc="pe")
            k.wait(V, "pe", pe_v)
            vtp_last[tc % 2] = k.op(V, (lambda e, h_=h, tt=tc, bb=b: e.tensor_copy(
                out=vtm[h_ % 2][:, tt * 128:(tt + 1) * 128], in_=bb[:, 0:128])),
                writes=("vtm",), inc="ve")
        vtm_ready[h] = k.val("ve")

    unit_pv_pe = {}
    v_prep(0)
    v_prep(1)
    scf_t = [s("scf"), s("scfB")]
    ppf_t = [s("ppf"), s("ppfB")]
    pnf_t = [s("pnf"), s("pnfB")]
    ptb_t = [SB["ptb"], SB["ptbB"]]
    scf_ve = [0, 0]
    exp_ae = [0, 0]
    pnf_ve = [0, 0]
    ptT_pe = [0, 0]
    ptb_ve = [0, 0]
    pv_pe = [0, 0]
    ao_ve = [0, 0]
    for h in range(NH):
        for qb in range(QB):
            p_ = (h * QB + qb) % 2
            psA = PS[0] if p_ == 0 else PS[1]
            psO = PS[4] if p_ == 0 else PS[5]
            mxc = 32 + 3 * p_
            k.wait(P, "ve", scf_ve[p_])
            pe_s = mm(psA[:, 0:256], SB["qbf"][:, h * T + qb * 128:
                                              h * T + (qb + 1) * 128],
                      SB["kbf"][:, h * TH + qb * 128:h * TH + qb * 128 + 256],
                      start=True, stop=True, inc="pe")
            k.wait(V, "pe", pe_s)
            k.wait(V, "ae", exp_ae[p_])
            scf_ve[p_] = k.op(V, (lambda e, b=psA, q_=qb, sc_=scf_t[p_]:
                              e.scalar_tensor_tensor(
                out=sc_, in0=b[:, 0:256], scalar=1.0 / math.sqrt(HD),
                in1=SB["maskt"][:, q_ * 256:(q_ + 1) * 256],
                op0=ALU.mult, op1=ALU.add)),
                writes=("scf",), inc="ve")
            k.op(V, (lambda e, sc_=scf_t[p_], c_=mxc: e.reduce_max(
                out=SB["tok8"][:, c_:c_ + 1], in_=sc_,
                axis=AX.X, negate=True)),
                reads=("scf",), writes=("mx",), inc="ve")
            ve_mx = k.val("ve")
            k.wait(A, "ve", ve_mx)
            k.wait(A, "ve", pnf_ve[p_])
            exp_ae[p_] = k.op(A, (lambda e, sc_=scf_t[p_], pp_=ppf_t[p_],
                              c_=mxc: e.activation(
                out=pp_, in_=sc_, func=AF.Exp,
                bias=SB["tok8"][:, c_:c_ + 1], scale=SB["svec"][:, 0:1])),
                inc="ae")
            k.wait(V, "ae", exp_ae[p_])
            k.op(V, (lambda e, pp_=ppf_t[p_], c_=mxc: e.reduce_sum(
                out=SB["tok8"][:, c_ + 1:c_ + 2], in_=pp_, axis=AX.X)),
                writes=("rs",))
            k.op(V, (lambda e, c_=mxc: e.reciprocal(
                out=SB["tok8"][:, c_ + 2:c_ + 3],
                in_=SB["tok8"][:, c_ + 1:c_ + 2])),
                reads=("rs",), writes=("rs",))
            k.wait(V, "pe", ptT_pe[p_])
            pnf_ve[p_] = k.op(V, (lambda e, pp_=ppf_t[p_], pn_=pnf_t[p_],
                              c_=mxc: e.tensor_scalar(
                out=pn_, in0=pp_, scalar1=SB["tok8"][:, c_ + 2:c_ + 3],
                scalar2=None, op0=ALU.mult)),
                reads=("rs",), writes=("pnf",), inc="ve")
            ve_pn = k.val("ve")
            k.wait(P, "ve", ve_pn)
            for j in range(2):
                k.wait(P, "ve", ptb_ve[j])
                pe_t = k.op(P, (lambda e, jj=j, b=PS[2 + j], pn_=pnf_t[p_]:
                            e.transpose(b[:, 0:128],
                                        pn_[:, jj * 128:(jj + 1) * 128],
                                        s("identf"))), inc="pe")
                ptT_pe[p_] = pe_t
                k.wait(V, "pe", pe_t)
                k.wait(V, "pe", pv_pe[p_])
                ptb_ve[j] = k.op(V, (lambda e, jj=j, b=PS[2 + j],
                                 pt_=ptb_t[p_]: e.tensor_copy(
                    out=pt_[:, jj * 128:(jj + 1) * 128], in_=b[:, 0:128])),
                    writes=("ptb",), inc="ve")
            ve_pt = k.val("ve")
            k.wait(P, "ve", ve_pt)
            k.wait(P, "ve", vtm_ready[h])
            k.wait(P, "ve", ao_ve[p_])
            for j in range(2):
                mm(psO[:, 0:128],
                   vtm[h % 2][:, (qb + j) * 128:(qb + j + 1) * 128],
                   ptb_t[p_][:, j * 128:(j + 1) * 128],
                   start=(j == 0), stop=(j == 1), inc="pe" if j == 1 else None)
            pv_pe[p_] = pe_pv = k.val("pe")
            k.wait(V, "pe", pe_pv)
            ao_ve[p_] = k.op(V, (lambda e, h_=h, q_=qb, b=psO: e.tensor_copy(
                out=SB["qbf"][:, h_ * T + q_ * 128:h_ * T + (q_ + 1) * 128],
                in_=b[:, 0:128])), writes=("ao",), inc="ve")
        unit_pv_pe[h] = k.val("pe")
        if h + 2 < NH:
            v_prep(h + 2)
    attn_v = k.val("ve")
    pp["attn_v"] = pp_attn = attn_v
    pp["qkv_done_pe"] = qkv_done_pe
    return pp


def nc_sq1(SB):
    return SB["xq1"][:]


def _build_p45(nc, k, t, pp):
    SB, PS, PT = t["SB"], t["PS"], t["PT"]
    s, mm, dma_s, dma_g = pp["s"], pp["mm"], pp["dma_s"], pp["dma_g"]
    sv = SB["svec"]
    rows = SB["rows"]
    attn_v = pp["attn_v"]

    k.wait(V, "p1v", 192)
    k.wait(V, "p1a", 192)
    k.wait(V, "p1f", k.val("p1f"))
    k.wait(V, "ae", k.val("ae"))
    scal = {}
    CELL0 = 2176

    def cell(name):
        if name not in scal:
            scal[name] = CELL0 + len(scal)
        return rows[:, scal[name]:scal[name] + 1]

    for wi, nm in enumerate(("s1", "s2", "s3")):
        f0 = 32 + wi * 4
        k.op(V, (lambda e, f=f0: e.reduce_sum(out=SB["acc1"][:, 44:45],
                                              in_=SB["acc1"][:, f:f + 4],
                                              axis=AX.X)),
             reads=("acc",), writes=("accV",))
        k.op(V, (lambda e, f=f0: e.reduce_sum(out=SB["acc2"][:, 44:45],
                                              in_=SB["acc2"][:, f:f + 4],
                                              axis=AX.X)),
             reads=("accG",), writes=("accG",))
        k.op(V, lambda e: e.tensor_tensor(out=SB["acc1"][:, 45:46],
                                          in0=SB["acc1"][:, 44:45],
                                          in1=SB["acc2"][:, 44:45], op=ALU.add),
             reads=("accV", "accG"), writes=("accS",), inc="ve")
        k.wait(P, "ve", k.val("ve"))
        pe_v = mm(PS[5][0:1, 0:1], s("onesf"), SB["acc1"][:, 45:46],
                  start=True, stop=True, inc="pe")
        k.wait(V, "pe", pe_v)
        k.op(V, (lambda e, c=cell(nm): e.tensor_scalar(
            out=c, in0=PS[5][0:1, 0:1], scalar1=1.0 / (FF * D), scalar2=None,
            op0=ALU.mult)), writes=("rows",), inc="ve")
        k.wait(P, "ve", k.val("ve"))
        k.op(V, (lambda e, c=cell(nm), o=cell(nm + "_pe"): e.tensor_scalar(
            out=o, in0=c, scalar1=1e-8, scalar2=None, op0=ALU.add)),
            reads=("rows",), writes=("rows",))
        k.op(V, (lambda e, i_=cell(nm + "_pe"), o=cell(nm + "_r"):
                 e.reciprocal(out=o, in_=i_)),
             reads=("rows",), writes=("rows",))
        k.op(V, (lambda e, c=cell(nm), o=cell(nm + "_1p"): e.tensor_scalar(
            out=o, in0=c, scalar1=1.0, scalar2=None, op0=ALU.add)),
            reads=("rows",), writes=("rows",))
        k.op(V, (lambda e, c=cell(nm), o=cell(nm + "_1m"): e.tensor_scalar(
            out=o, in0=c, scalar1=-1.0, scalar2=1.0, op0=ALU.mult, op1=ALU.add)),
            reads=("rows",), writes=("rows",))
        k.op(V, (lambda e, i_=cell(nm + "_1m"), o=cell(nm + "_1mr"):
                 e.reciprocal(out=o, in_=i_)),
             reads=("rows",), writes=("rows",))
        k.op(V, (lambda e, a=cell(nm + "_1p"), b=cell(nm + "_1mr"),
                 o=cell(nm + "_q"): e.tensor_tensor(out=o, in0=a, in1=b,
                                                    op=ALU.mult)),
             reads=("rows",), writes=("rows",), inc="ve")
        k.wait(A, "ve", k.val("ve"))
        k.op(A, (lambda e, i_=cell(nm + "_q"), o=cell(nm + "_ln"):
                 e.activation(out=o, in_=i_, func=AF.Ln, bias=sv[0:1, 4:5],
                              scale=sv[0:1, 0:1])),
             writes=("rowsA",), inc="ae")
        k.wait(V, "ae", k.val("ae"))
        k.op(V, (lambda e, i_=cell(nm + "_ln"), o=cell(nm + "_lam"):
                 e.tensor_scalar(out=o, in0=i_, scalar1=0.25, scalar2=None,
                                 op0=ALU.mult)),
             reads=("rows", "rowsA"), writes=("rows",), inc="ve")
    k.wait(P, "ve", k.val("ve"))
    for wi, nm in enumerate(("s1", "s2", "s3")):
        pe_v = mm(PS[5][:, 0:1], s("onesr"), cell(nm + "_r"),
                  start=True, stop=True, inc="pe")
        k.wait(V, "pe", pe_v)
        ve_v = k.op(V, (lambda e, w=wi: e.tensor_copy(out=sv[:, 5 + w:6 + w],
                                                      in_=PS[5][:, 0:1])),
                    writes=("svec",), inc="ve")
        k.wait(P, "ve", ve_v)
    sgate_v = k.val("ve")

    wst = [SB["st0"], SB["st1"]]
    wsq = [SB["stq0"], SB["stq1"]]
    xres = [SB["xres0"], SB["xres1"]]
    npc2 = [(0, 512), (512, 512)]
    k.wait(P, "ve", attn_v)
    k.wait(E, "ve", attn_v)
    k.waitt(E, pp["park_qg"])
    ogcast = [None] * KC
    evac_after = {}
    xres_ld = [None] * KC
    pe_after = 0
    for og in range(KC):
        if og >= 2:
            k.wait(E, "po", ogcast[og - 2])
            k.wait(E, "ve", evac_after[og - 2])
        else:
            k.wait(E, "po", k.val("po"))
        ld = dma_s(wst[og % 2][:], t["wo"][og], "sx%d" % (og % 2))
        xres_ld[og] = dma_s(xres[og % 2][:], t["x_park"][og], "sr%d" % (og % 2))
        k.waitt(G, ld)
        if og >= 2:
            k.wait(G, "pe", pe_after)
        else:
            k.wait(G, "pe", k.val("pe"))
        ogcast[og] = k.op(G, (lambda e, o=og: e.tensor_copy(
            out=wsq[o % 2][:], in_=wst[o % 2][:])), inc="po")
        k.wait(P, "po", ogcast[og])
        banks = [PS[0], PS[1]] if og % 2 == 0 else [PS[2], PS[3]]
        if og >= 2:
            k.wait(P, "ve", evac_after[og - 2])
        for kc in range(KC):
            for bi, (off, ln) in enumerate(npc2):
                mm(banks[bi][:, 0:ln], wsq[og % 2][:, kc * 128:(kc + 1) * 128],
                   SB["qbf"][:, kc * T + off:kc * T + off + ln],
                   start=(kc == 0), stop=(kc == KC - 1),
                   inc="pe" if (kc == KC - 1 and bi == 1) else None)
        if og == KC - 1:
            mm(PS[4][0:1, 0:1], s("onesb"), SB["qbf"][:, 0:1],
               start=True, stop=True)
            mm(PS[4][0:1, 0:1], s("onesb"), SB["qbf"][:, 0:1],
               start=True, stop=True, inc="pe")
        pe_after = k.val("pe")
        k.wait(V, "pe", pe_after)
        k.waitt(V, xres_ld[og])
        for bi, (off, ln) in enumerate(npc2):
            k.op(V, (lambda e, b=banks[bi], o=off, l=ln, c=og: e.tensor_scalar(
                out=SB["wgrp"][:, o:o + l],
                in0=b[:, 0:l], scalar1=SB["bout"][:, c:c + 1], scalar2=None,
                op0=ALU.add)), writes=("wgrp",))
            k.op(V, (lambda e, o=off, l=ln, c=og: e.scalar_tensor_tensor(
                out=SB["xmid"][:, c * T + o:c * T + o + l],
                in0=SB["wgrp"][:, o:o + l],
                scalar=SB["modv"][:, 2 * KC + c:2 * KC + c + 1],
                in1=xres[c % 2][:, o:o + l], op0=ALU.mult, op1=ALU.add)),
                reads=("wgrp", "modv"), writes=("xmid",),
                inc="ve" if bi == 1 else None)
        evac_after[og] = k.val("ve")
    xmid_v = k.val("ve")
    pp["wo_cast_po"] = ogcast[KC - 1]

    k.wait(V, "ve", xmid_v)
    cast_v = k.op(V, lambda e: e.tensor_copy(out=s("xmbf"), in_=s("xmid")),
                  writes=("xmbf",), inc="ve")
    k.wait(A, "ve", cast_v)
    sq_a = k.op(A, lambda e: e.activation(out=s("sq2"), in_=s("xmbf"),
                                          func=AF.Square),
                writes=("sq2",), inc="ae")
    k.wait(P, "ve", cast_v)
    k.wait(P, "ae", sq_a)
    for kc in range(KC):
        for bi, (off, ln) in enumerate(npc2):
            mm(PS[bi][0:1, 0:ln], s("onesb"),
               SB["xmbf"][:, kc * T + off:kc * T + off + ln],
               start=(kc == 0), stop=(kc == KC - 1))
            mm(PS[2 + bi][0:1, 0:ln], s("onesb"),
               SB["sq2"][:, kc * T + off:kc * T + off + ln],
               start=(kc == 0), stop=(kc == KC - 1),
               inc="pe" if (kc == KC - 1 and bi == 1) else None)
    ln2_pe = k.val("pe")
    k.wait(V, "pe", ln2_pe)
    for bi, (off, ln) in enumerate(npc2):
        k.op(V, (lambda e, b=PS[bi], o=off, l=ln: e.tensor_scalar(
            out=rows[:, o:o + l], in0=b[0:1, 0:l], scalar1=1.0 / D,
            scalar2=None, op0=ALU.mult)), writes=("rows",))
        k.op(V, (lambda e, b=PS[2 + bi], o=off, l=ln: e.tensor_scalar(
            out=rows[:, TH + o:TH + o + l], in0=b[0:1, 0:l], scalar1=1.0 / D,
            scalar2=None, op0=ALU.mult)), writes=("rows",))
    rowC = rows[:, 1152 + 1024:1152 + 1024 + 1]
    rowM2 = SB["acc1"][0:1, 64:64 + 64]
    k.op(V, lambda e: e.tensor_tensor(out=rows[:, 2 * TH:2 * TH + T] if False
                                      else SB["m2B"][0:1, 0:T],
                                      in0=rows[:, 0:T], in1=rows[:, 0:T],
                                      op=ALU.mult),
         reads=("rows",), writes=("m2Brow",))
    k.op(V, lambda e: e.tensor_tensor(out=rows[:, TH:TH + T],
                                      in0=rows[:, TH:TH + T],
                                      in1=SB["m2B"][0:1, 0:T], op=ALU.subtract),
         reads=("rows", "m2Brow"), writes=("rows",), inc="ve")
    v_var2 = k.val("ve")
    k.wait(A, "ve", v_var2)
    k.op(A, lambda e: e.activation(out=SB["m2B"][0:1, 0:T],
                                   in_=rows[:, TH:TH + T], func=AF.Ln,
                                   bias=sv[0:1, 2:3], scale=sv[0:1, 0:1]),
         writes=("m2Brow",))
    k.op(A, lambda e: e.activation(out=rows[:, TH:TH + T],
                                   in_=SB["m2B"][0:1, 0:T], func=AF.Exp,
                                   bias=sv[0:1, 4:5], scale=sv[0:1, 1:2]),
         reads=("m2Brow",), writes=("rowsX",), inc="ae")
    rstd2_a = k.val("ae")
    k.wait(V, "ae", rstd2_a)
    k.wait(P, "ae", rstd2_a)
    pp["bcast"](rows[:, 0:T], s("m2B"), T, PS[0], "m2B")
    pp["bcast"](rows[:, TH:TH + T], s("r2B"), T, PS[1], "r2B")
    bc2_v = k.val("ve")
    k.wait(G, "ve", bc2_v)
    g2_done = {}
    for c in range(KC):
        scr = SB["scW"][:] if c % 2 == 0 else SB["ivW"][:]
        tg = "scW" if c % 2 == 0 else "ivW"
        k.op(V, (lambda e, cc=c, sc_=scr: e.tensor_tensor(
            out=sc_, in0=SB["xmid"][:, cc * T:(cc + 1) * T], in1=s("m2B"),
            op=ALU.subtract)), reads=("m2B", tg), writes=(tg,))
        k.op(V, (lambda e, sc_=scr: e.tensor_tensor(
            out=sc_, in0=sc_, in1=s("r2B"), op=ALU.mult)),
            reads=(tg, "r2B"), writes=(tg,), inc="ve")
        g2_done[c] = k.op(V, (lambda e, cc=c, sc_=scr: e.tensor_scalar(
            out=SB["h2"][:, cc * T:(cc + 1) * T], in0=sc_,
            scalar1=SB["modv"][:, 4 * KC + cc:4 * KC + cc + 1],
            scalar2=SB["modv"][:, 3 * KC + cc:3 * KC + cc + 1],
            op0=ALU.mult, op1=ALU.add)),
            reads=("modv", tg), writes=("h2",), inc="po")
    h2_v, h2_po = k.val("ve"), k.val("po")

    k.wait(G, "ve", h2_v)
    k.wait(G, "po", h2_po)
    for kc in range(KC):
        park2 = dma_g(t["xm_park"][kc], SB["xmid"][:, kc * T:(kc + 1) * T],
                      "gp2")
    pp["park2"] = park2

    k.wait(A, "ve", h2_v)
    k.wait(A, "po", h2_po)
    k.waitt(A, pp["park2"])
    sqa2 = k.op(A, lambda e: e.activation(
        out=SB["xq1"][:].bitcast(BF16) if False else
        nc_sq1(SB), in_=s("h2"), func=AF.Square),
        writes=("sqh",), inc="ae")
    k.wait(P, "ae", sqa2)
    for kc in range(KC):
        for bi, (off, ln) in enumerate(npc2):
            mm(PS[2 + bi][0:1, 0:ln], s("onesb"),
               nc_sq1(SB)[:, kc * T + off:kc * T + off + ln],
               start=(kc == 0), stop=(kc == KC - 1),
               inc="pe" if (kc == KC - 1 and bi == 1) else None)
    rms1_pe = k.val("pe")
    k.wait(V, "pe", rms1_pe)
    for bi, (off, ln) in enumerate(npc2):
        k.op(V, (lambda e, b=PS[2 + bi], o=off, l=ln: e.tensor_scalar(
            out=rows[:, o:o + l], in0=b[0:1, 0:l], scalar1=1.0 / D,
            scalar2=None, op0=ALU.mult)), writes=("rows",), inc="ve")
    v_ms1 = k.val("ve")
    k.wait(A, "ve", v_ms1)
    k.op(A, lambda e: e.activation(out=rows[:, TH:TH + T], in_=rows[:, 0:T],
                                   func=AF.Ln, bias=sv[0:1, 3:4],
                                   scale=sv[0:1, 0:1]),
         writes=("rowsY",))
    k.op(A, lambda e: e.activation(out=rows[:, 0:T], in_=rows[:, TH:TH + T],
                                   func=AF.Exp, bias=sv[0:1, 4:5],
                                   scale=sv[0:1, 1:2]),
         reads=("rowsY",), writes=("rowsZ",), inc="ae")
    rstd1_a = k.val("ae")
    k.wait(V, "ae", rstd1_a)
    k.wait(P, "ae", rstd1_a)
    pp["bcast"](rows[:, 0:T], s("m2B"), T, PS[0], "r1B")
    r1b_v = k.val("ve")
    pp["h2_v"] = h2_v
    pp["r1b_v"] = r1b_v
    pp["sgate_v"] = sgate_v
    pp["cell"] = cell
    return pp


def _build_p5b(nc, k, t, pp):
    SB, PS = t["SB"], t["PS"]
    s, mm = pp["s"], pp["mm"]
    sv = SB["svec"]
    cell = pp["cell"]

    def r1bc(n2):
        return SB["m2B"][:].unsqueeze(1).broadcast_to([128, n2, T])

    def gbc(gt, c0, n2):
        return SB[gt][:, c0:c0 + n2].unsqueeze(2).broadcast_to([128, n2, T])

    def scbc(n2):
        return SB["scW"][:].unsqueeze(1).broadcast_to([128, n2, T])

    def quant(gt, pb, xq_name, lam_cell, wait_v):
        wg, rg, pg = SB["wgrp"], SB["remg"], SB["pg"]
        amax = [SB["scW"], SB["ivW"]]
        k.wait(V, "ve", wait_v)
        k.wait(G, "ve", wait_v)
        k.waitt(V, pp["park2"])
        k.waitt(G, pp["park2"])
        for grp in range(8):
            c0 = grp * 2
            k.op(V, (lambda e, c=c0: e.tensor_tensor(
                out=pg[:].rearrange("p (a b) -> p a b", a=2),
                in0=SB["h2"][:, c * T:(c + 2) * T].rearrange("p (a b) -> p a b", a=2),
                in1=r1bc(2), op=ALU.mult)),
                reads=("m2B",), writes=("pg",))
            k.op(V, (lambda e, c=c0: e.tensor_tensor(
                out=wg[:].rearrange("p (a b) -> p a b", a=2),
                in0=pg[:].rearrange("p (a b) -> p a b", a=2),
                in1=gbc(gt, c, 2), op=ALU.mult)),
                reads=("pg",), writes=("wgrp",))
            k.op(V, lambda e: e.scalar_tensor_tensor(
                out=pg[:], in0=wg[:], scalar=-1.0, in1=wg[:],
                op0=ALU.mult, op1=ALU.max),
                reads=("wgrp",), writes=("pg",))
            k.op(V, lambda e: e.tensor_tensor(
                out=rg[:, 0:T], in0=pg[:, 0:T], in1=pg[:, T:2 * T], op=ALU.max),
                reads=("pg",), writes=("remg",))
            if grp == 0:
                k.op(V, lambda e: e.tensor_copy(out=amax[0][:], in_=rg[:, 0:T]),
                     reads=("remg",), writes=("amax",))
            else:
                src_t, dst_t = amax[(grp + 1) % 2], amax[grp % 2]
                k.op(V, (lambda e, a=src_t, b=dst_t: e.tensor_tensor(
                    out=b[:], in0=rg[:, 0:T], in1=a[:], op=ALU.max)),
                    reads=("remg", "amax"), writes=("amax",),
                    inc="ve" if grp == 7 else None)
        amax_f = amax[7 % 2]
        k.wait(P, "ve", k.val("ve"))
        bank_ve = [0, 0]
        for j in range(8):
            b = PS[4 + j % 2]
            k.wait(P, "ve", bank_ve[j % 2])
            pe_v = k.op(P, (lambda e, jj=j, bb=b, af=amax_f: e.transpose(
                bb[:, 0:128], af[:, jj * 128:(jj + 1) * 128], s("identf"))),
                inc="pe")
            k.wait(V, "pe", pe_v)
            bank_ve[j % 2] = k.op(V, (lambda e, jj=j, bb=b: e.reduce_max(
                out=SB["tok8"][:, pb * 8 + jj:pb * 8 + jj + 1],
                in_=bb[:, 0:128], axis=AX.X)),
                writes=("tok8",), inc="ve")
        pl = lambda i: SB["tok8"][:, (pb + i) * 8:(pb + i) * 8 + 8]
        k.op(V, lambda e: e.tensor_scalar(out=pl(1), in0=pl(0), scalar1=1e-4,
                                          scalar2=None, op0=ALU.max),
             reads=("tok8",), writes=("tok8",))
        k.op(V, lambda e: e.reciprocal(out=pl(0), in_=pl(1)),
             reads=("tok8",), writes=("tok8",))
        k.op(V, lambda e: e.tensor_scalar(out=pl(2), in0=pl(0), scalar1=127.0,
                                          scalar2=None, op0=ALU.mult),
             reads=("tok8",), writes=("tok8",))
        svcol = 8 if pb == 0 else 9
        k.wait(P, "ve", k.val("ve"))
        pe_v = mm(PS[4][:, 0:1], s("onesr"), lam_cell, start=True, stop=True,
                  inc="pe")
        k.wait(V, "pe", pe_v)
        k.op(V, (lambda e: e.tensor_copy(out=sv[:, svcol:svcol + 1],
                                         in_=PS[4][:, 0:1])),
             writes=("svec",), inc="ve")
        k.op(V, (lambda e: e.tensor_scalar(out=pl(0), in0=pl(1),
                                           scalar1=1.0 / 127.0,
                                           scalar2=sv[:, svcol:svcol + 1],
                                           op0=ALU.mult, op1=ALU.mult)),
             reads=("tok8", "svec"), writes=("tok8",), inc="ve")
        k.wait(P, "ve", k.val("ve"))
        pe_v = k.op(P, (lambda e: e.transpose(PS[4][0:8, 0:128], pl(2),
                                              s("identf"))), inc="pe")
        k.wait(V, "pe", pe_v)
        ve_v = k.op(V, lambda e: e.tensor_copy(out=SB["cpad"][0:8, 0:128],
                                               in_=PS[4][0:8, 0:128]),
                    writes=("cpad",), inc="ve")
        k.wait(P, "ve", ve_v)
        for j in range(8):
            b = PS[4 + j % 2]
            k.wait(P, "ve", bank_ve[j % 2])
            pe_v = mm(b[:, 0:128], SB["sel8"][:, j * 128:(j + 1) * 128],
                      SB["cpad"][0:8, 0:128], start=True, stop=True, inc="pe")
            k.wait(V, "pe", pe_v)
            bank_ve[j % 2] = k.op(V, (lambda e, jj=j, bb=b: e.tensor_copy(
                out=SB["scW"][:, jj * 128:(jj + 1) * 128], in_=bb[:, 0:128])),
                writes=("scW",), inc="ve")
        scb_v = k.val("ve")
        k.wait(G, "ve", scb_v)
        for grp in range(8):
            c0 = grp * 2
            ve_h = [0, 0]
            ae_h = [0, 0]
            for hf in range(2):
                c = c0 + hf
                o = hf * T
                k.op(V, (lambda e, cc=c, o_=o: e.tensor_tensor(
                    out=pg[:, o_:o_ + T], in0=SB["h2"][:, cc * T:(cc + 1) * T],
                    in1=SB["m2B"][:], op=ALU.mult)),
                    reads=("m2B", "pgh%d" % hf), writes=("pgh%d" % hf,))
                k.op(V, (lambda e, cc=c, o_=o, g_=gt: e.tensor_scalar(
                    out=rg[:, o_:o_ + T], in0=pg[:, o_:o_ + T],
                    scalar1=SB[g_][:, cc:cc + 1], scalar2=None, op0=ALU.mult)),
                    reads=("pgh%d" % hf,), writes=("rgh%d" % hf,))
                k.op(V, (lambda e, o_=o: e.tensor_tensor(
                    out=wg[:, o_:o_ + T], in0=rg[:, o_:o_ + T],
                    in1=SB["scW"][:], op=ALU.mult)),
                    reads=("rgh%d" % hf, "scW"), writes=("wgh%d" % hf,))
                ve_h[hf] = k.op(V, (lambda e, o_=o: e.scalar_tensor_tensor(
                    out=rg[:, o_:o_ + T], in0=wg[:, o_:o_ + T], scalar=-1.0,
                    in1=wg[:, o_:o_ + T], op0=ALU.mult, op1=ALU.max)),
                    reads=("wgh%d" % hf,), writes=("rgh%d" % hf,), inc="ve")
            for d in range(6, -1, -1):
                for hf in range(2):
                    o = hf * T
                    k.wait(A, "ve", ve_h[hf])
                    ae_h[hf] = k.op(A, (lambda e, o_=o: e.activation(
                        out=pg[:, o_:o_ + T], in_=rg[:, o_:o_ + T],
                        func=AF.Sign, bias=-63.5, scale=1.0)),
                        reads=("rgh%d" % hf,), writes=("pgh%d" % hf,),
                        inc="ae")
                    k.wait(V, "ae", ae_h[hf])
                    ve_h[hf] = k.op(V, (lambda e, dd=d, o_=o:
                                        e.scalar_tensor_tensor(
                        out=rg[:, o_:o_ + T], in0=pg[:, o_:o_ + T],
                        scalar=-float(1 << dd) / 2.0, in1=rg[:, o_:o_ + T],
                        op0=ALU.mult, op1=ALU.add)),
                        reads=("rgh%d" % hf, "pgh%d" % hf),
                        writes=("rgh%d" % hf,), inc="ve")
            for hf in range(2):
                c = c0 + hf
                o = hf * T
                k.wait(A, "ve", ve_h[hf])
                ae_h[hf] = k.op(A, (lambda e, o_=o: e.activation(
                    out=pg[:, o_:o_ + T], in_=wg[:, o_:o_ + T], func=AF.Sign,
                    bias=0.0, scale=1.0)),
                    reads=("wgh%d" % hf,), writes=("pgh%d" % hf,), inc="ae")
                k.op(V, (lambda e, o_=o: e.tensor_scalar(
                    out=rg[:, o_:o_ + T], in0=rg[:, o_:o_ + T], scalar1=-63.5,
                    scalar2=None, op0=ALU.add)),
                    reads=("rgh%d" % hf,), writes=("rgh%d" % hf,))
                k.wait(V, "ae", ae_h[hf])
                k.op(V, (lambda e, o_=o: e.tensor_tensor(
                    out=pg[:, o_:o_ + T], in0=pg[:, o_:o_ + T],
                    in1=rg[:, o_:o_ + T], op=ALU.mult)),
                    reads=("rgh%d" % hf, "pgh%d" % hf),
                    writes=("pgh%d" % hf,))
                k.op(V, (lambda e, cc=c, o_=o, xq=xq_name: e.tensor_tensor(
                    out=SB[xq][:, cc * T:(cc + 1) * T], in0=wg[:, o_:o_ + T],
                    in1=pg[:, o_:o_ + T], op=ALU.subtract)),
                    reads=("pgh%d" % hf, "wgh%d" % hf), writes=(xq_name,),
                    inc="ve")
        return k.val("ve")

    q1_v = quant("g1t", 0, "xq1", pp["cell"]("s1_lam"), pp["r1b_v"])
    q2_v = quant("g2t", 3, "xq2", pp["cell"]("s2_lam"), q1_v)
    pp["q12_v"] = q2_v
    return pp


def _build_p6(nc, k, t, pp):
    SB, PS = t["SB"], t["PS"]
    s, mm, dma_s, dma_g = pp["s"], pp["mm"], pp["dma_s"], pp["dma_g"]
    sv = SB["svec"]

    k.wait(P, "ve", pp["q12_v"])
    for pb, dst in ((0, "i1B"), (3, "i2B")):
        pl0 = SB["tok8"][:, pb * 8:pb * 8 + 8]
        pe_v = k.op(P, (lambda e, p_=pl0: e.transpose(PS[4][0:8, 0:128], p_,
                                                      s("identf"))), inc="pe")
        k.wait(V, "pe", pe_v)
        ve_v = k.op(V, lambda e: e.tensor_copy(out=SB["cpad"][0:8, 0:128],
                                               in_=PS[4][0:8, 0:128]),
                    writes=("cpad",), inc="ve")
        k.wait(P, "ve", ve_v)
        for j in range(8):
            pe_v = mm(PS[4][:, 0:128], SB["sel8"][:, j * 128:(j + 1) * 128],
                      SB["cpad"][0:8, 0:128], start=True, stop=True, inc="pe")
            k.wait(V, "pe", pe_v)
            ve_v = k.op(V, (lambda e, jj=j, d=dst: e.tensor_copy(
                out=SB[d][:, jj * 128:(jj + 1) * 128], in_=PS[4][:, 0:128])),
                writes=(dst,), inc="ve")
            k.wait(P, "ve", ve_v)
    iB_v = k.val("ve")

    fs1 = [SB["f10"], SB["f11"]]
    fs2 = [SB["f20"], SB["f21"]]
    hm = [SB["hm0"], SB["hm1"]]
    amx_half = [SB["remg"][:, 0:1024], SB["remg"][:, 1024:2048]]
    k.op(V, lambda e: e.memset(SB["remg"][:, 0:1024], 0.0), writes=("remg",))
    k.op(V, lambda e: e.tensor_tensor(out=SB["acc1"][:, 0:FC],
                                      in0=SB["g3t"][:], in1=SB["g3t"][:],
                                      op=ALU.mult),
         reads=("acc1",), writes=("acc1",), inc="ve")
    k.wait(V, "ve", iB_v)
    k.wait(A, "ve", iB_v)
    k.wait(E, "po", pp["wo_cast_po"])
    hm_qg = {}
    evac_ve = {}
    pe_og = {}
    half_pe = {}
    amx_po = {}
    sq_ae = {}
    ssq_pe = {}
    sign_ae = {}
    sqmax_ve = {}
    for og in range(FC):
        b1 = [PS[0], PS[1]] if og % 2 == 0 else [PS[2], PS[3]]
        b2 = [PS[2], PS[3]] if og % 2 == 0 else [PS[0], PS[1]]
        for wi, (wd, fs, fa, fb, fq, svc) in enumerate((
                (t["w1"], fs1, "fa1", "fb1", "fq1", 5),
                (t["w2"], fs2, "fa2", "fb2", "fq2", 6))):
            for hf in range(2):
                k.wait(E, "ae", sign_ae.get((og - 1, wi, hf), 0))
                ld = dma_s(fs[hf][:], wd[og][:, hf * 1024:(hf + 1) * 1024],
                           "sf%d" % (wi * 2 + hf))
                k.waitt(A, ld)
                k.wait(A, "pe", half_pe.get((og, wi, hf - 1),
                                            half_pe.get((og - 1, wi, 1), 0)))
                k.op(A, (lambda e, f_=fs[hf], d=fa, sc_=svc: e.activation(
                    out=SB[d][:], in_=f_[:], func=AF.Sign,
                    scale=sv[:, sc_:sc_ + 1], bias=sv[:, 11:12])),
                    writes=(fa,))
                ae_v = k.op(A, (lambda e, f_=fs[hf], d=fb, sc_=svc: e.activation(
                    out=SB[d][:], in_=f_[:], func=AF.Sign,
                    scale=sv[:, sc_:sc_ + 1], bias=sv[:, 12:13])),
                    writes=(fb,), inc="ae")
                sign_ae[(og, wi, hf)] = ae_v
                k.wait(V, "ae", ae_v)
                ve_v = k.op(V, (lambda e, a=fa, b_=fb, q=fq, h_=hf: e.tensor_tensor(
                    out=SB[q][:, h_ * 512:(h_ + 1) * 512].rearrange("p a -> p a")
                    if False else SB[q][:], in0=SB[a][:], in1=SB[b_][:],
                    op=ALU.add)),
                    reads=(fa, fb), writes=(fq,), inc="ve")
                k.wait(P, "ve", ve_v)
                if og >= 2:
                    k.wait(P, "ve", evac_ve.get(og - 2, 0))
                banks = b1 if wi == 0 else b2
                fqt = SB[fq]
                for kc in range(8):
                    kcg = hf * 8 + kc
                    xq = SB["xq1"] if wi == 0 else SB["xq2"]
                    for th in range(2):
                        mm(banks[th][:, 0:512],
                           fqt[:, kc * 128:(kc + 1) * 128],
                           xq[:, kcg * T + th * 512:kcg * T + th * 512 + 512],
                           start=(kcg == 0), stop=(kcg == KC - 1),
                           inc="pe" if (kc == 7 and th == 1) else None)
                half_pe[(og, wi, hf)] = k.val("pe")
        pe_og[og] = k.val("pe")
        k.wait(V, "pe", pe_og[og])
        for th in range(2):
            k.op(V, (lambda e, b=b1[th], t_=th: e.tensor_tensor(
                out=SB["sil0"][:, t_ * 512:(t_ + 1) * 512], in0=b[:, 0:512],
                in1=SB["i1B"][:, t_ * 512:(t_ + 1) * 512], op=ALU.mult)),
                reads=("i1B",), writes=("sil0",), inc="ve" if th == 1 else None)
        ve_r1 = k.val("ve")
        k.wait(A, "ve", ve_r1)
        ae_si = k.op(A, lambda e: e.activation(out=s("sil1"), in_=s("sil0"),
                                               func=AF.Silu),
                     writes=("sil1",), inc="ae")
        if og >= 2:
            k.waitt(V, hm_qg[og - 2])
            k.wait(V, "ae", sq_ae[og - 2])
        for th in range(2):
            k.op(V, (lambda e, b=b2[th], t_=th, h_=hm[og % 2]: e.tensor_tensor(
                out=h_[:, t_ * 512:(t_ + 1) * 512], in0=b[:, 0:512],
                in1=SB["i2B"][:, t_ * 512:(t_ + 1) * 512], op=ALU.mult)),
                reads=("i2B",), writes=("hm",),
                inc="ve" if th == 1 else None)
        k.wait(V, "ae", ae_si)
        ve_hm = k.op(V, (lambda e, h_=hm[og % 2]: e.tensor_tensor(
            out=h_[:], in0=s("sil1"), in1=h_[:], op=ALU.mult)),
            reads=("hm", "sil1"), writes=("hm",), inc="ve")
        evac_ve[og] = k.val("ve")
        k.wait(G, "ve", ve_hm)
        hm_qg[og] = dma_g(t["hmid_dr"][og], hm[og % 2][:],
                          "gh%d" % (og % 2))
        amx_po[og] = k.val("po")
        k.wait(A, "ve", ve_hm)
        k.wait(A, "pe", ssq_pe.get(og - 1, 0))
        k.wait(A, "ve", sqmax_ve.get(og - 1, 0))
        ae_sq = k.op(A, (lambda e, h_=hm[og % 2]: e.activation(
            out=SB["sqb"][:], in_=h_[:], func=AF.Square)),
            writes=("sqb",), inc="ae")
        sq_ae[og] = ae_sq
        k.wait(V, "ae", ae_sq)
        k.op(V, (lambda e, a=amx_half[og % 2], b=amx_half[(og + 1) % 2],
                 o_=og: e.scalar_tensor_tensor(
                 out=b[:], in0=SB["sqb"][:], scalar=SB["acc1"][:, o_:o_ + 1],
                 in1=a[:], op0=ALU.mult, op1=ALU.max)),
             reads=("sqb", "remg", "acc1"), writes=("remg",), inc="ve")
        sqmax_ve[og] = k.val("ve")
        k.wait(P, "ae", ae_sq)
        for th in range(2):
            mm(PS[4 + th][0:1, 0:512], s("onesb"),
               SB["sqb"][:, th * 512:(th + 1) * 512],
               start=(og == 0), stop=(og == FC - 1),
               inc="pe" if th == 1 else None)
        ssq_pe[og] = k.val("pe")
    ss3_pe = k.val("pe")
    amx3_ve = k.val("ve")
    hm_last = hm_qg[FC - 1]
    pp["ss3_pe"] = ss3_pe
    pp["amx3_ve"] = amx3_ve
    pp["amx3_final"] = amx_half[FC % 2]
    pp["hm_last"] = hm_last
    pp["p6_ae"] = k.val("ae")
    pp["p6_pe"] = k.val("pe")
    pp["p6_ve"] = k.val("ve")
    return pp


def _build_p7(nc, k, t, pp):
    SB, PS = t["SB"], t["PS"]
    s, mm, dma_s, dma_g = pp["s"], pp["mm"], pp["dma_s"], pp["dma_g"]
    sv = SB["svec"]
    rows = SB["rows"]
    cell = pp["cell"]

    k.wait(V, "pe", pp["ss3_pe"])
    for th in range(2):
        k.op(V, (lambda e, t_=th: e.tensor_scalar(
            out=rows[:, t_ * 512:(t_ + 1) * 512], in0=PS[4 + t_][0:1, 0:512],
            scalar1=1.0 / FF, scalar2=None, op0=ALU.mult)),
            writes=("rows",), inc="ve" if th == 1 else None)
    v_ms3 = k.val("ve")
    k.wait(A, "ve", v_ms3)
    k.op(A, lambda e: e.activation(out=rows[:, TH:TH + T], in_=rows[:, 0:T],
                                   func=AF.Ln, bias=sv[0:1, 3:4],
                                   scale=sv[0:1, 0:1]),
         writes=("rows3",))
    k.op(A, lambda e: e.activation(out=rows[:, 0:T], in_=rows[:, TH:TH + T],
                                   func=AF.Exp, bias=sv[0:1, 4:5],
                                   scale=sv[0:1, 1:2]),
         reads=("rows3",), writes=("rows",), inc="ae")
    rstd3_a = k.val("ae")
    k.wait(V, "ae", rstd3_a)
    k.wait(P, "ae", rstd3_a)
    pp["bcast"](rows[:, 0:T], s("rs3B"), T, PS[0], "rs3B")
    amxf = pp["amx3_final"]
    k.wait(P, "ve", pp["amx3_ve"])
    bank_ve = [0, 0]
    for j in range(8):
        b = PS[2 + j % 2]
        k.wait(P, "ve", bank_ve[j % 2])
        pe_v = k.op(P, (lambda e, jj=j, bb=b, af=amxf: e.transpose(
            bb[:, 0:128], af[:, jj * 128:(jj + 1) * 128], s("identf"))),
            inc="pe")
        k.wait(V, "pe", pe_v)
        bank_ve[j % 2] = k.op(V, (lambda e, jj=j, bb=b: e.reduce_max(
            out=SB["tok8"][:, 48 + jj:48 + jj + 1], in_=bb[:, 0:128],
            axis=AX.X)), writes=("tok8",), inc="ve")
    pl6 = SB["tok8"][:, 48:56]
    pl7 = SB["tok8"][:, 56:64]
    k.op(V, lambda e: e.nop(), inc="ve")
    k.wait(A, "ve", k.val("ve"))
    k.op(A, lambda e: e.activation(out=pl6, in_=pl6, func=AF.Sqrt,
                                   bias=sv[:, 4:5], scale=sv[:, 0:1]),
         writes=("tok8A",), inc="ae")
    k.wait(V, "ae", k.val("ae"))
    rpl = SB["cpad"][:, 96:104]
    for j in range(8):
        b = PS[2 + j % 2]
        k.wait(P, "ve", bank_ve[j % 2])
        pe_v = k.op(P, (lambda e, jj=j, bb=b: e.transpose(
            bb[:, 0:128], s("rs3B")[:, jj * 128:(jj + 1) * 128],
            s("identf"))), reads=("rs3B",), inc="pe")
        k.wait(V, "pe", pe_v)
        bank_ve[j % 2] = k.op(V, (lambda e, jj=j, bb=b, r_=rpl: e.tensor_copy(
            out=r_[:, jj:jj + 1], in_=bb[:, 0:1])),
            writes=("cpad",), inc="ve")
    k.op(V, lambda e: e.tensor_tensor(out=pl6, in0=pl6, in1=rpl,
                                      op=ALU.mult),
         reads=("tok8", "tok8A", "cpad"), writes=("tok8",), inc="ve")
    k.op(V, lambda e: e.tensor_scalar(out=pl7, in0=pl6, scalar1=1e-4,
                                      scalar2=None, op0=ALU.max),
         reads=("tok8", "tok8A"), writes=("tok8",))
    k.op(V, lambda e: e.reciprocal(out=pl6, in_=pl7),
         reads=("tok8",), writes=("tok8",))
    k.op(V, lambda e: e.tensor_scalar(out=pl6, in0=pl6, scalar1=127.0,
                                      scalar2=None, op0=ALU.mult),
         reads=("tok8",), writes=("tok8",), inc="ve")
    k.wait(P, "ve", k.val("ve"))
    pe_v = mm(PS[0][:, 0:1], s("onesr"), cell("s3_lam"), start=True, stop=True,
              inc="pe")
    k.wait(V, "pe", pe_v)
    k.op(V, lambda e: e.tensor_copy(out=sv[:, 10:11], in_=PS[0][:, 0:1]),
         writes=("svec",))
    k.op(V, lambda e: e.tensor_scalar(out=pl7, in0=pl7, scalar1=1.0 / 127.0,
                                      scalar2=sv[:, 10:11], op0=ALU.mult,
                                      op1=ALU.mult),
         reads=("tok8", "svec"), writes=("tok8",), inc="ve")
    k.wait(P, "ve", k.val("ve"))
    for pl, dst in ((pl6, "u3g"), (pl7, "i3B")):
        pe_v = k.op(P, (lambda e, p_=pl: e.transpose(PS[0][0:8, 0:128], p_,
                                                     s("identf"))), inc="pe")
        k.wait(V, "pe", pe_v)
        ve_v = k.op(V, lambda e: e.tensor_copy(out=SB["cpad"][0:8, 0:128],
                                               in_=PS[0][0:8, 0:128]),
                    writes=("cpad",), inc="ve")
        k.wait(P, "ve", ve_v)
        for j in range(8):
            b = PS[2 + j % 2]
            k.wait(P, "ve", bank_ve[j % 2])
            pe_v = mm(b[:, 0:128], SB["sel8"][:, j * 128:(j + 1) * 128],
                      SB["cpad"][0:8, 0:128], start=True, stop=True, inc="pe")
            k.wait(V, "pe", pe_v)
            bank_ve[j % 2] = k.op(V, (lambda e, jj=j, bb=b, d=dst: e.tensor_copy(
                out=SB[d][:, jj * 128:(jj + 1) * 128], in_=bb[:, 0:128])),
                writes=(dst,), inc="ve")
    k.op(V, lambda e: e.tensor_tensor(out=s("rs3B"), in0=s("rs3B"),
                                      in1=s("u3g"), op=ALU.mult),
         reads=("rs3B", "u3g"), writes=("rs3B",), inc="ve")
    rs3_v = k.val("ve")

    hin = [SB["hin0"], SB["hin1"]]
    k.wait(V, "ve", rs3_v)
    k.wait(V, "gh0", k.val("gh0"))
    k.wait(V, "gh1", k.val("gh1"))
    k.wait(E, "gh0", k.val("gh0"))
    k.wait(E, "gh1", k.val("gh1"))
    casc_ve = {}
    for kc in range(FC):
        if kc >= 2:
            k.wait(E, "ve", casc_ve[kc - 2])
        ld = dma_s(hin[kc % 2][:], t["hmid_dr"][kc], "sh%d" % (kc % 2))
        k.waitt(V, ld)
        u3, r3, p3 = SB["u3g"], SB["rem3"], SB["p3"]
        HF = ((0, 512), (512, 512))
        ve_h = [0, 0]
        ae_h = [0, 0]
        for hf, (o, l) in enumerate(HF):
            k.op(V, (lambda e, c=kc, h_=hin[kc % 2], o_=o, l_=l:
                     e.scalar_tensor_tensor(
                out=u3[:, o_:o_ + l_], in0=h_[:, o_:o_ + l_],
                scalar=SB["g3t"][:, c:c + 1], in1=SB["rs3B"][:, o_:o_ + l_],
                op0=ALU.mult, op1=ALU.mult)),
                reads=("u3g%d" % hf, "rs3B"), writes=("u3g%d" % hf,))
            ve_h[hf] = k.op(V, (lambda e, o_=o, l_=l: e.scalar_tensor_tensor(
                out=r3[:, o_:o_ + l_], in0=u3[:, o_:o_ + l_], scalar=-1.0,
                in1=u3[:, o_:o_ + l_], op0=ALU.mult, op1=ALU.max)),
                reads=("u3g%d" % hf,), writes=("rem3%d" % hf,), inc="ve")
        for d in range(6, -1, -1):
            for hf, (o, l) in enumerate(HF):
                k.wait(A, "ve", ve_h[hf])
                ae_h[hf] = k.op(A, (lambda e, o_=o, l_=l: e.activation(
                    out=p3[:, o_:o_ + l_], in_=r3[:, o_:o_ + l_],
                    func=AF.Sign, bias=-63.5, scale=1.0)),
                    reads=("rem3%d" % hf,), writes=("p3%d" % hf,), inc="ae")
                k.wait(V, "ae", ae_h[hf])
                ve_h[hf] = k.op(V, (lambda e, dd=d, o_=o, l_=l:
                                    e.scalar_tensor_tensor(
                    out=r3[:, o_:o_ + l_], in0=p3[:, o_:o_ + l_],
                    scalar=-float(1 << dd) / 2.0, in1=r3[:, o_:o_ + l_],
                    op0=ALU.mult, op1=ALU.add)),
                    reads=("rem3%d" % hf, "p3%d" % hf),
                    writes=("rem3%d" % hf,), inc="ve")
        for hf, (o, l) in enumerate(HF):
            k.wait(A, "ve", ve_h[hf])
            ae_h[hf] = k.op(A, (lambda e, o_=o, l_=l: e.activation(
                out=p3[:, o_:o_ + l_], in_=u3[:, o_:o_ + l_], func=AF.Sign,
                bias=0.0, scale=1.0)),
                reads=("u3g%d" % hf,), writes=("p3%d" % hf,), inc="ae")
            k.op(V, (lambda e, o_=o, l_=l: e.tensor_scalar(
                out=r3[:, o_:o_ + l_], in0=r3[:, o_:o_ + l_], scalar1=-63.5,
                scalar2=None, op0=ALU.add)),
                reads=("rem3%d" % hf,), writes=("rem3%d" % hf,))
            k.wait(V, "ae", ae_h[hf])
            k.op(V, (lambda e, o_=o, l_=l: e.tensor_tensor(
                out=p3[:, o_:o_ + l_], in0=p3[:, o_:o_ + l_],
                in1=r3[:, o_:o_ + l_], op=ALU.mult)),
                reads=("rem3%d" % hf, "p3%d" % hf), writes=("p3%d" % hf,))
            k.op(V, (lambda e, c=kc, o_=o, l_=l: e.tensor_tensor(
                out=SB["xq3"][:, c * T + o_:c * T + o_ + l_],
                in0=u3[:, o_:o_ + l_], in1=p3[:, o_:o_ + l_],
                op=ALU.subtract)),
                reads=("p3%d" % hf, "u3g%d" % hf), writes=("xq3",), inc="ve")
        casc_ve[kc] = k.val("ve")
    xq3_v = k.val("ve")

    w3s = [SB["w30"], SB["w31"]]
    wq3 = [SB["wq30"], SB["wq31"]]
    xm = [SB["xm0"], SB["xm1"]]
    k.wait(P, "ve", xq3_v)
    k.wait(E, "ae", pp["p6_ae"])
    k.wait(E, "pe", pp["p6_pe"])
    k.wait(E, "ve", pp["p6_ve"])
    k.waitt(E, pp["park2"])
    sgn3_ae = {}
    add3_ve = {}
    pe_g = {}
    evac3_ve = {}
    pe_oc = {}
    tr_ve = [0, 0]
    os1_qg = [0, 0]
    for oc in range(KC):
        banks = [PS[0], PS[1]] if oc % 2 == 0 else [PS[2], PS[3]]
        if oc >= 2:
            k.wait(E, "ve", evac3_ve[oc - 2])
        xm_ld = dma_s(xm[oc % 2][:], t["xm_park"][oc], "sr%d" % (oc % 2))
        if oc >= 2:
            k.wait(P, "ve", evac3_ve[oc - 2])
        for pc in range(8):
            g = oc * 8 + pc
            if g < 2:
                k.wait(E, "po", k.val("po"))
            k.wait(E, "ae", sgn3_ae.get(g - 2, 0))
            ld = dma_s(w3s[g % 2][:], t["w3"][oc][:, pc * 1024:(pc + 1) * 1024],
                       "sx%d" % (g % 2))
            k.waitt(A, ld)
            k.wait(A, "ve", add3_ve.get(g - 1, 0))
            k.op(A, (lambda e, f_=w3s[g % 2]: e.activation(
                out=s("wa3"), in_=f_[:], func=AF.Sign, scale=sv[:, 7:8],
                bias=sv[:, 11:12])), writes=("wa3",))
            ae_v = k.op(A, (lambda e, f_=w3s[g % 2]: e.activation(
                out=s("wb3"), in_=f_[:], func=AF.Sign, scale=sv[:, 7:8],
                bias=sv[:, 12:13])), writes=("wb3",), inc="ae")
            sgn3_ae[g] = ae_v
            k.wait(V, "ae", ae_v)
            k.wait(V, "pe", pe_g.get(g - 2, 0))
            ve_v = k.op(V, (lambda e, q=wq3[g % 2]: e.tensor_tensor(
                out=q[:], in0=s("wa3"), in1=s("wb3"), op=ALU.add)),
                reads=("wa3", "wb3"), writes=("wq3",), inc="ve")
            add3_ve[g] = ve_v
            k.wait(P, "ve", ve_v)
            for kc in range(8):
                kcg = pc * 8 + kc
                for th in range(2):
                    mm(banks[th][:, 0:512],
                       wq3[g % 2][:, kc * 128:(kc + 1) * 128],
                       SB["xq3"][:, kcg * T + th * 512:kcg * T + th * 512 + 512],
                       start=(kcg == 0), stop=(kcg == FC - 1),
                       inc="pe" if (kc == 7 and th == 1) else None)
            pe_g[g] = k.val("pe")
        pe_oc[oc] = k.val("pe")
        k.wait(V, "pe", pe_oc[oc])
        k.waitt(V, xm_ld)
        for th in range(2):
            k.op(V, (lambda e, b=banks[th], t_=th, c=oc: e.scalar_tensor_tensor(
                out=SB["os0"][:, t_ * 512:(t_ + 1) * 512], in0=b[:, 0:512],
                scalar=SB["modv"][:, 5 * KC + c:5 * KC + c + 1],
                in1=SB["i3B"][:, t_ * 512:(t_ + 1) * 512],
                op0=ALU.mult, op1=ALU.mult)),
                reads=("i3B", "modv"), writes=("os0",))
            k.op(V, (lambda e, t_=th, c=oc: e.tensor_tensor(
                out=SB["os0"][:, t_ * 512:(t_ + 1) * 512],
                in0=SB["os0"][:, t_ * 512:(t_ + 1) * 512],
                in1=xm[c % 2][:, t_ * 512:(t_ + 1) * 512], op=ALU.add)),
                reads=("os0",), writes=("os0",), inc="ve" if th == 1 else None)
        evac3_ve[oc] = k.val("ve")
        k.wait(P, "ve", evac3_ve[oc])
        for j in range(QB):
            b = PS[4 + j % 2]
            k.wait(P, "ve", tr_ve[j % 2])
            pe_v = k.op(P, (lambda e, jj=j, bb=b: e.transpose(
                bb[:, 0:128], SB["os0"][:, jj * 128:(jj + 1) * 128],
                s("identf"))), inc="pe")
            k.wait(V, "pe", pe_v)
            k.waitt(V, os1_qg[j % 2])
            tr_ve[j % 2] = k.op(V, (lambda e, jj=j, bb=b: e.tensor_copy(
                out=SB["os1"][:, (jj % 2) * 128:(jj % 2) * 128 + 128],
                in_=bb[:, 0:128])), writes=("os1",), inc="ve")
            k.wait(G, "ve", tr_ve[j % 2])
            os1_qg[j % 2] = dma_g(
                t["out"][j][:, oc * 128:(oc + 1) * 128],
                SB["os1"][:, (j % 2) * 128:(j % 2) * 128 + 128],
                "go%d" % (j % 2))
    k.wait(G, "go0", k.val("go0"))
    k.wait(G, "go1", k.val("go1"))
    return pp


_CACHE = {}


def _get_program():
    if "nc" in _CACHE:
        return _CACHE["nc"]
    nc = bass.Bass("TRN2", target_bir_lowering=False,
                   detect_race_conditions=False)
    k = K()
    t = _alloc(nc)
    import contextlib
    stack = contextlib.ExitStack()
    pp = _build(nc, k, t)
    pp = _build_p23(nc, k, t, pp)
    pp = _build_p45(nc, k, t, pp)
    pp = _build_p5b(nc, k, t, pp)
    pp = _build_p6(nc, k, t, pp)
    pp = _build_p7(nc, k, t, pp)
    for name in sorted(k.semv):
        k.sems[name] = stack.enter_context(nc.semaphore(name))
    with nc.Block() as block:
        @block.sync
        def _(eng):
            for f in k.q["sync"]:
                f(eng)

        @block.tensor
        def _(eng):
            for f in k.q["tensor"]:
                f(eng)

        @block.vector
        def _(eng):
            for f in k.q["vector"]:
                f(eng)

        @block.scalar
        def _(eng):
            for f in k.q["scalar"]:
                f(eng)

        @block.gpsimd
        def _(eng):
            for f in k.q["gpsimd"]:
                f(eng)
    stack.close()
    _CACHE["nc"] = nc
    return nc


def kernel(**inputs):
    in_maps = _host_prepare(
        inputs["L0_attn_x"], inputs["L0_attn_W_qkv"], inputs["L0_attn_W_out"],
        inputs["L0_attn_b_out"], inputs["L0_ffn_w1"], inputs["L0_ffn_g1"],
        inputs["L0_ffn_w2"], inputs["L0_ffn_g2"], inputs["L0_ffn_w3"],
        inputs["L0_ffn_g3"], inputs["L0_c"], inputs["L0_wmod"])
    nc = _get_program()
    res = run_bass_kernel_spmd(nc, in_maps, core_ids=list(range(NC)))
    outs = [np.asarray(r["out"], dtype=np.float32).reshape(T, D)
            for r in res.results]
    return np.concatenate(outs, axis=0).reshape(1, S, D)


# version 11 (final)
# speedup vs baseline: 1.1351x; 1.1351x over previous
import math
import numpy as np
import ml_dtypes

import concourse.bass as bass
import concourse.mybir as mybir
from concourse.bass_utils import run_bass_kernel_spmd

F32 = mybir.dt.float32
F16 = mybir.dt.float16
BF16 = mybir.dt.bfloat16
AF = mybir.ActivationFunctionType
ALU = mybir.AluOpType
AX = mybir.AxisListType

S, D, NH, HD, FF, NC = 8192, 2048, 16, 128, 8192, 8
T = S // NC
TH = T + 128
KC = D // 128
FC = FF // 128
QB = T // 128
TC = TH // 128
THETA = 10000.0
NEG = -1.0e30
ATANH_HALF = 0.5493061443340549
NWM = 6 * D
MS = NWM // 128

DBG_PHASE = 0


def _sw(A, name=None):
    Kd, Nd = A.shape
    return np.ascontiguousarray(
        A.reshape(Kd // 128, 128, Nd // 128, 128).transpose(2, 1, 0, 3)
        .reshape(Nd // 128, 128, Kd))


def _host_prepare(x, W_qkv, W_out, b_out, w1, g1, w2, g2, w3, g3, c, wmod):
    f32 = np.float32
    X = np.ascontiguousarray(np.asarray(x, dtype=f32).reshape(S, D))
    Wq = np.asarray(W_qkv, dtype=f32)
    n = np.arange(3 * D)
    kind, r = n // D, n % D
    hh, hd = r // HD, r % HD
    Wq_sw = _sw(Wq[:, hh * (HD * 3) + hd * 3 + kind])
    Wo_sw = _sw(np.asarray(W_out, dtype=f32))
    w1_sw = _sw(np.ascontiguousarray(np.asarray(w1, dtype=f32).T))
    w2_sw = _sw(np.ascontiguousarray(np.asarray(w2, dtype=f32).T))
    w3_sw = _sw(np.ascontiguousarray(np.asarray(w3, dtype=f32).T))
    wmod_sw = _sw(np.asarray(wmod, dtype=f32))
    vecs = dict(
        b_out2=np.asarray(b_out, dtype=f32).reshape(KC, 128).copy(),
        g1_2=np.asarray(g1, dtype=f32).reshape(KC, 128).copy(),
        g2_2=np.asarray(g2, dtype=f32).reshape(KC, 128).copy(),
        g3_2=np.asarray(g3, dtype=f32).reshape(FC, 128).copy(),
        c2=np.asarray(c, dtype=f32).reshape(KC, 128).copy())
    cosg, sing = _rope_tables()
    in_maps = []
    for cix in range(NC):
        start = cix * T
        xs = np.zeros((TH, D), dtype=f32)
        lo = max(0, start - 128)
        xs[128 - (start - lo):] = X[lo:start + T]
        pk = np.clip(np.arange(start - 128, start + T), 0, S - 1)
        cK = np.ascontiguousarray(cosg[pk].T)
        sK = np.ascontiguousarray(sing[pk].T)
        cQ = np.ascontiguousarray(cosg[start:start + T].T)
        sQ = np.ascontiguousarray(sing[start:start + T].T)
        masks = np.full((QB, 128, 256), NEG, dtype=f32)
        rr = np.arange(128)[:, None]
        cc = np.arange(256)[None, :]
        for qb in range(QB):
            i = start + 128 * qb + rr
            j = start - 128 + 128 * qb + cc
            ok = (j >= 0) & (j <= i) & (j >= i - 128)
            masks[qb][ok] = 0.0
        in_maps.append(dict(
            x_sh=np.ascontiguousarray(xs.reshape(TC, 128, D)),
            wq=Wq_sw, wo=Wo_sw, w1=w1_sw, w2=w2_sw, w3=w3_sw, wmod=wmod_sw,
            cosK=cK, sinK=sK, cosQ=cQ, sinQ=sQ,
            masks=masks, **vecs))
    return in_maps


_TBL = {}


def _rope_tables():
    if "c" in _TBL:
        return _TBL["c"], _TBL["s"]
    try:
        import jax
        import jax.numpy as jnp
        with jax.default_device(jax.devices("cpu")[0]):
            inv_freq = 1.0 / THETA ** (jnp.arange(0, HD, 2, dtype=jnp.float32)
                                       / HD)
            tt = jnp.arange(S, dtype=jnp.float32)
            ang = jnp.repeat(tt[:, None] * inv_freq[None, :], 2, axis=-1)
            c, s_ = np.asarray(jnp.cos(ang)), np.asarray(jnp.sin(ang))
    except Exception:
        inv = (np.float32(1.0) / np.power(
            np.float32(THETA),
            np.arange(0, HD, 2).astype(np.float32) / np.float32(HD)))
        ang = np.repeat(np.arange(S, dtype=np.float32)[:, None] * inv[None, :],
                        2, axis=-1)
        c, s_ = np.cos(ang), np.sin(ang)
    _TBL["c"], _TBL["s"] = c, s_
    return c, s_


ENGINES = ("sync", "tensor", "vector", "scalar", "gpsimd")
V, P, A, E, G = "vector", "tensor", "scalar", "sync", "gpsimd"


class K:

    def __init__(self):
        self.q = {e: [] for e in ENGINES}
        self.semv = {}
        self.sems = {}
        self.hi = {e: {} for e in ENGINES}
        self.pend = {e: set() for e in ENGINES}

    def plan(self, name, n=1):
        self.semv[name] = self.semv.get(name, 0) + n
        return self.semv[name]

    def val(self, name):
        return self.semv.get(name, 0)

    def waitt(self, eng, tok):
        if tok is None or tok == 0:
            return
        self.wait(eng, tok[0], tok[1])

    def wait(self, eng, name, value):
        if value is None or value <= 0:
            return
        if self.hi[eng].get(name, -1) >= value:
            return
        self.hi[eng][name] = value
        self.q[eng].append(lambda e, n=name, v=value: e.wait_ge(self.sems[n], v))

    def op(self, eng, fn, reads=(), writes=(), inc=None, incn=1):
        tags = set(reads) | set(writes)
        if tags & self.pend[eng]:
            self.q[eng].append(lambda e: e.drain())
            self.pend[eng].clear()
        tgt = self.plan(inc, incn) if inc else None

        def f(e, fn=fn, inc=inc, incn=incn):
            ins = fn(e)
            if inc is not None:
                ins.then_inc(self.sems[inc], incn)
        self.q[eng].append(f)
        self.pend[eng].update(writes)
        return tgt


def _alloc(nc):
    t = {}

    def din(name, shape, dtype=F32):
        t[name] = nc.dram_tensor(name, shape, dtype, kind="ExternalInput").ap()

    din("x_sh", [TC, 128, D])
    din("wq", [48, 128, D]); din("wo", [KC, 128, D])
    din("w1", [FC, 128, D]); din("w2", [FC, 128, D]); din("w3", [KC, 128, FF])
    din("wmod", [MS, 128, D])
    din("b_out2", [KC, 128]); din("g1_2", [KC, 128]); din("g2_2", [KC, 128])
    din("g3_2", [FC, 128]); din("c2", [KC, 128])
    din("cosK", [128, TH]); din("sinK", [128, TH])
    din("cosQ", [128, T]); din("sinQ", [128, T])
    din("masks", [QB, 128, 256])
    t["out"] = nc.dram_tensor("out", [QB, 128, D], F32, kind="ExternalOutput").ap()
    if DBG_PHASE:
        t["dbg"] = nc.dram_tensor("dbg", [128, 18432], F32, kind="ExternalOutput").ap()
    t["x_park"] = nc.dram_tensor("x_park", [KC, 128, T], F32, kind="Internal").ap()
    t["xm_park"] = nc.dram_tensor("xm_park", [KC, 128, T], F32, kind="Internal").ap()
    t["v_dram"] = nc.dram_tensor("v_dram", [NH, 128, TH], BF16, kind="Internal").ap()
    t["hmid_dr"] = nc.dram_tensor("hmid_dr", [FC, 128, T], F16, kind="Internal").ap()

    t["ident_d"] = nc.inline_tensor(np.eye(128, dtype=np.float32), "ident_d").ap()
    t["identb_d"] = nc.inline_tensor(np.eye(128, dtype=ml_dtypes.bfloat16),
                                     "identb_d").ap()
    Rnp = np.zeros((128, 128), dtype=ml_dtypes.bfloat16)
    for i in range(64):
        Rnp[2 * i + 1, 2 * i] = -1.0
        Rnp[2 * i, 2 * i + 1] = 1.0
    t["R_d"] = nc.inline_tensor(Rnp, "R_d").ap()
    t["ones_d"] = nc.inline_tensor(np.ones((1, 128), dtype=np.float32), "ones_d").ap()
    seln = np.zeros((8, 8 * 128), dtype=np.float32)
    for j in range(8):
        seln[j, j * 128:(j + 1) * 128] = 1.0
    t["sel8_d"] = nc.inline_tensor(seln, "sel8_d").ap()

    arena = nc.alloc_sbuf_tensor("arena", [128, 206848], mybir.dt.uint8)
    A0 = nc.lookup_mloc(arena).addr
    SB = {}

    def at(name, rel, shape, dtype=F32):
        SB[name] = nc.alloc_sbuf_tensor_at(name, shape, dtype, offset=A0 + rel)

    KB = 1024
    at("identf", 0, [128, 128])
    at("Rb", 512, [128, 128], BF16)
    at("onesr", 768, [1, 128])
    at("onesb", 1280, [128, 1], BF16)
    at("scTb", 1312, [128, 16], BF16)
    at("sel8", 1344, [8, 8 * 128])
    at("bout", 5440, [128, KC])
    at("g1t", 5504, [128, KC])
    at("g2t", 5568, [128, KC])
    at("g3t", 5632, [128, FC])
    at("modv", 5888, [128, 8 * KC])
    at("svec", 6400, [128, 24])
    at("tok8", 6496, [128, 64])
    at("c2t", 6752, [16, 128])
    at("cpad", 7264, [128, 128])
    at("acc1", 7776, [128, 256])
    at("acc2", 8800, [128, 256])
    at("identb", 9824, [128, 128], BF16)
    at("onesf", 10080, [128, 1])
    at("rows", 10 * KB, [1, 2304])
    at("xm1", 10 * KB, [128, 1024])
    RT = 19456
    at("cosKt", RT, [128, TH]); at("sinKt", RT + 4608, [128, TH])
    at("cosQt", RT + 9216, [128, T]); at("sinQt", RT + 13312, [128, T])
    at("maskt", RT + 17408, [128, QB * 256])
    at("wgrp", RT, [128, 2048])
    at("remg", RT + 8192, [128, 2048])
    at("pg", RT + 16384, [128, 2048])
    at("hin0", RT, [128, T], F16)
    at("hin1", RT + 2048, [128, T], F16)
    at("u3g", RT + 4096, [128, 1024])
    at("rem3", RT + 8192, [128, 1024])
    at("p3", RT + 12288, [128, 1024])
    at("rs3B", RT + 16384, [128, 1024])
    at("i3B", RT + 20480, [128, 1024])
    at("sqb", RT + 20480, [128, 1024], BF16)
    STb = 45568
    at("st0", STb, [128, 2048]); at("st1", STb + 8192, [128, 2048])
    at("stq0", STb + 16384, [128, 2048], BF16)
    at("stq1", STb + 20480, [128, 2048], BF16)
    at("p1s0", STb + 24576, [128, 512]); at("p1s1", STb + 26624, [128, 512])
    at("scW", STb + 16384, [128, 1024]); at("ivW", STb + 20480, [128, 1024])
    at("m2B", STb + 24576, [128, 1024])
    at("r2B", 19456 + 16384, [128, 1024])
    at("f10", STb, [128, 1024]); at("f11", STb + 4096, [128, 1024])
    at("f20", STb + 8192, [128, 1024]); at("f21", STb + 12288, [128, 1024])
    at("fa1", STb + 16384, [128, 1024], BF16); at("fb1", STb + 18432, [128, 1024], BF16)
    at("fq1", STb + 20480, [128, 2048], BF16)
    at("fa2", STb + 24576, [128, 1024], BF16); at("fb2", STb + 26624, [128, 1024], BF16)
    at("fq2", 74752 + 24576, [128, 2048], BF16)
    at("w30", STb, [128, 1024]); at("w31", STb + 4096, [128, 1024])
    at("os0", STb + 8192, [128, 1024]); at("os1", STb + 12288, [128, 1024])
    at("wa3", STb + 16384, [128, 1024], BF16); at("wb3", STb + 18432, [128, 1024], BF16)
    at("wq30", STb + 20480, [128, 1024], BF16); at("wq31", STb + 22528, [128, 1024], BF16)
    at("xm0", STb + 24576, [128, 1024])
    BG = 74752
    at("xfm", BG, [128, KC * TH])
    at("qbf", BG, [128, NH * T], BF16)
    at("kbf", BG + 32768, [128, NH * TH], BF16)
    at("hbf", BG + 73728, [128, KC * TH], BF16)
    at("vbuf0", BG + 110592, [128, TH], BF16)
    at("vbuf1", BG + 112896, [128, TH], BF16)
    at("vfm0", BG + 110592, [128, TH], BF16)
    at("vfm1", BG + 112896, [128, TH], BF16)
    at("vtm0", BG + 115200, [128, TC * 128], BF16)
    at("vtm1", BG + 117504, [128, TC * 128], BF16)
    at("scf", BG + 119808, [128, 256])
    at("ppf", BG + 120832, [128, 256])
    at("pnf", BG + 121856, [128, 256])
    at("ptb", BG + 122880, [128, 256], BF16)
    at("scfB", BG + 123392, [128, 256])
    at("ppfB", BG + 124416, [128, 256])
    at("pnfB", BG + 125440, [128, 256])
    at("ptbB", BG + 126464, [128, 256], BF16)
    at("lnt0", BG + 110592, [128, 1152])
    at("lnt1", BG + 115200, [128, 1152])
    at("xbp", BG + 119808, [128, 2 * TH], BF16)
    at("mB", BG + 119808, [128, TH])
    at("rB", BG + 124416, [128, TH])
    at("xres0", BG + 32768, [128, 1024])
    at("xres1", BG + 36864, [128, 1024])
    at("xmid", BG + 65536, [128, KC * T])
    at("xmbf", BG + 32768, [128, KC * T], BF16)
    at("sq2", BG, [128, KC * T], BF16)
    at("h2", BG, [128, KC * T])
    at("xq1", BG + 65536, [128, KC * T], BF16)
    at("xq2", BG + 98304, [128, KC * T], BF16)
    at("sil0", BG, [128, 1024]); at("sil1", BG + 4096, [128, 1024])
    at("hm0", BG + 8192, [128, 1024], F16); at("hm1", BG + 10240, [128, 1024], F16)
    at("amx3", BG + 12288, [128, 1024])
    at("i1B", BG + 16384, [128, 1024]); at("i2B", BG + 20480, [128, 1024])
    at("xq3", BG, [128, FC * T], BF16)
    at("sqp", BG + 124416, [128, 2 * TH], BF16)

    PSl = [nc.alloc_psum_tensor("ps%d" % i, [128, 512], F32) for i in range(6)]
    t["SB"] = SB
    t["PS"] = PSl
    t["PT"] = [nc.alloc_psum_tensor("pt%d" % i, [128, 1024], BF16) for i in range(2)]
    return t


def _build(nc, k, t):
    SB, PS = t["SB"], t["PS"]

    def s(name):
        return SB[name][:]

    def dma_s(dst, src, st="sc"):
        return st, k.op(E, lambda e, d=dst, sr=src: e.dma_start(out=d, in_=sr),
                        inc=st, incn=16)

    def dma_g(dst, src, st="gp"):
        return st, k.op(G, lambda e, d=dst, sr=src: e.dma_start(out=d, in_=sr),
                        inc=st, incn=16)

    def mm(out, lhsT, rhs, start, stop, inc=None):
        return k.op(P, (lambda e, o=out, l=lhsT, r=rhs, st=start, sp=stop:
                        e.matmul(o, l, r, start=st, stop=sp)), inc=inc)

    class TP:

        def __init__(self, banks):
            self.banks = banks
            self.pend_ve = [0, 0]
            self.i = 0

        def do(self, src_ap, dst_ap, eng=V):
            b = self.i % 2
            bank = self.banks[b]
            k.wait(P, "ve", self.pend_ve[b])
            tp_pe = k.op(P, (lambda e, o=bank[:, 0:128], i_=src_ap:
                             e.transpose(o, i_, s("identf"))), inc="pe")
            k.wait(eng, "pe", tp_pe)
            self.pend_ve[b] = k.op(
                eng, (lambda e, d=dst_ap, bb=bank:
                      e.tensor_copy(out=d, in_=bb[:, 0:128])), inc="ve")
            self.i += 1

    def bcast(rows_ap, dst_ap, n, bank, name="bc"):
        done = []
        for off in range(0, n, 512):
            ln = min(512, n - off)
            pe_v = mm(bank[:, 0:ln], s("onesr"), rows_ap[:, off:off + ln],
                      start=True, stop=True, inc="pe")
            k.wait(V, "pe", pe_v)
            ve_v = k.op(V, (lambda e, d=dst_ap[:, off:off + ln], b=bank, l=ln:
                            e.tensor_copy(out=d, in_=b[:, 0:l])),
                        writes=(name,), inc="ve")
            k.wait(P, "ve", ve_v)
            done.append(ve_v)
        return done[-1]

    ldc = {}
    ldc["identf"] = dma_s(s("identf"), t["ident_d"])
    ldc["Rb"] = dma_s(s("Rb"), t["R_d"])
    ldc["identb"] = dma_s(s("identb"), t["identb_d"])
    ldc["onesr"] = dma_s(s("onesr"), t["ones_d"])
    ldc["sel8"] = dma_s(s("sel8"), t["sel8_d"])
    ldc["c2t"] = dma_s(s("c2t"), t["c2"])
    for c in range(KC):
        dma_s(SB["bout"][:, c:c + 1], t["b_out2"][c].unsqueeze(1))
        dma_s(SB["g1t"][:, c:c + 1], t["g1_2"][c].unsqueeze(1))
        ldc["g2_%d" % c] = dma_s(SB["g2t"][:, c:c + 1],
                                 t["g2_2"][c].unsqueeze(1))
    for c in range(FC):
        ldc["g3_%d" % c] = dma_s(SB["g3t"][:, c:c + 1],
                                 t["g3_2"][c].unsqueeze(1))
    ldc["cosK"] = dma_s(s("cosKt"), t["cosK"])
    ldc["sinK"] = dma_s(s("sinKt"), t["sinK"])
    ldc["cosQ"] = dma_s(s("cosQt"), t["cosQ"])
    ldc["sinQ"] = dma_s(s("sinQt"), t["sinQ"])
    for qb in range(QB):
        ldc["mask"] = dma_s(SB["maskt"][:, qb * 256:(qb + 1) * 256], t["masks"][qb])
    sc_tok = ("sc", k.val("sc"))

    sv = SB["svec"]
    k.op(V, lambda e: e.memset(sv[:, 0:1], 1.0), writes=("svec",))
    k.op(V, lambda e: e.memset(sv[:, 1:2], -0.5), writes=("svec",))
    k.op(V, lambda e: e.memset(sv[:, 4:5], 0.0), writes=("svec",))
    k.op(V, lambda e: e.memset(sv[:, 13:14], -63.5), writes=("svec",))
    nc.const_aps.register(mybir.dt.float32, -63.5, sv[:, 13:14])
    nc.const_aps.register(mybir.dt.float32, 0.0, sv[:, 4:5])
    nc.const_aps.register(mybir.dt.float32, 1.0, sv[:, 0:1])
    for col, val in ((2, 1e-5), (3, 1e-8), (11, -ATANH_HALF), (12, ATANH_HALF)):
        k.op(V, (lambda e, c_=col, v_=val: e.tensor_scalar(
            out=sv[:, c_:c_ + 1], in0=sv[:, 0:1], scalar1=v_, scalar2=None,
            op0=ALU.mult)), reads=("svec",), writes=("svec",))
    k.op(V, lambda e: e.memset(SB["onesf"][:], 1.0), writes=("onesf",))
    k.op(V, lambda e: e.memset(SB["cpad"][:], 0.0), writes=("cpad",))
    k.op(V, lambda e: e.memset(s("onesb"), 1.0), writes=("onesb",), inc="ve")
    v_consts = k.val("ve")

    xst = [SB["st0"], SB["st1"]]
    xtp = TP([PS[0], PS[1]])
    xld = []
    xtrans_done = []
    for tt in range(TC):
        if tt >= 2:
            k.wait(E, "pe", xtrans_done[tt - 2])
        xld.append(dma_s(xst[tt % 2][:], t["x_sh"][tt], "sx%d" % (tt % 2)))
        k.waitt(P, xld[tt])
        if tt == 0:
            k.waitt(P, sc_tok)
        for c in range(KC):
            xtp.do(xst[tt % 2][:, c * 128:(c + 1) * 128],
                   SB["xfm"][:, c * TH + tt * 128: c * TH + tt * 128 + 128])
        xtrans_done.append(k.val("pe"))
    x_in_v = k.val("ve")

    k.waitt(A, sc_tok)
    k.wait(A, "ve", v_consts)
    k.op(A, lambda e: e.activation(out=SB["cpad"][0:16, :], in_=s("c2t"),
                                   func=AF.Silu), writes=("cpad",), inc="ae")
    k.wait(P, "ae", k.val("ae"))
    pe_v = k.op(P, lambda e: e.transpose(PS[5][:, 0:128], s("cpad"),
                                         s("identf")), inc="pe")
    k.wait(V, "pe", pe_v)
    k.op(V, lambda e: e.tensor_copy(out=s("scTb"), in_=PS[5][:, 0:16]),
         writes=("scTb",))
    k.op(V, lambda e: e.tensor_copy(out=SB["cpad"][:, 64:80],
                                    in_=PS[5][:, 0:16]),
         writes=("cpad",), inc="ve")
    v_sct = k.val("ve")

    wmst = [SB["st0"], SB["st1"]]
    scTf = SB["cpad"][:, 64:80]
    k.wait(E, "pe", xtrans_done[-1])
    k.wait(P, "ve", v_sct)
    mod_pe0 = {}
    for ms in range(MS):
        if ms >= 2:
            k.wait(E, "pe", mod_pe0[ms - 2])
        w_ld = dma_s(wmst[ms % 2][:], t["wmod"][ms], "sx%d" % (ms % 2))
        k.waitt(P, w_ld)
        for kc in range(KC):
            mm(PS[4][:, 0:1], wmst[ms % 2][:, kc * 128:(kc + 1) * 128],
               scTf[:, kc:kc + 1], start=(kc == 0), stop=(kc == KC - 1),
               inc="pe" if kc == KC - 1 else None)
        mod_pe0[ms] = k.val("pe")
        pe_v = k.val("pe")
        k.wait(V, "pe", pe_v)
        vi, ch = ms // KC, ms % KC
        ve_v = k.op(V, (lambda e, v_=vi, c_=ch:
                        e.tensor_copy(out=SB["modv"][:, v_ * KC + c_:
                                                     v_ * KC + c_ + 1],
                                      in_=PS[4][:, 0:1])),
                    writes=("modv",), inc="ve")
        k.wait(P, "ve", ve_v)
    mo = SB["modv"]

    def slot(i):
        return mo[:, i * KC:(i + 1) * KC]
    k.op(V, lambda e: e.tensor_scalar(out=slot(1), in0=slot(1), scalar1=1.0,
                                      scalar2=None, op0=ALU.add),
         reads=("modv",), writes=("modv",))
    k.op(V, lambda e: e.tensor_scalar(out=slot(4), in0=slot(4), scalar1=1.0,
                                      scalar2=None, op0=ALU.add),
         reads=("modv",), writes=("modv",))
    for gi in (2, 5):
        k.wait(A, "ve", k.val("ve"))
        ae_v = k.op(A, (lambda e, i=gi: e.activation(out=slot(6), in_=slot(i),
                                                     func=AF.Sign)),
                    inc="ae")
        k.wait(V, "ae", ae_v)
        k.op(V, lambda e: e.scalar_tensor_tensor(out=slot(7), in0=slot(6),
                                                 scalar=-1.0, in1=slot(6),
                                                 op0=ALU.mult, op1=ALU.mult),
             writes=("modv7",))
        k.op(V, lambda e: e.tensor_scalar(out=slot(7), in0=slot(7), scalar1=1.0,
                                          scalar2=None, op0=ALU.add),
             reads=("modv7",), writes=("modv7",))
        k.op(V, lambda e: e.tensor_tensor(out=slot(7), in0=slot(7), in1=slot(6),
                                          op=ALU.add),
             reads=("modv7",), writes=("modv7",))
        k.op(V, lambda e, i=gi: e.scalar_tensor_tensor(out=slot(6), in0=slot(i),
                                                       scalar=-1.0, in1=slot(i),
                                                       op0=ALU.mult, op1=ALU.max),
             writes=("modv6",))
        k.op(V, lambda e: e.tensor_scalar(out=slot(6), in0=slot(6), scalar1=0.25,
                                          scalar2=None, op0=ALU.max),
             reads=("modv6",), writes=("modv6",))
        k.op(V, lambda e, i=gi: e.tensor_tensor(out=slot(i), in0=slot(6),
                                                in1=slot(7), op=ALU.mult),
             reads=("modv6", "modv7"), writes=("modv",), inc="ve")
    mod_v = k.val("ve")

    npieces = [(0, 512), (512, 512), (1024, 128)]
    sum_banks = [PS[0], PS[1], PS[2]]
    sq_banks = [PS[3], PS[4], PS[5]]
    k.wait(V, "ve", x_in_v)
    grp_pe = 0
    for g in range(8):
        k.wait(V, "pe", grp_pe)
        cast_v = k.op(V, (lambda e, gg=g: e.tensor_copy(
            out=s("xbp"), in_=SB["xfm"][:, gg * 2 * TH:(gg + 1) * 2 * TH])),
            writes=("xbp",), inc="ve")
        k.wait(A, "ve", cast_v)
        k.wait(A, "pe", grp_pe)
        sq_v = k.op(A, lambda e: e.activation(out=s("sqp"), in_=s("xbp"),
                                              func=AF.Square),
                    writes=("sqp",), inc="ae")
        k.wait(P, "ve", cast_v)
        k.wait(P, "ae", sq_v)
        for ci in range(2):
            for bi, (off, ln) in enumerate(npieces):
                st = (g == 0 and ci == 0)
                sp = (g == 7 and ci == 1)
                mm(sum_banks[bi][0:1, 0:ln], s("onesb"),
                   SB["xbp"][:, ci * TH + off:ci * TH + off + ln], st, sp)
                mm(sq_banks[bi][0:1, 0:ln], s("onesb"),
                   SB["sqp"][:, ci * TH + off:ci * TH + off + ln], st, sp,
                   inc="pe" if (ci == 1 and bi == 2) else None)
        grp_pe = k.val("pe")
    ln1_pe = k.val("pe")
    rows = SB["rows"]
    rowC = SB["lnt0"][0:1, :]
    k.wait(V, "pe", ln1_pe)
    for bi, (off, ln) in enumerate(npieces):
        k.op(V, (lambda e, b=sum_banks[bi], o=off, l=ln:
                 e.tensor_scalar(out=rows[:, o:o + l], in0=b[0:1, 0:l],
                                 scalar1=1.0 / D, scalar2=None, op0=ALU.mult)),
             writes=("rows",))
        k.op(V, (lambda e, b=sq_banks[bi], o=off, l=ln:
                 e.tensor_scalar(out=rows[:, TH + o:TH + o + l], in0=b[0:1, 0:l],
                                 scalar1=1.0 / D, scalar2=None, op0=ALU.mult)),
             writes=("rows",))
    k.op(V, lambda e: e.tensor_tensor(out=rowC, in0=rows[:, 0:TH],
                                      in1=rows[:, 0:TH], op=ALU.mult),
         reads=("rows",), writes=("rowC",))
    k.op(V, lambda e: e.tensor_tensor(out=rows[:, TH:2 * TH],
                                      in0=rows[:, TH:2 * TH], in1=rowC,
                                      op=ALU.subtract),
         reads=("rows", "rowC"), writes=("rows",), inc="ve")
    lnv_v = k.val("ve")
    k.wait(A, "ve", lnv_v)
    k.op(A, lambda e: e.activation(out=rowC, in_=rows[:, TH:2 * TH],
                                   func=AF.Ln, bias=sv[0:1, 2:3],
                                   scale=sv[0:1, 0:1]),
         writes=("rowC",))
    k.op(A, lambda e: e.activation(out=rows[:, TH:2 * TH], in_=rowC,
                                   func=AF.Exp, bias=sv[0:1, 4:5],
                                   scale=sv[0:1, 1:2]),
         reads=("rowC",), writes=("rows",), inc="ae")
    rstd_a = k.val("ae")
    k.wait(V, "ae", rstd_a)
    k.wait(P, "ae", rstd_a)
    bcast(rows[:, 0:TH], s("mB"), TH, PS[0], "mB")
    bcast(rows[:, TH:2 * TH], s("rB"), TH, PS[1], "rB")
    bc_v = k.val("ve")
    k.wait(G, "ve", max(mod_v, bc_v))
    g_done = {}
    for c in range(KC):
        scr = s("lnt1") if c % 2 == 0 else SB["lnt0"][:]
        tg = "lnt1" if c % 2 == 0 else "lnt0"
        k.op(V, (lambda e, cc=c, sc_=scr: e.tensor_tensor(
            out=sc_, in0=SB["xfm"][:, cc * TH:(cc + 1) * TH], in1=s("mB"),
            op=ALU.subtract)), reads=("mB", "rowC", tg), writes=(tg,))
        k.op(V, (lambda e, sc_=scr: e.tensor_tensor(out=sc_, in0=sc_,
                                                    in1=s("rB"), op=ALU.mult)),
             reads=(tg, "rB"), writes=(tg,), inc="ve")
        g_done[c] = k.op(V, (lambda e, cc=c, sc_=scr: e.tensor_scalar(
            out=SB["hbf"][:, cc * TH:(cc + 1) * TH], in0=sc_,
            scalar1=SB["modv"][:, KC + cc:KC + cc + 1],
            scalar2=SB["modv"][:, cc:cc + 1],
            op0=ALU.mult, op1=ALU.add)),
            reads=("modv", tg), writes=("hbf",), inc="po")
    h_v, h_po = k.val("ve"), k.val("po")
    return dict(ldc=ldc, h_v=h_v, h_po=h_po, mod_v=mod_v, x_in_v=x_in_v,
                TPcls=TP, bcast=bcast, mm=mm, dma_s=dma_s, dma_g=dma_g, s=s)


def _build_p23(nc, k, t, pp):
    SB, PS, PT = t["SB"], t["PS"], t["PT"]
    s, mm, dma_s, dma_g = pp["s"], pp["mm"], pp["dma_s"], pp["dma_g"]
    h_v, h_po = pp["h_v"], pp["h_po"]

    npieces = [(0, 512), (512, 512), (1024, 128)]
    wst = [SB["st0"], SB["st1"]]
    wsq = [SB["stq0"], SB["stq1"]]
    vbuf = [SB["vbuf0"], SB["vbuf1"]]

    p1_pieces = []
    for wname, base in (("w1", 0), ("w2", 256), ("w3", 512)):
        wd = t[wname]
        nslab = wd.shape[0]
        per = wd.shape[2] // 512
        for sl in range(nslab):
            for pc in range(per):
                i = base + sl * per + pc
                p1_pieces.append((wd[sl][:, pc * 512:(pc + 1) * 512], i))
    assert len(p1_pieces) == 768
    p1s = [SB["p1s0"], SB["p1s1"]]
    p1pair = SB["p1s0"][:].rearrange("p (a b) -> p a b", a=1) if False else None
    p1_iter = [0]
    p1_cnt = {V: 0, A: 0}
    p1_fold = {}

    def pump_pass1(n):
        for _ in range(n):
            j = p1_iter[0]
            if j >= 384:
                return
            p1_iter[0] += 1
            if j >= 1:
                peng = "p1v" if (j - 1) % 2 == 0 else "p1a"
                k.wait(E, peng, k.val(peng))
            dma_s(p1s[0][:], p1_pieces[2 * j][0], "sp")
            ld2 = dma_s(p1s[1][:], p1_pieces[2 * j + 1][0], "sp")
            eng = V if j % 2 == 0 else A
            k.waitt(eng, ld2)
            acc = SB["acc1"] if eng == V else SB["acc2"]
            pi = p1_cnt[eng]
            p1_cnt[eng] += 1
            col = pi % 16
            if pi >= 16:
                k.wait(eng, "p1f", p1_fold[(eng, pi // 16 - 1)])
            if eng == V:
                k.op(V, (lambda e, a=acc, cl=col:
                         e.reduce_sum(out=a[:, cl:cl + 1],
                                      in_=SB["p1s0"][:].rearrange(
                                          "p (a b) -> p a b", a=2,
                                          b=512).flatten_outer_dims()
                                      if False else p1s[0][:, 0:512],
                                      axis=AX.X, apply_absolute_value=True)),
                     reads=("acc",), writes=("acc", "p1b"))
                k.op(V, (lambda e, a=acc, cl=col:
                         e.reduce_sum(out=a[:, 16 + cl:16 + cl + 1],
                                      in_=p1s[1][:], axis=AX.X,
                                      apply_absolute_value=True)),
                     reads=("acc",), writes=("acc", "p1b"), inc="p1v")
            else:
                k.op(A, (lambda e, a=acc, cl=col: e.activation(
                    out=p1s[0][:], in_=p1s[0][:], func=AF.Abs,
                    accum_out=a[:, cl:cl + 1])),
                    reads=("acc",), writes=("acc", "p1b"))
                k.op(A, (lambda e, a=acc, cl=col: e.activation(
                    out=p1s[1][:], in_=p1s[1][:], func=AF.Abs,
                    accum_out=a[:, 16 + cl:16 + cl + 1])),
                    reads=("acc",), writes=("acc", "p1b"), inc="p1a")
            if pi % 16 == 15:
                fcol = 32 + pi // 16
                if eng == A:
                    k.wait(V, "p1a", k.val("p1a"))
                k.op(V, (lambda e, a=acc, fc=fcol:
                         e.reduce_sum(out=a[:, fc:fc + 1], in_=a[:, 0:32],
                                      axis=AX.X)),
                     reads=("acc",), writes=("acc",), inc="p1f")
                p1_fold[(eng, pi // 16)] = k.val("p1f")

    k.wait(G, "ve", h_v)
    k.wait(G, "po", h_po)
    for kc in range(KC):
        park_qg = dma_g(t["x_park"][kc],
                        SB["xfm"][:, kc * TH + 128:(kc + 1) * TH], "gp")
    k.wait(P, "ve", h_v)
    k.wait(P, "po", h_po)
    k.waitt(V, park_qg)
    pp["park_qg"] = park_qg
    ogld = [None] * 48
    ogcast = [None] * 48
    ev_after = {}
    po_after = {}
    qg_after = {}
    for og in range(48):
        if og >= 2:
            k.wait(E, "po", ogcast[og - 2])
        if og < 2:
            k.wait(E, "po", k.val("po"))
        ogld[og] = dma_s(wst[og % 2][:], t["wq"][og], "sx%d" % (og % 2))
        pump_pass1(8)
        k.waitt(G, ogld[og])
        if og >= 2:
            k.wait(G, "pe", pe_after_og)
        else:
            k.wait(G, "pe", k.val("pe"))
        ogcast[og] = k.op(G, (lambda e, o=og: e.tensor_copy(
            out=wsq[o % 2][:], in_=wst[o % 2][:])), inc="po")
        k.wait(P, "po", ogcast[og])
        banks = [PS[0], PS[1], PS[2]] if og % 2 == 0 else [PS[3], PS[4], PS[5]]
        if og >= 2:
            k.wait(P, "ve", ev_after[og - 2])
        for kc in range(KC):
            for bi, (off, ln) in enumerate(npieces):
                mm(banks[bi][:, 0:ln], wsq[og % 2][:, kc * 128:(kc + 1) * 128],
                   SB["hbf"][:, kc * TH + off:kc * TH + off + ln],
                   start=(kc == 0), stop=(kc == KC - 1),
                   inc="pe" if (kc == KC - 1 and bi == 2) else None)
        pe_after_og = k.val("pe")
        k.wait(V, "pe", pe_after_og)
        if og < 16:
            hidx = og
            dsts = [(SB["qbf"][:, hidx * T:hidx * T + 384], banks[0][:, 128:512]),
                    (SB["qbf"][:, hidx * T + 384:hidx * T + 896], banks[1][:, 0:512]),
                    (SB["qbf"][:, hidx * T + 896:hidx * T + 1024], banks[2][:, 0:128])]
            for di, (d, sr) in enumerate(dsts):
                k.op(V, (lambda e, dd=d, ss=sr: e.tensor_copy(out=dd, in_=ss)),
                     writes=("qbf",), inc="ve" if di == 2 else None)
        elif og < 32:
            hidx = og - 16
            for bi, (off, ln) in enumerate(npieces):
                k.op(V, (lambda e, h_=hidx, o=off, l=ln, b=banks[bi]:
                         e.tensor_copy(out=SB["kbf"][:, h_ * TH + o:h_ * TH + o + l],
                                       in_=b[:, 0:l])), writes=("kbf",),
                     inc="ve" if bi == 2 else None)
        else:
            hidx = og - 32
            vb = vbuf[hidx % 2]
            if hidx >= 2:
                k.waitt(V, qg_after[hidx - 2])
            for bi, (off, ln) in enumerate(npieces):
                k.op(V, (lambda e, o=off, l=ln, b=banks[bi], v_=vb:
                         e.tensor_copy(out=v_[:, o:o + l], in_=b[:, 0:l])),
                     writes=("vbuf",), inc="ve" if bi == 2 else None)
            k.wait(G, "ve", k.val("ve"))
            qg_after[hidx] = dma_g(t["v_dram"][hidx], vb[:],
                                   "ga%d" % (hidx % 2))
        ev_after[og] = k.val("ve")
    qkv_done_pe = k.val("pe")
    v_all_stored = qg_after[15]

    k.wait(P, "ve", ev_after[47])
    k.wait(V, "ga0", k.val("ga0"))
    k.wait(V, "ga1", k.val("ga1"))
    for h in range(NH):
        banks = [PS[0], PS[1], PS[2]] if h % 2 == 0 else [PS[3], PS[4], PS[5]]
        for (x_sl, cosT, sinT, n_tok) in (
                (SB["qbf"][:, h * T:(h + 1) * T], s("cosQt"), s("sinQt"), T),
                (SB["kbf"][:, h * TH:(h + 1) * TH], s("cosKt"), s("sinKt"), TH)):
            pieces = [(0, 512), (512, 512)] if n_tok == T else npieces
            k.wait(P, "ve", k.val("ve"))
            pe_rot = None
            for bi, (off, ln) in enumerate(pieces):
                pe_rot = mm(banks[bi][:, 0:ln], s("Rb"), x_sl[:, off:off + ln],
                            start=True, stop=True,
                            inc="pe" if bi == len(pieces) - 1 else None)
            k.wait(G, "pe", pe_rot)
            k.wait(V, "pe", pe_rot)
            for bi, (off, ln) in enumerate(pieces):
                k.op(V, (lambda e, xs=x_sl, ct=cosT, o=off, l=ln:
                         e.tensor_tensor(out=SB["lnt0"][:, o:o + l],
                                         in0=xs[:, o:o + l], in1=ct[:, o:o + l],
                                         op=ALU.mult)),
                     reads=("qk",), writes=("lnt0",))
                k.op(V, (lambda e, b=banks[bi], st_=sinT, o=off, l=ln:
                         e.tensor_tensor(out=SB["lnt1"][:, o:o + l],
                                         in0=b[:, 0:l], in1=st_[:, o:o + l],
                                         op=ALU.mult)),
                     writes=("lnt1",))
                k.op(V, (lambda e, xs=x_sl, o=off, l=ln:
                         e.tensor_tensor(out=xs[:, o:o + l],
                                         in0=SB["lnt0"][:, o:o + l],
                                         in1=SB["lnt1"][:, o:o + l],
                                         op=ALU.add)),
                     reads=("lnt0", "lnt1"), writes=("qk",), inc="ve")
    rope_v = k.val("ve")
    pump_pass1(384)

    vfm = [SB["vfm0"], SB["vfm1"]]
    vtm = [SB["vtm0"], SB["vtm1"]]
    vin_ld = [None] * NH
    vtm_ready = [None] * NH
    k.wait(G, "ga0", k.val("ga0"))
    k.wait(G, "ga1", k.val("ga1"))
    k.wait(P, "ve", rope_v)

    vtp_last = [0, 0]

    def v_prep(h):
        if h >= 2:
            k.wait(G, "pe", unit_pv_pe[h - 2])
        else:
            k.wait(G, "ve", rope_v)
        vin_ld[h] = dma_g(vfm[h % 2][:], t["v_dram"][h], "gb%d" % (h % 2))
        k.waitt(P, vin_ld[h])
        for tc in range(TC):
            b = PT[tc % 2]
            k.wait(P, "ve", vtp_last[tc % 2])
            pe_v = k.op(P, (lambda e, h_=h, tt=tc, bb=b: e.transpose(
                bb[:, 0:128], vfm[h_ % 2][:, tt * 128:(tt + 1) * 128],
                s("identb"))), inc="pe")
            k.wait(V, "pe", pe_v)
            vtp_last[tc % 2] = k.op(V, (lambda e, h_=h, tt=tc, bb=b: e.tensor_copy(
                out=vtm[h_ % 2][:, tt * 128:(tt + 1) * 128], in_=bb[:, 0:128])),
                writes=("vtm",), inc="ve")
        vtm_ready[h] = k.val("ve")

    unit_pv_pe = {}
    v_prep(0)
    v_prep(1)
    scf_t = [s("scf"), s("scfB")]
    ppf_t = [s("ppf"), s("ppfB")]
    pnf_t = [s("pnf"), s("pnfB")]
    ptb_t = [SB["ptb"], SB["ptbB"]]
    scf_ve = [0, 0]
    exp_ae = [0, 0]
    pnf_ve = [0, 0]
    ptT_pe = [0, 0]
    ptb_ve = [0, 0]
    pv_pe = [0, 0]
    ao_ve = [0, 0]
    for h in range(NH):
        for qb in range(QB):
            p_ = (h * QB + qb) % 2
            psA = PS[0] if p_ == 0 else PS[1]
            psO = PS[4] if p_ == 0 else PS[5]
            mxc = 32 + 3 * p_
            k.wait(P, "ve", scf_ve[p_])
            pe_s = mm(psA[:, 0:256], SB["qbf"][:, h * T + qb * 128:
                                              h * T + (qb + 1) * 128],
                      SB["kbf"][:, h * TH + qb * 128:h * TH + qb * 128 + 256],
                      start=True, stop=True, inc="pe")
            k.wait(V, "pe", pe_s)
            k.wait(V, "ae", exp_ae[p_])
            scf_ve[p_] = k.op(V, (lambda e, b=psA, q_=qb, sc_=scf_t[p_]:
                              e.scalar_tensor_tensor(
                out=sc_, in0=b[:, 0:256], scalar=1.0 / math.sqrt(HD),
                in1=SB["maskt"][:, q_ * 256:(q_ + 1) * 256],
                op0=ALU.mult, op1=ALU.add)),
                writes=("scf",), inc="ve")
            k.op(V, (lambda e, sc_=scf_t[p_], c_=mxc: e.reduce_max(
                out=SB["tok8"][:, c_:c_ + 1], in_=sc_,
                axis=AX.X, negate=True)),
                reads=("scf",), writes=("mx",), inc="ve")
            ve_mx = k.val("ve")
            k.wait(A, "ve", ve_mx)
            k.wait(A, "ve", pnf_ve[p_])
            exp_ae[p_] = k.op(A, (lambda e, sc_=scf_t[p_], pp_=ppf_t[p_],
                              c_=mxc: e.activation(
                out=pp_, in_=sc_, func=AF.Exp,
                bias=SB["tok8"][:, c_:c_ + 1], scale=SB["svec"][:, 0:1])),
                inc="ae")
            k.wait(V, "ae", exp_ae[p_])
            k.op(V, (lambda e, pp_=ppf_t[p_], c_=mxc: e.reduce_sum(
                out=SB["tok8"][:, c_ + 1:c_ + 2], in_=pp_, axis=AX.X)),
                writes=("rs",))
            k.op(V, (lambda e, c_=mxc: e.reciprocal(
                out=SB["tok8"][:, c_ + 2:c_ + 3],
                in_=SB["tok8"][:, c_ + 1:c_ + 2])),
                reads=("rs",), writes=("rs",))
            k.wait(V, "pe", ptT_pe[p_])
            pnf_ve[p_] = k.op(V, (lambda e, pp_=ppf_t[p_], pn_=pnf_t[p_],
                              c_=mxc: e.tensor_scalar(
                out=pn_, in0=pp_, scalar1=SB["tok8"][:, c_ + 2:c_ + 3],
                scalar2=None, op0=ALU.mult)),
                reads=("rs",), writes=("pnf",), inc="ve")
            ve_pn = k.val("ve")
            k.wait(P, "ve", ve_pn)
            for j in range(2):
                k.wait(P, "ve", ptb_ve[j])
                pe_t = k.op(P, (lambda e, jj=j, b=PS[2 + j], pn_=pnf_t[p_]:
                            e.transpose(b[:, 0:128],
                                        pn_[:, jj * 128:(jj + 1) * 128],
                                        s("identf"))), inc="pe")
                ptT_pe[p_] = pe_t
                k.wait(V, "pe", pe_t)
                k.wait(V, "pe", pv_pe[p_])
                ptb_ve[j] = k.op(V, (lambda e, jj=j, b=PS[2 + j],
                                 pt_=ptb_t[p_]: e.tensor_copy(
                    out=pt_[:, jj * 128:(jj + 1) * 128], in_=b[:, 0:128])),
                    writes=("ptb",), inc="ve")
            ve_pt = k.val("ve")
            k.wait(P, "ve", ve_pt)
            k.wait(P, "ve", vtm_ready[h])
            k.wait(P, "ve", ao_ve[p_])
            for j in range(2):
                mm(psO[:, 0:128],
                   vtm[h % 2][:, (qb + j) * 128:(qb + j + 1) * 128],
                   ptb_t[p_][:, j * 128:(j + 1) * 128],
                   start=(j == 0), stop=(j == 1), inc="pe" if j == 1 else None)
            pv_pe[p_] = pe_pv = k.val("pe")
            k.wait(V, "pe", pe_pv)
            ao_ve[p_] = k.op(V, (lambda e, h_=h, q_=qb, b=psO: e.tensor_copy(
                out=SB["qbf"][:, h_ * T + q_ * 128:h_ * T + (q_ + 1) * 128],
                in_=b[:, 0:128])), writes=("ao",), inc="ve")
        unit_pv_pe[h] = k.val("pe")
        if h + 2 < NH:
            v_prep(h + 2)
    attn_v = k.val("ve")
    pp["attn_v"] = pp_attn = attn_v
    pp["qkv_done_pe"] = qkv_done_pe
    return pp


def nc_sq1(SB):
    return SB["xq1"][:]


def _build_p45(nc, k, t, pp):
    SB, PS, PT = t["SB"], t["PS"], t["PT"]
    s, mm, dma_s, dma_g = pp["s"], pp["mm"], pp["dma_s"], pp["dma_g"]
    sv = SB["svec"]
    rows = SB["rows"]
    attn_v = pp["attn_v"]

    k.wait(V, "p1v", 192)
    k.wait(V, "p1a", 192)
    k.wait(V, "p1f", k.val("p1f"))
    k.wait(V, "ae", k.val("ae"))
    scal = {}
    CELL0 = 2176

    def cell(name):
        if name not in scal:
            scal[name] = CELL0 + len(scal)
        return rows[:, scal[name]:scal[name] + 1]

    for wi, nm in enumerate(("s1", "s2", "s3")):
        f0 = 32 + wi * 4
        k.op(V, (lambda e, f=f0: e.reduce_sum(out=SB["acc1"][:, 44:45],
                                              in_=SB["acc1"][:, f:f + 4],
                                              axis=AX.X)),
             reads=("acc",), writes=("accV",))
        k.op(V, (lambda e, f=f0: e.reduce_sum(out=SB["acc2"][:, 44:45],
                                              in_=SB["acc2"][:, f:f + 4],
                                              axis=AX.X)),
             reads=("accG",), writes=("accG",))
        k.op(V, lambda e: e.tensor_tensor(out=SB["acc1"][:, 45:46],
                                          in0=SB["acc1"][:, 44:45],
                                          in1=SB["acc2"][:, 44:45], op=ALU.add),
             reads=("accV", "accG"), writes=("accS",), inc="ve")
        k.wait(P, "ve", k.val("ve"))
        pe_v = mm(PS[5][0:1, 0:1], s("onesf"), SB["acc1"][:, 45:46],
                  start=True, stop=True, inc="pe")
        k.wait(V, "pe", pe_v)
        k.op(V, (lambda e, c=cell(nm): e.tensor_scalar(
            out=c, in0=PS[5][0:1, 0:1], scalar1=1.0 / (FF * D), scalar2=None,
            op0=ALU.mult)), writes=("rows",), inc="ve")
        k.wait(P, "ve", k.val("ve"))
        k.op(V, (lambda e, c=cell(nm), o=cell(nm + "_pe"): e.tensor_scalar(
            out=o, in0=c, scalar1=1e-8, scalar2=None, op0=ALU.add)),
            reads=("rows",), writes=("rows",))
        k.op(V, (lambda e, i_=cell(nm + "_pe"), o=cell(nm + "_r"):
                 e.reciprocal(out=o, in_=i_)),
             reads=("rows",), writes=("rows",))
        k.op(V, (lambda e, c=cell(nm), o=cell(nm + "_1p"): e.tensor_scalar(
            out=o, in0=c, scalar1=1.0, scalar2=None, op0=ALU.add)),
            reads=("rows",), writes=("rows",))
        k.op(V, (lambda e, c=cell(nm), o=cell(nm + "_1m"): e.tensor_scalar(
            out=o, in0=c, scalar1=-1.0, scalar2=1.0, op0=ALU.mult, op1=ALU.add)),
            reads=("rows",), writes=("rows",))
        k.op(V, (lambda e, i_=cell(nm + "_1m"), o=cell(nm + "_1mr"):
                 e.reciprocal(out=o, in_=i_)),
             reads=("rows",), writes=("rows",))
        k.op(V, (lambda e, a=cell(nm + "_1p"), b=cell(nm + "_1mr"),
                 o=cell(nm + "_q"): e.tensor_tensor(out=o, in0=a, in1=b,
                                                    op=ALU.mult)),
             reads=("rows",), writes=("rows",), inc="ve")
        k.wait(A, "ve", k.val("ve"))
        k.op(A, (lambda e, i_=cell(nm + "_q"), o=cell(nm + "_ln"):
                 e.activation(out=o, in_=i_, func=AF.Ln, bias=sv[0:1, 4:5],
                              scale=sv[0:1, 0:1])),
             writes=("rowsA",), inc="ae")
        k.wait(V, "ae", k.val("ae"))
        k.op(V, (lambda e, i_=cell(nm + "_ln"), o=cell(nm + "_lam"):
                 e.tensor_scalar(out=o, in0=i_, scalar1=0.25, scalar2=None,
                                 op0=ALU.mult)),
             reads=("rows", "rowsA"), writes=("rows",), inc="ve")
    k.wait(P, "ve", k.val("ve"))
    for wi, nm in enumerate(("s1", "s2", "s3")):
        pe_v = mm(PS[5][:, 0:1], s("onesr"), cell(nm + "_r"),
                  start=True, stop=True, inc="pe")
        k.wait(V, "pe", pe_v)
        ve_v = k.op(V, (lambda e, w=wi: e.tensor_copy(out=sv[:, 5 + w:6 + w],
                                                      in_=PS[5][:, 0:1])),
                    writes=("svec",), inc="ve")
        k.wait(P, "ve", ve_v)
    sgate_v = k.val("ve")

    wst = [SB["st0"], SB["st1"]]
    wsq = [SB["stq0"], SB["stq1"]]
    xres = [SB["xres0"], SB["xres1"]]
    npc2 = [(0, 512), (512, 512)]
    k.wait(P, "ve", attn_v)
    k.wait(E, "ve", attn_v)
    k.waitt(E, pp["park_qg"])
    ogcast = [None] * KC
    evac_after = {}
    xres_ld = [None] * KC
    pe_after = 0
    for og in range(KC):
        if og >= 2:
            k.wait(E, "po", ogcast[og - 2])
            k.wait(E, "ve", evac_after[og - 2])
        else:
            k.wait(E, "po", k.val("po"))
        ld = dma_s(wst[og % 2][:], t["wo"][og], "sx%d" % (og % 2))
        xres_ld[og] = dma_s(xres[og % 2][:], t["x_park"][og], "sr%d" % (og % 2))
        k.waitt(G, ld)
        if og >= 2:
            k.wait(G, "pe", pe_after)
        else:
            k.wait(G, "pe", k.val("pe"))
        ogcast[og] = k.op(G, (lambda e, o=og: e.tensor_copy(
            out=wsq[o % 2][:], in_=wst[o % 2][:])), inc="po")
        k.wait(P, "po", ogcast[og])
        banks = [PS[0], PS[1]] if og % 2 == 0 else [PS[2], PS[3]]
        if og >= 2:
            k.wait(P, "ve", evac_after[og - 2])
        for kc in range(KC):
            for bi, (off, ln) in enumerate(npc2):
                mm(banks[bi][:, 0:ln], wsq[og % 2][:, kc * 128:(kc + 1) * 128],
                   SB["qbf"][:, kc * T + off:kc * T + off + ln],
                   start=(kc == 0), stop=(kc == KC - 1),
                   inc="pe" if (kc == KC - 1 and bi == 1) else None)
        if og == KC - 1:
            mm(PS[4][0:1, 0:1], s("onesb"), SB["qbf"][:, 0:1],
               start=True, stop=True)
            mm(PS[4][0:1, 0:1], s("onesb"), SB["qbf"][:, 0:1],
               start=True, stop=True, inc="pe")
        pe_after = k.val("pe")
        k.wait(V, "pe", pe_after)
        k.waitt(V, xres_ld[og])
        for bi, (off, ln) in enumerate(npc2):
            k.op(V, (lambda e, b=banks[bi], o=off, l=ln, c=og: e.tensor_scalar(
                out=SB["wgrp"][:, o:o + l],
                in0=b[:, 0:l], scalar1=SB["bout"][:, c:c + 1], scalar2=None,
                op0=ALU.add)), writes=("wgrp",))
            k.op(V, (lambda e, o=off, l=ln, c=og: e.scalar_tensor_tensor(
                out=SB["xmid"][:, c * T + o:c * T + o + l],
                in0=SB["wgrp"][:, o:o + l],
                scalar=SB["modv"][:, 2 * KC + c:2 * KC + c + 1],
                in1=xres[c % 2][:, o:o + l], op0=ALU.mult, op1=ALU.add)),
                reads=("wgrp", "modv"), writes=("xmid",),
                inc="ve" if bi == 1 else None)
        evac_after[og] = k.val("ve")
    xmid_v = k.val("ve")
    pp["wo_cast_po"] = ogcast[KC - 1]

    k.wait(V, "ve", xmid_v)
    cast_v = k.op(V, lambda e: e.tensor_copy(out=s("xmbf"), in_=s("xmid")),
                  writes=("xmbf",), inc="ve")
    k.wait(A, "ve", cast_v)
    sq_a = k.op(A, lambda e: e.activation(out=s("sq2"), in_=s("xmbf"),
                                          func=AF.Square),
                writes=("sq2",), inc="ae")
    k.wait(P, "ve", cast_v)
    k.wait(P, "ae", sq_a)
    for kc in range(KC):
        for bi, (off, ln) in enumerate(npc2):
            mm(PS[bi][0:1, 0:ln], s("onesb"),
               SB["xmbf"][:, kc * T + off:kc * T + off + ln],
               start=(kc == 0), stop=(kc == KC - 1))
            mm(PS[2 + bi][0:1, 0:ln], s("onesb"),
               SB["sq2"][:, kc * T + off:kc * T + off + ln],
               start=(kc == 0), stop=(kc == KC - 1),
               inc="pe" if (kc == KC - 1 and bi == 1) else None)
    ln2_pe = k.val("pe")
    k.wait(V, "pe", ln2_pe)
    for bi, (off, ln) in enumerate(npc2):
        k.op(V, (lambda e, b=PS[bi], o=off, l=ln: e.tensor_scalar(
            out=rows[:, o:o + l], in0=b[0:1, 0:l], scalar1=1.0 / D,
            scalar2=None, op0=ALU.mult)), writes=("rows",))
        k.op(V, (lambda e, b=PS[2 + bi], o=off, l=ln: e.tensor_scalar(
            out=rows[:, TH + o:TH + o + l], in0=b[0:1, 0:l], scalar1=1.0 / D,
            scalar2=None, op0=ALU.mult)), writes=("rows",))
    rowC = rows[:, 1152 + 1024:1152 + 1024 + 1]
    rowM2 = SB["acc1"][0:1, 64:64 + 64]
    k.op(V, lambda e: e.tensor_tensor(out=rows[:, 2 * TH:2 * TH + T] if False
                                      else SB["m2B"][0:1, 0:T],
                                      in0=rows[:, 0:T], in1=rows[:, 0:T],
                                      op=ALU.mult),
         reads=("rows",), writes=("m2Brow",))
    k.op(V, lambda e: e.tensor_tensor(out=rows[:, TH:TH + T],
                                      in0=rows[:, TH:TH + T],
                                      in1=SB["m2B"][0:1, 0:T], op=ALU.subtract),
         reads=("rows", "m2Brow"), writes=("rows",), inc="ve")
    v_var2 = k.val("ve")
    k.wait(A, "ve", v_var2)
    k.op(A, lambda e: e.activation(out=SB["m2B"][0:1, 0:T],
                                   in_=rows[:, TH:TH + T], func=AF.Ln,
                                   bias=sv[0:1, 2:3], scale=sv[0:1, 0:1]),
         writes=("m2Brow",))
    k.op(A, lambda e: e.activation(out=rows[:, TH:TH + T],
                                   in_=SB["m2B"][0:1, 0:T], func=AF.Exp,
                                   bias=sv[0:1, 4:5], scale=sv[0:1, 1:2]),
         reads=("m2Brow",), writes=("rowsX",), inc="ae")
    rstd2_a = k.val("ae")
    k.wait(V, "ae", rstd2_a)
    k.wait(P, "ae", rstd2_a)
    pp["bcast"](rows[:, 0:T], s("m2B"), T, PS[0], "m2B")
    pp["bcast"](rows[:, TH:TH + T], s("r2B"), T, PS[1], "r2B")
    bc2_v = k.val("ve")
    k.wait(G, "ve", bc2_v)
    g2_done = {}
    for c in range(KC):
        scr = SB["scW"][:] if c % 2 == 0 else SB["ivW"][:]
        tg = "scW" if c % 2 == 0 else "ivW"
        k.op(V, (lambda e, cc=c, sc_=scr: e.tensor_tensor(
            out=sc_, in0=SB["xmid"][:, cc * T:(cc + 1) * T], in1=s("m2B"),
            op=ALU.subtract)), reads=("m2B", tg), writes=(tg,))
        k.op(V, (lambda e, sc_=scr: e.tensor_tensor(
            out=sc_, in0=sc_, in1=s("r2B"), op=ALU.mult)),
            reads=(tg, "r2B"), writes=(tg,), inc="ve")
        g2_done[c] = k.op(V, (lambda e, cc=c, sc_=scr: e.tensor_scalar(
            out=SB["h2"][:, cc * T:(cc + 1) * T], in0=sc_,
            scalar1=SB["modv"][:, 4 * KC + cc:4 * KC + cc + 1],
            scalar2=SB["modv"][:, 3 * KC + cc:3 * KC + cc + 1],
            op0=ALU.mult, op1=ALU.add)),
            reads=("modv", tg), writes=("h2",), inc="po")
    h2_v, h2_po = k.val("ve"), k.val("po")

    k.wait(G, "ve", h2_v)
    k.wait(G, "po", h2_po)
    for kc in range(KC):
        park2 = dma_g(t["xm_park"][kc], SB["xmid"][:, kc * T:(kc + 1) * T],
                      "gp2")
    pp["park2"] = park2

    k.wait(A, "ve", h2_v)
    k.wait(A, "po", h2_po)
    k.waitt(A, pp["park2"])
    sqa2 = k.op(A, lambda e: e.activation(
        out=SB["xq1"][:].bitcast(BF16) if False else
        nc_sq1(SB), in_=s("h2"), func=AF.Square),
        writes=("sqh",), inc="ae")
    k.wait(P, "ae", sqa2)
    for kc in range(KC):
        for bi, (off, ln) in enumerate(npc2):
            mm(PS[2 + bi][0:1, 0:ln], s("onesb"),
               nc_sq1(SB)[:, kc * T + off:kc * T + off + ln],
               start=(kc == 0), stop=(kc == KC - 1),
               inc="pe" if (kc == KC - 1 and bi == 1) else None)
    rms1_pe = k.val("pe")
    k.wait(V, "pe", rms1_pe)
    for bi, (off, ln) in enumerate(npc2):
        k.op(V, (lambda e, b=PS[2 + bi], o=off, l=ln: e.tensor_scalar(
            out=rows[:, o:o + l], in0=b[0:1, 0:l], scalar1=1.0 / D,
            scalar2=None, op0=ALU.mult)), writes=("rows",), inc="ve")
    v_ms1 = k.val("ve")
    k.wait(A, "ve", v_ms1)
    k.op(A, lambda e: e.activation(out=rows[:, TH:TH + T], in_=rows[:, 0:T],
                                   func=AF.Ln, bias=sv[0:1, 3:4],
                                   scale=sv[0:1, 0:1]),
         writes=("rowsY",))
    k.op(A, lambda e: e.activation(out=rows[:, 0:T], in_=rows[:, TH:TH + T],
                                   func=AF.Exp, bias=sv[0:1, 4:5],
                                   scale=sv[0:1, 1:2]),
         reads=("rowsY",), writes=("rowsZ",), inc="ae")
    rstd1_a = k.val("ae")
    k.wait(V, "ae", rstd1_a)
    k.wait(P, "ae", rstd1_a)
    pp["bcast"](rows[:, 0:T], s("m2B"), T, PS[0], "r1B")
    r1b_v = k.val("ve")
    pp["h2_v"] = h2_v
    pp["r1b_v"] = r1b_v
    pp["sgate_v"] = sgate_v
    pp["cell"] = cell
    return pp


def _build_p5b(nc, k, t, pp):
    SB, PS = t["SB"], t["PS"]
    s, mm = pp["s"], pp["mm"]
    sv = SB["svec"]
    cell = pp["cell"]

    def r1bc(n2):
        return SB["m2B"][:].unsqueeze(1).broadcast_to([128, n2, T])

    def gbc(gt, c0, n2):
        return SB[gt][:, c0:c0 + n2].unsqueeze(2).broadcast_to([128, n2, T])

    def scbc(n2):
        return SB["scW"][:].unsqueeze(1).broadcast_to([128, n2, T])

    def quant(gt, pb, xq_name, lam_cell, wait_v):
        wg, rg, pg = SB["wgrp"], SB["remg"], SB["pg"]
        amax = [SB["scW"], SB["ivW"]]
        k.wait(V, "ve", wait_v)
        k.wait(G, "ve", wait_v)
        k.waitt(V, pp["park2"])
        k.waitt(G, pp["park2"])
        for grp in range(8):
            c0 = grp * 2
            k.op(V, (lambda e, c=c0: e.tensor_tensor(
                out=pg[:].rearrange("p (a b) -> p a b", a=2),
                in0=SB["h2"][:, c * T:(c + 2) * T].rearrange("p (a b) -> p a b", a=2),
                in1=r1bc(2), op=ALU.mult)),
                reads=("m2B",), writes=("pg",))
            k.op(V, (lambda e, c=c0: e.tensor_tensor(
                out=wg[:].rearrange("p (a b) -> p a b", a=2),
                in0=pg[:].rearrange("p (a b) -> p a b", a=2),
                in1=gbc(gt, c, 2), op=ALU.mult)),
                reads=("pg",), writes=("wgrp",))
            k.op(V, lambda e: e.scalar_tensor_tensor(
                out=pg[:], in0=wg[:], scalar=-1.0, in1=wg[:],
                op0=ALU.mult, op1=ALU.max),
                reads=("wgrp",), writes=("pg",))
            k.op(V, lambda e: e.tensor_tensor(
                out=rg[:, 0:T], in0=pg[:, 0:T], in1=pg[:, T:2 * T], op=ALU.max),
                reads=("pg",), writes=("remg",))
            if grp == 0:
                k.op(V, lambda e: e.tensor_copy(out=amax[0][:], in_=rg[:, 0:T]),
                     reads=("remg",), writes=("amax",))
            else:
                src_t, dst_t = amax[(grp + 1) % 2], amax[grp % 2]
                k.op(V, (lambda e, a=src_t, b=dst_t: e.tensor_tensor(
                    out=b[:], in0=rg[:, 0:T], in1=a[:], op=ALU.max)),
                    reads=("remg", "amax"), writes=("amax",),
                    inc="ve" if grp == 7 else None)
        amax_f = amax[7 % 2]
        k.wait(P, "ve", k.val("ve"))
        bank_ve = [0, 0]
        for j in range(8):
            b = PS[4 + j % 2]
            k.wait(P, "ve", bank_ve[j % 2])
            pe_v = k.op(P, (lambda e, jj=j, bb=b, af=amax_f: e.transpose(
                bb[:, 0:128], af[:, jj * 128:(jj + 1) * 128], s("identf"))),
                inc="pe")
            k.wait(V, "pe", pe_v)
            bank_ve[j % 2] = k.op(V, (lambda e, jj=j, bb=b: e.reduce_max(
                out=SB["tok8"][:, pb * 8 + jj:pb * 8 + jj + 1],
                in_=bb[:, 0:128], axis=AX.X)),
                writes=("tok8",), inc="ve")
        pl = lambda i: SB["tok8"][:, (pb + i) * 8:(pb + i) * 8 + 8]
        k.op(V, lambda e: e.tensor_scalar(out=pl(1), in0=pl(0), scalar1=1e-4,
                                          scalar2=None, op0=ALU.max),
             reads=("tok8",), writes=("tok8",))
        k.op(V, lambda e: e.reciprocal(out=pl(0), in_=pl(1)),
             reads=("tok8",), writes=("tok8",))
        k.op(V, lambda e: e.tensor_scalar(out=pl(2), in0=pl(0), scalar1=127.0,
                                          scalar2=None, op0=ALU.mult),
             reads=("tok8",), writes=("tok8",))
        svcol = 8 if pb == 0 else 9
        k.wait(P, "ve", k.val("ve"))
        pe_v = mm(PS[4][:, 0:1], s("onesr"), lam_cell, start=True, stop=True,
                  inc="pe")
        k.wait(V, "pe", pe_v)
        k.op(V, (lambda e: e.tensor_copy(out=sv[:, svcol:svcol + 1],
                                         in_=PS[4][:, 0:1])),
             writes=("svec",), inc="ve")
        k.op(V, (lambda e: e.tensor_scalar(out=pl(0), in0=pl(1),
                                           scalar1=1.0 / 127.0,
                                           scalar2=sv[:, svcol:svcol + 1],
                                           op0=ALU.mult, op1=ALU.mult)),
             reads=("tok8", "svec"), writes=("tok8",), inc="ve")
        k.wait(P, "ve", k.val("ve"))
        pe_v = k.op(P, (lambda e: e.transpose(PS[4][0:8, 0:128], pl(2),
                                              s("identf"))), inc="pe")
        k.wait(V, "pe", pe_v)
        ve_v = k.op(V, lambda e: e.tensor_copy(out=SB["cpad"][0:8, 0:128],
                                               in_=PS[4][0:8, 0:128]),
                    writes=("cpad",), inc="ve")
        k.wait(P, "ve", ve_v)
        for j in range(8):
            b = PS[4 + j % 2]
            k.wait(P, "ve", bank_ve[j % 2])
            pe_v = mm(b[:, 0:128], SB["sel8"][:, j * 128:(j + 1) * 128],
                      SB["cpad"][0:8, 0:128], start=True, stop=True, inc="pe")
            k.wait(V, "pe", pe_v)
            bank_ve[j % 2] = k.op(V, (lambda e, jj=j, bb=b: e.tensor_copy(
                out=SB["scW"][:, jj * 128:(jj + 1) * 128], in_=bb[:, 0:128])),
                writes=("scW",), inc="ve")
        scb_v = k.val("ve")
        k.wait(G, "ve", scb_v)
        for grp in range(8):
            c0 = grp * 2
            ve_h = [0, 0]
            ae_h = [0, 0]
            for hf in range(2):
                c = c0 + hf
                o = hf * T
                k.op(V, (lambda e, cc=c, o_=o: e.tensor_tensor(
                    out=pg[:, o_:o_ + T], in0=SB["h2"][:, cc * T:(cc + 1) * T],
                    in1=SB["m2B"][:], op=ALU.mult)),
                    reads=("m2B", "pgh%d" % hf), writes=("pgh%d" % hf,))
                k.op(V, (lambda e, cc=c, o_=o, g_=gt: e.tensor_scalar(
                    out=rg[:, o_:o_ + T], in0=pg[:, o_:o_ + T],
                    scalar1=SB[g_][:, cc:cc + 1], scalar2=None, op0=ALU.mult)),
                    reads=("pgh%d" % hf,), writes=("rgh%d" % hf,))
                k.op(V, (lambda e, o_=o: e.tensor_tensor(
                    out=wg[:, o_:o_ + T], in0=rg[:, o_:o_ + T],
                    in1=SB["scW"][:], op=ALU.mult)),
                    reads=("rgh%d" % hf, "scW"), writes=("wgh%d" % hf,))
                ve_h[hf] = k.op(V, (lambda e, o_=o: e.scalar_tensor_tensor(
                    out=rg[:, o_:o_ + T], in0=wg[:, o_:o_ + T], scalar=-1.0,
                    in1=wg[:, o_:o_ + T], op0=ALU.mult, op1=ALU.max)),
                    reads=("wgh%d" % hf,), writes=("rgh%d" % hf,), inc="ve")
            for d in range(6, -1, -1):
                for hf in range(2):
                    o = hf * T
                    k.wait(A, "ve", ve_h[hf])
                    ae_h[hf] = k.op(A, (lambda e, o_=o: e.activation(
                        out=pg[:, o_:o_ + T], in_=rg[:, o_:o_ + T],
                        func=AF.Sign, bias=-63.5, scale=1.0)),
                        reads=("rgh%d" % hf,), writes=("pgh%d" % hf,),
                        inc="ae")
                    k.wait(V, "ae", ae_h[hf])
                    ve_h[hf] = k.op(V, (lambda e, dd=d, o_=o:
                                        e.scalar_tensor_tensor(
                        out=rg[:, o_:o_ + T], in0=pg[:, o_:o_ + T],
                        scalar=-float(1 << dd) / 2.0, in1=rg[:, o_:o_ + T],
                        op0=ALU.mult, op1=ALU.add)),
                        reads=("rgh%d" % hf, "pgh%d" % hf),
                        writes=("rgh%d" % hf,), inc="ve")
            for hf in range(2):
                c = c0 + hf
                o = hf * T
                k.wait(A, "ve", ve_h[hf])
                ae_h[hf] = k.op(A, (lambda e, o_=o: e.activation(
                    out=pg[:, o_:o_ + T], in_=wg[:, o_:o_ + T], func=AF.Sign,
                    bias=0.0, scale=1.0)),
                    reads=("wgh%d" % hf,), writes=("pgh%d" % hf,), inc="ae")
                k.op(V, (lambda e, o_=o: e.tensor_scalar(
                    out=rg[:, o_:o_ + T], in0=rg[:, o_:o_ + T], scalar1=-63.5,
                    scalar2=None, op0=ALU.add)),
                    reads=("rgh%d" % hf,), writes=("rgh%d" % hf,))
                k.wait(V, "ae", ae_h[hf])
                k.op(V, (lambda e, o_=o: e.tensor_tensor(
                    out=pg[:, o_:o_ + T], in0=pg[:, o_:o_ + T],
                    in1=rg[:, o_:o_ + T], op=ALU.mult)),
                    reads=("rgh%d" % hf, "pgh%d" % hf),
                    writes=("pgh%d" % hf,))
                k.op(V, (lambda e, cc=c, o_=o, xq=xq_name: e.tensor_tensor(
                    out=SB[xq][:, cc * T:(cc + 1) * T], in0=wg[:, o_:o_ + T],
                    in1=pg[:, o_:o_ + T], op=ALU.subtract)),
                    reads=("pgh%d" % hf, "wgh%d" % hf), writes=(xq_name,),
                    inc="ve")
        return k.val("ve")

    q1_v = quant("g1t", 0, "xq1", pp["cell"]("s1_lam"), pp["r1b_v"])
    q2_v = quant("g2t", 3, "xq2", pp["cell"]("s2_lam"), q1_v)
    pp["q12_v"] = q2_v
    return pp


def _build_p6(nc, k, t, pp):
    SB, PS = t["SB"], t["PS"]
    s, mm, dma_s, dma_g = pp["s"], pp["mm"], pp["dma_s"], pp["dma_g"]
    sv = SB["svec"]

    k.wait(P, "ve", pp["q12_v"])
    for pb, dst in ((0, "i1B"), (3, "i2B")):
        pl0 = SB["tok8"][:, pb * 8:pb * 8 + 8]
        pe_v = k.op(P, (lambda e, p_=pl0: e.transpose(PS[4][0:8, 0:128], p_,
                                                      s("identf"))), inc="pe")
        k.wait(V, "pe", pe_v)
        ve_v = k.op(V, lambda e: e.tensor_copy(out=SB["cpad"][0:8, 0:128],
                                               in_=PS[4][0:8, 0:128]),
                    writes=("cpad",), inc="ve")
        k.wait(P, "ve", ve_v)
        for j in range(8):
            pe_v = mm(PS[4][:, 0:128], SB["sel8"][:, j * 128:(j + 1) * 128],
                      SB["cpad"][0:8, 0:128], start=True, stop=True, inc="pe")
            k.wait(V, "pe", pe_v)
            ve_v = k.op(V, (lambda e, jj=j, d=dst: e.tensor_copy(
                out=SB[d][:, jj * 128:(jj + 1) * 128], in_=PS[4][:, 0:128])),
                writes=(dst,), inc="ve")
            k.wait(P, "ve", ve_v)
    iB_v = k.val("ve")

    fs1 = [SB["f10"], SB["f11"]]
    fs2 = [SB["f20"], SB["f21"]]
    hm = [SB["hm0"], SB["hm1"]]
    amx_half = [SB["remg"][:, 0:1024], SB["remg"][:, 1024:2048]]
    k.op(V, lambda e: e.memset(SB["remg"][:, 0:1024], 0.0), writes=("remg",))
    k.op(V, lambda e: e.tensor_tensor(out=SB["acc1"][:, 0:FC],
                                      in0=SB["g3t"][:], in1=SB["g3t"][:],
                                      op=ALU.mult),
         reads=("acc1",), writes=("acc1",), inc="ve")
    k.wait(V, "ve", iB_v)
    k.wait(A, "ve", iB_v)
    k.wait(E, "po", pp["wo_cast_po"])
    hm_qg = {}
    evac_ve = {}
    pe_og = {}
    half_pe = {}
    add_ve = {}
    amx_po = {}
    sq_ae = {}
    ssq_pe = {}
    sign_ae = {}
    sqmax_ve = {}
    for og in range(FC):
        b1 = [PS[0], PS[1]] if og % 2 == 0 else [PS[2], PS[3]]
        b2 = [PS[2], PS[3]] if og % 2 == 0 else [PS[0], PS[1]]
        for wi, (wd, fs, fa, fb, fq, svc) in enumerate((
                (t["w1"], fs1, "fa1", "fb1", "fq1", 5),
                (t["w2"], fs2, "fa2", "fb2", "fq2", 6))):
            for hf in range(2):
                k.wait(E, "ae", sign_ae.get((og - 1, wi, hf), 0))
                ld = dma_s(fs[hf][:], wd[og][:, hf * 1024:(hf + 1) * 1024],
                           "sf%d" % (wi * 2 + hf))
                k.waitt(A, ld)
                k.wait(A, "ve", add_ve.get((og, wi, hf - 1),
                                           add_ve.get((og - 1, wi, 1), 0)))
                k.op(A, (lambda e, f_=fs[hf], d=fa, sc_=svc: e.activation(
                    out=SB[d][:], in_=f_[:], func=AF.Sign,
                    scale=sv[:, sc_:sc_ + 1], bias=sv[:, 11:12])),
                    writes=(fa,))
                ae_v = k.op(A, (lambda e, f_=fs[hf], d=fb, sc_=svc: e.activation(
                    out=SB[d][:], in_=f_[:], func=AF.Sign,
                    scale=sv[:, sc_:sc_ + 1], bias=sv[:, 12:13])),
                    writes=(fb,), inc="ae")
                sign_ae[(og, wi, hf)] = ae_v
                k.wait(V, "ae", ae_v)
                k.wait(V, "pe", half_pe.get((og - 1, wi, hf), 0))
                ve_v = k.op(V, (lambda e, a=fa, b_=fb, q=fq, h_=hf:
                                e.tensor_tensor(
                    out=SB[q][:, h_ * 1024:(h_ + 1) * 1024], in0=SB[a][:],
                    in1=SB[b_][:], op=ALU.add)),
                    reads=(fa, fb), writes=(fq,), inc="ve")
                add_ve[(og, wi, hf)] = ve_v
                k.wait(P, "ve", ve_v)
                if og >= 2:
                    k.wait(P, "ve", evac_ve.get(og - 2, 0))
                banks = b1 if wi == 0 else b2
                fqt = SB[fq]
                for kc in range(8):
                    kcg = hf * 8 + kc
                    xq = SB["xq1"] if wi == 0 else SB["xq2"]
                    for th in range(2):
                        mm(banks[th][:, 0:512],
                           fqt[:, kcg * 128:(kcg + 1) * 128],
                           xq[:, kcg * T + th * 512:kcg * T + th * 512 + 512],
                           start=(kcg == 0), stop=(kcg == KC - 1),
                           inc="pe" if (kc == 7 and th == 1) else None)
                half_pe[(og, wi, hf)] = k.val("pe")
        pe_og[og] = k.val("pe")
        k.wait(V, "pe", pe_og[og])
        for th in range(2):
            k.op(V, (lambda e, b=b1[th], t_=th: e.tensor_tensor(
                out=SB["sil0"][:, t_ * 512:(t_ + 1) * 512], in0=b[:, 0:512],
                in1=SB["i1B"][:, t_ * 512:(t_ + 1) * 512], op=ALU.mult)),
                reads=("i1B",), writes=("sil0",), inc="ve" if th == 1 else None)
        ve_r1 = k.val("ve")
        k.wait(A, "ve", ve_r1)
        ae_si = k.op(A, lambda e: e.activation(out=s("sil1"), in_=s("sil0"),
                                               func=AF.Silu),
                     writes=("sil1",), inc="ae")
        if og >= 2:
            k.waitt(V, hm_qg[og - 2])
            k.wait(V, "ae", sq_ae[og - 2])
        for th in range(2):
            k.op(V, (lambda e, b=b2[th], t_=th, h_=hm[og % 2]: e.tensor_tensor(
                out=h_[:, t_ * 512:(t_ + 1) * 512], in0=b[:, 0:512],
                in1=SB["i2B"][:, t_ * 512:(t_ + 1) * 512], op=ALU.mult)),
                reads=("i2B",), writes=("hm",),
                inc="ve" if th == 1 else None)
        k.wait(V, "ae", ae_si)
        ve_hm = k.op(V, (lambda e, h_=hm[og % 2]: e.tensor_tensor(
            out=h_[:], in0=s("sil1"), in1=h_[:], op=ALU.mult)),
            reads=("hm", "sil1"), writes=("hm",), inc="ve")
        evac_ve[og] = k.val("ve")
        k.wait(G, "ve", ve_hm)
        hm_qg[og] = dma_g(t["hmid_dr"][og], hm[og % 2][:],
                          "gh%d" % (og % 2))
        amx_po[og] = k.val("po")
        k.wait(A, "ve", ve_hm)
        k.wait(A, "pe", ssq_pe.get(og - 1, 0))
        k.wait(A, "ve", sqmax_ve.get(og - 1, 0))
        ae_sq = k.op(A, (lambda e, h_=hm[og % 2]: e.activation(
            out=SB["sqb"][:], in_=h_[:], func=AF.Square)),
            writes=("sqb",), inc="ae")
        sq_ae[og] = ae_sq
        k.wait(V, "ae", ae_sq)
        k.op(V, (lambda e, a=amx_half[og % 2], b=amx_half[(og + 1) % 2],
                 o_=og: e.scalar_tensor_tensor(
                 out=b[:], in0=SB["sqb"][:], scalar=SB["acc1"][:, o_:o_ + 1],
                 in1=a[:], op0=ALU.mult, op1=ALU.max)),
             reads=("sqb", "remg", "acc1"), writes=("remg",), inc="ve")
        sqmax_ve[og] = k.val("ve")
        k.wait(P, "ae", ae_sq)
        for th in range(2):
            mm(PS[4 + th][0:1, 0:512], s("onesb"),
               SB["sqb"][:, th * 512:(th + 1) * 512],
               start=(og == 0), stop=(og == FC - 1),
               inc="pe" if th == 1 else None)
        ssq_pe[og] = k.val("pe")
    ss3_pe = k.val("pe")
    amx3_ve = k.val("ve")
    hm_last = hm_qg[FC - 1]
    pp["ss3_pe"] = ss3_pe
    pp["amx3_ve"] = amx3_ve
    pp["amx3_final"] = amx_half[FC % 2]
    pp["hm_last"] = hm_last
    pp["p6_ae"] = k.val("ae")
    pp["p6_pe"] = k.val("pe")
    pp["p6_ve"] = k.val("ve")
    return pp


def _build_p7(nc, k, t, pp):
    SB, PS = t["SB"], t["PS"]
    s, mm, dma_s, dma_g = pp["s"], pp["mm"], pp["dma_s"], pp["dma_g"]
    sv = SB["svec"]
    rows = SB["rows"]
    cell = pp["cell"]

    k.wait(V, "pe", pp["ss3_pe"])
    for th in range(2):
        k.op(V, (lambda e, t_=th: e.tensor_scalar(
            out=rows[:, t_ * 512:(t_ + 1) * 512], in0=PS[4 + t_][0:1, 0:512],
            scalar1=1.0 / FF, scalar2=None, op0=ALU.mult)),
            writes=("rows",), inc="ve" if th == 1 else None)
    v_ms3 = k.val("ve")
    k.wait(A, "ve", v_ms3)
    k.op(A, lambda e: e.activation(out=rows[:, TH:TH + T], in_=rows[:, 0:T],
                                   func=AF.Ln, bias=sv[0:1, 3:4],
                                   scale=sv[0:1, 0:1]),
         writes=("rows3",))
    k.op(A, lambda e: e.activation(out=rows[:, 0:T], in_=rows[:, TH:TH + T],
                                   func=AF.Exp, bias=sv[0:1, 4:5],
                                   scale=sv[0:1, 1:2]),
         reads=("rows3",), writes=("rows",), inc="ae")
    rstd3_a = k.val("ae")
    k.wait(V, "ae", rstd3_a)
    k.wait(P, "ae", rstd3_a)
    pp["bcast"](rows[:, 0:T], s("rs3B"), T, PS[0], "rs3B")
    amxf = pp["amx3_final"]
    k.wait(P, "ve", pp["amx3_ve"])
    bank_ve = [0, 0]
    for j in range(8):
        b = PS[2 + j % 2]
        k.wait(P, "ve", bank_ve[j % 2])
        pe_v = k.op(P, (lambda e, jj=j, bb=b, af=amxf: e.transpose(
            bb[:, 0:128], af[:, jj * 128:(jj + 1) * 128], s("identf"))),
            inc="pe")
        k.wait(V, "pe", pe_v)
        bank_ve[j % 2] = k.op(V, (lambda e, jj=j, bb=b: e.reduce_max(
            out=SB["tok8"][:, 48 + jj:48 + jj + 1], in_=bb[:, 0:128],
            axis=AX.X)), writes=("tok8",), inc="ve")
    pl6 = SB["tok8"][:, 48:56]
    pl7 = SB["tok8"][:, 56:64]
    k.op(V, lambda e: e.nop(), inc="ve")
    k.wait(A, "ve", k.val("ve"))
    k.op(A, lambda e: e.activation(out=pl6, in_=pl6, func=AF.Sqrt,
                                   bias=sv[:, 4:5], scale=sv[:, 0:1]),
         writes=("tok8A",), inc="ae")
    k.wait(V, "ae", k.val("ae"))
    rpl = SB["cpad"][:, 96:104]
    for j in range(8):
        b = PS[2 + j % 2]
        k.wait(P, "ve", bank_ve[j % 2])
        pe_v = k.op(P, (lambda e, jj=j, bb=b: e.transpose(
            bb[:, 0:128], s("rs3B")[:, jj * 128:(jj + 1) * 128],
            s("identf"))), reads=("rs3B",), inc="pe")
        k.wait(V, "pe", pe_v)
        bank_ve[j % 2] = k.op(V, (lambda e, jj=j, bb=b, r_=rpl: e.tensor_copy(
            out=r_[:, jj:jj + 1], in_=bb[:, 0:1])),
            writes=("cpad",), inc="ve")
    k.op(V, lambda e: e.tensor_tensor(out=pl6, in0=pl6, in1=rpl,
                                      op=ALU.mult),
         reads=("tok8", "tok8A", "cpad"), writes=("tok8",), inc="ve")
    k.op(V, lambda e: e.tensor_scalar(out=pl7, in0=pl6, scalar1=1e-4,
                                      scalar2=None, op0=ALU.max),
         reads=("tok8", "tok8A"), writes=("tok8",))
    k.op(V, lambda e: e.reciprocal(out=pl6, in_=pl7),
         reads=("tok8",), writes=("tok8",))
    k.op(V, lambda e: e.tensor_scalar(out=pl6, in0=pl6, scalar1=127.0,
                                      scalar2=None, op0=ALU.mult),
         reads=("tok8",), writes=("tok8",), inc="ve")
    k.wait(P, "ve", k.val("ve"))
    pe_v = mm(PS[0][:, 0:1], s("onesr"), cell("s3_lam"), start=True, stop=True,
              inc="pe")
    k.wait(V, "pe", pe_v)
    k.op(V, lambda e: e.tensor_copy(out=sv[:, 10:11], in_=PS[0][:, 0:1]),
         writes=("svec",))
    k.op(V, lambda e: e.tensor_scalar(out=pl7, in0=pl7, scalar1=1.0 / 127.0,
                                      scalar2=sv[:, 10:11], op0=ALU.mult,
                                      op1=ALU.mult),
         reads=("tok8", "svec"), writes=("tok8",), inc="ve")
    k.wait(P, "ve", k.val("ve"))
    for pl, dst in ((pl6, "u3g"), (pl7, "i3B")):
        pe_v = k.op(P, (lambda e, p_=pl: e.transpose(PS[0][0:8, 0:128], p_,
                                                     s("identf"))), inc="pe")
        k.wait(V, "pe", pe_v)
        ve_v = k.op(V, lambda e: e.tensor_copy(out=SB["cpad"][0:8, 0:128],
                                               in_=PS[0][0:8, 0:128]),
                    writes=("cpad",), inc="ve")
        k.wait(P, "ve", ve_v)
        for j in range(8):
            b = PS[2 + j % 2]
            k.wait(P, "ve", bank_ve[j % 2])
            pe_v = mm(b[:, 0:128], SB["sel8"][:, j * 128:(j + 1) * 128],
                      SB["cpad"][0:8, 0:128], start=True, stop=True, inc="pe")
            k.wait(V, "pe", pe_v)
            bank_ve[j % 2] = k.op(V, (lambda e, jj=j, bb=b, d=dst: e.tensor_copy(
                out=SB[d][:, jj * 128:(jj + 1) * 128], in_=bb[:, 0:128])),
                writes=(dst,), inc="ve")
    k.op(V, lambda e: e.tensor_tensor(out=s("rs3B"), in0=s("rs3B"),
                                      in1=s("u3g"), op=ALU.mult),
         reads=("rs3B", "u3g"), writes=("rs3B",), inc="ve")
    rs3_v = k.val("ve")

    hin = [SB["hin0"], SB["hin1"]]
    k.wait(V, "ve", rs3_v)
    k.wait(V, "gh0", k.val("gh0"))
    k.wait(V, "gh1", k.val("gh1"))
    k.wait(E, "gh0", k.val("gh0"))
    k.wait(E, "gh1", k.val("gh1"))
    casc_ve = {}
    for kc in range(FC):
        if kc >= 2:
            k.wait(E, "ve", casc_ve[kc - 2])
        ld = dma_s(hin[kc % 2][:], t["hmid_dr"][kc], "sh%d" % (kc % 2))
        k.waitt(V, ld)
        u3, r3, p3 = SB["u3g"], SB["rem3"], SB["p3"]
        HF = ((0, 512), (512, 512))
        ve_h = [0, 0]
        ae_h = [0, 0]
        for hf, (o, l) in enumerate(HF):
            k.op(V, (lambda e, c=kc, h_=hin[kc % 2], o_=o, l_=l:
                     e.scalar_tensor_tensor(
                out=u3[:, o_:o_ + l_], in0=h_[:, o_:o_ + l_],
                scalar=SB["g3t"][:, c:c + 1], in1=SB["rs3B"][:, o_:o_ + l_],
                op0=ALU.mult, op1=ALU.mult)),
                reads=("u3g%d" % hf, "rs3B"), writes=("u3g%d" % hf,))
            ve_h[hf] = k.op(V, (lambda e, o_=o, l_=l: e.scalar_tensor_tensor(
                out=r3[:, o_:o_ + l_], in0=u3[:, o_:o_ + l_], scalar=-1.0,
                in1=u3[:, o_:o_ + l_], op0=ALU.mult, op1=ALU.max)),
                reads=("u3g%d" % hf,), writes=("rem3%d" % hf,), inc="ve")
        for d in range(6, -1, -1):
            for hf, (o, l) in enumerate(HF):
                k.wait(A, "ve", ve_h[hf])
                ae_h[hf] = k.op(A, (lambda e, o_=o, l_=l: e.activation(
                    out=p3[:, o_:o_ + l_], in_=r3[:, o_:o_ + l_],
                    func=AF.Sign, bias=-63.5, scale=1.0)),
                    reads=("rem3%d" % hf,), writes=("p3%d" % hf,), inc="ae")
                k.wait(V, "ae", ae_h[hf])
                ve_h[hf] = k.op(V, (lambda e, dd=d, o_=o, l_=l:
                                    e.scalar_tensor_tensor(
                    out=r3[:, o_:o_ + l_], in0=p3[:, o_:o_ + l_],
                    scalar=-float(1 << dd) / 2.0, in1=r3[:, o_:o_ + l_],
                    op0=ALU.mult, op1=ALU.add)),
                    reads=("rem3%d" % hf, "p3%d" % hf),
                    writes=("rem3%d" % hf,), inc="ve")
        for hf, (o, l) in enumerate(HF):
            k.wait(A, "ve", ve_h[hf])
            ae_h[hf] = k.op(A, (lambda e, o_=o, l_=l: e.activation(
                out=p3[:, o_:o_ + l_], in_=u3[:, o_:o_ + l_], func=AF.Sign,
                bias=0.0, scale=1.0)),
                reads=("u3g%d" % hf,), writes=("p3%d" % hf,), inc="ae")
            k.op(V, (lambda e, o_=o, l_=l: e.tensor_scalar(
                out=r3[:, o_:o_ + l_], in0=r3[:, o_:o_ + l_], scalar1=-63.5,
                scalar2=None, op0=ALU.add)),
                reads=("rem3%d" % hf,), writes=("rem3%d" % hf,))
            k.wait(V, "ae", ae_h[hf])
            k.op(V, (lambda e, o_=o, l_=l: e.tensor_tensor(
                out=p3[:, o_:o_ + l_], in0=p3[:, o_:o_ + l_],
                in1=r3[:, o_:o_ + l_], op=ALU.mult)),
                reads=("rem3%d" % hf, "p3%d" % hf), writes=("p3%d" % hf,))
            k.op(V, (lambda e, c=kc, o_=o, l_=l: e.tensor_tensor(
                out=SB["xq3"][:, c * T + o_:c * T + o_ + l_],
                in0=u3[:, o_:o_ + l_], in1=p3[:, o_:o_ + l_],
                op=ALU.subtract)),
                reads=("p3%d" % hf, "u3g%d" % hf), writes=("xq3",), inc="ve")
        casc_ve[kc] = k.val("ve")
    xq3_v = k.val("ve")

    w3s = [SB["w30"], SB["w31"]]
    wq3 = [SB["wq30"], SB["wq31"]]
    xm = [SB["xm0"], SB["xm1"]]
    k.wait(P, "ve", xq3_v)
    k.wait(E, "ae", pp["p6_ae"])
    k.wait(E, "pe", pp["p6_pe"])
    k.wait(E, "ve", pp["p6_ve"])
    k.waitt(E, pp["park2"])
    sgn3_ae = {}
    add3_ve = {}
    pe_g = {}
    evac3_ve = {}
    pe_oc = {}
    tr_ve = [0, 0]
    os1_qg = [0, 0]
    for oc in range(KC):
        banks = [PS[0], PS[1]] if oc % 2 == 0 else [PS[2], PS[3]]
        if oc >= 2:
            k.wait(E, "ve", evac3_ve[oc - 2])
        xm_ld = dma_s(xm[oc % 2][:], t["xm_park"][oc], "sr%d" % (oc % 2))
        if oc >= 2:
            k.wait(P, "ve", evac3_ve[oc - 2])
        for pc in range(8):
            g = oc * 8 + pc
            if g < 2:
                k.wait(E, "po", k.val("po"))
            k.wait(E, "ae", sgn3_ae.get(g - 2, 0))
            ld = dma_s(w3s[g % 2][:], t["w3"][oc][:, pc * 1024:(pc + 1) * 1024],
                       "sx%d" % (g % 2))
            k.waitt(A, ld)
            k.wait(A, "ve", add3_ve.get(g - 1, 0))
            k.op(A, (lambda e, f_=w3s[g % 2]: e.activation(
                out=s("wa3"), in_=f_[:], func=AF.Sign, scale=sv[:, 7:8],
                bias=sv[:, 11:12])), writes=("wa3",))
            ae_v = k.op(A, (lambda e, f_=w3s[g % 2]: e.activation(
                out=s("wb3"), in_=f_[:], func=AF.Sign, scale=sv[:, 7:8],
                bias=sv[:, 12:13])), writes=("wb3",), inc="ae")
            sgn3_ae[g] = ae_v
            k.wait(V, "ae", ae_v)
            k.wait(V, "pe", pe_g.get(g - 2, 0))
            ve_v = k.op(V, (lambda e, q=wq3[g % 2]: e.tensor_tensor(
                out=q[:], in0=s("wa3"), in1=s("wb3"), op=ALU.add)),
                reads=("wa3", "wb3"), writes=("wq3",), inc="ve")
            add3_ve[g] = ve_v
            k.wait(P, "ve", ve_v)
            for kc in range(8):
                kcg = pc * 8 + kc
                for th in range(2):
                    mm(banks[th][:, 0:512],
                       wq3[g % 2][:, kc * 128:(kc + 1) * 128],
                       SB["xq3"][:, kcg * T + th * 512:kcg * T + th * 512 + 512],
                       start=(kcg == 0), stop=(kcg == FC - 1),
                       inc="pe" if (kc == 7 and th == 1) else None)
            pe_g[g] = k.val("pe")
        pe_oc[oc] = k.val("pe")
        k.wait(V, "pe", pe_oc[oc])
        k.waitt(V, xm_ld)
        for th in range(2):
            k.op(V, (lambda e, b=banks[th], t_=th, c=oc: e.scalar_tensor_tensor(
                out=SB["os0"][:, t_ * 512:(t_ + 1) * 512], in0=b[:, 0:512],
                scalar=SB["modv"][:, 5 * KC + c:5 * KC + c + 1],
                in1=SB["i3B"][:, t_ * 512:(t_ + 1) * 512],
                op0=ALU.mult, op1=ALU.mult)),
                reads=("i3B", "modv"), writes=("os0",))
            k.op(V, (lambda e, t_=th, c=oc: e.tensor_tensor(
                out=SB["os0"][:, t_ * 512:(t_ + 1) * 512],
                in0=SB["os0"][:, t_ * 512:(t_ + 1) * 512],
                in1=xm[c % 2][:, t_ * 512:(t_ + 1) * 512], op=ALU.add)),
                reads=("os0",), writes=("os0",), inc="ve" if th == 1 else None)
        evac3_ve[oc] = k.val("ve")
        k.wait(P, "ve", evac3_ve[oc])
        for j in range(QB):
            b = PS[4 + j % 2]
            k.wait(P, "ve", tr_ve[j % 2])
            pe_v = k.op(P, (lambda e, jj=j, bb=b: e.transpose(
                bb[:, 0:128], SB["os0"][:, jj * 128:(jj + 1) * 128],
                s("identf"))), inc="pe")
            k.wait(V, "pe", pe_v)
            k.waitt(V, os1_qg[j % 2])
            tr_ve[j % 2] = k.op(V, (lambda e, jj=j, bb=b: e.tensor_copy(
                out=SB["os1"][:, (jj % 2) * 128:(jj % 2) * 128 + 128],
                in_=bb[:, 0:128])), writes=("os1",), inc="ve")
            k.wait(G, "ve", tr_ve[j % 2])
            os1_qg[j % 2] = dma_g(
                t["out"][j][:, oc * 128:(oc + 1) * 128],
                SB["os1"][:, (j % 2) * 128:(j % 2) * 128 + 128],
                "go%d" % (j % 2))
    k.wait(G, "go0", k.val("go0"))
    k.wait(G, "go1", k.val("go1"))
    return pp


_CACHE = {}


def _get_program():
    if "nc" in _CACHE:
        return _CACHE["nc"]
    nc = bass.Bass("TRN2", target_bir_lowering=False,
                   detect_race_conditions=False)
    k = K()
    t = _alloc(nc)
    import contextlib
    stack = contextlib.ExitStack()
    pp = _build(nc, k, t)
    pp = _build_p23(nc, k, t, pp)
    pp = _build_p45(nc, k, t, pp)
    pp = _build_p5b(nc, k, t, pp)
    pp = _build_p6(nc, k, t, pp)
    pp = _build_p7(nc, k, t, pp)
    for name in sorted(k.semv):
        k.sems[name] = stack.enter_context(nc.semaphore(name))
    with nc.Block() as block:
        @block.sync
        def _(eng):
            for f in k.q["sync"]:
                f(eng)

        @block.tensor
        def _(eng):
            for f in k.q["tensor"]:
                f(eng)

        @block.vector
        def _(eng):
            for f in k.q["vector"]:
                f(eng)

        @block.scalar
        def _(eng):
            for f in k.q["scalar"]:
                f(eng)

        @block.gpsimd
        def _(eng):
            for f in k.q["gpsimd"]:
                f(eng)
    stack.close()
    _CACHE["nc"] = nc
    return nc


def kernel(**inputs):
    in_maps = _host_prepare(
        inputs["L0_attn_x"], inputs["L0_attn_W_qkv"], inputs["L0_attn_W_out"],
        inputs["L0_attn_b_out"], inputs["L0_ffn_w1"], inputs["L0_ffn_g1"],
        inputs["L0_ffn_w2"], inputs["L0_ffn_g2"], inputs["L0_ffn_w3"],
        inputs["L0_ffn_g3"], inputs["L0_c"], inputs["L0_wmod"])
    nc = _get_program()
    res = run_bass_kernel_spmd(nc, in_maps, core_ids=list(range(NC)))
    outs = [np.asarray(r["out"], dtype=np.float32).reshape(T, D)
            for r in res.results]
    return np.concatenate(outs, axis=0).reshape(1, S, D)


# version 12
# speedup vs baseline: 1.1535x; 1.0162x over previous
import math
import numpy as np
import ml_dtypes

import concourse.bass as bass
import concourse.mybir as mybir
from concourse.bass_utils import run_bass_kernel_spmd

F32 = mybir.dt.float32
F16 = mybir.dt.float16
BF16 = mybir.dt.bfloat16
AF = mybir.ActivationFunctionType
ALU = mybir.AluOpType
AX = mybir.AxisListType

S, D, NH, HD, FF, NC = 8192, 2048, 16, 128, 8192, 8
T = S // NC
TH = T + 128
KC = D // 128
FC = FF // 128
QB = T // 128
TC = TH // 128
THETA = 10000.0
NEG = -1.0e30
ATANH_HALF = 0.5493061443340549
NWM = 6 * D
MS = NWM // 128

DBG_PHASE = 0


def _sw(A, name=None):
    Kd, Nd = A.shape
    return np.ascontiguousarray(
        A.reshape(Kd // 128, 128, Nd // 128, 128).transpose(2, 1, 0, 3)
        .reshape(Nd // 128, 128, Kd))


def _host_prepare(x, W_qkv, W_out, b_out, w1, g1, w2, g2, w3, g3, c, wmod):
    f32 = np.float32
    X = np.ascontiguousarray(np.asarray(x, dtype=f32).reshape(S, D))
    Wq = np.asarray(W_qkv, dtype=f32)
    n = np.arange(3 * D)
    kind, r = n // D, n % D
    hh, hd = r // HD, r % HD
    Wq_sw = _sw(Wq[:, hh * (HD * 3) + hd * 3 + kind])
    Wo_sw = _sw(np.asarray(W_out, dtype=f32))
    w1_sw = _sw(np.ascontiguousarray(np.asarray(w1, dtype=f32).T))
    w2_sw = _sw(np.ascontiguousarray(np.asarray(w2, dtype=f32).T))
    w3_sw = _sw(np.ascontiguousarray(np.asarray(w3, dtype=f32).T))
    wmod_sw = _sw(np.asarray(wmod, dtype=f32))
    vecs = dict(
        b_out2=np.asarray(b_out, dtype=f32).reshape(KC, 128).copy(),
        g1_2=np.asarray(g1, dtype=f32).reshape(KC, 128).copy(),
        g2_2=np.asarray(g2, dtype=f32).reshape(KC, 128).copy(),
        g3_2=np.asarray(g3, dtype=f32).reshape(FC, 128).copy(),
        c2=np.asarray(c, dtype=f32).reshape(KC, 128).copy())
    cosg, sing = _rope_tables()
    in_maps = []
    for cix in range(NC):
        start = cix * T
        xs = np.zeros((TH, D), dtype=f32)
        lo = max(0, start - 128)
        xs[128 - (start - lo):] = X[lo:start + T]
        pk = np.clip(np.arange(start - 128, start + T), 0, S - 1)
        cK = np.ascontiguousarray(cosg[pk].T)
        sK = np.ascontiguousarray(sing[pk].T)
        cQ = np.ascontiguousarray(cosg[start:start + T].T)
        sQ = np.ascontiguousarray(sing[start:start + T].T)
        masks = np.full((QB, 128, 256), NEG, dtype=f32)
        rr = np.arange(128)[:, None]
        cc = np.arange(256)[None, :]
        for qb in range(QB):
            i = start + 128 * qb + rr
            j = start - 128 + 128 * qb + cc
            ok = (j >= 0) & (j <= i) & (j >= i - 128)
            masks[qb][ok] = 0.0
        in_maps.append(dict(
            x_sh=np.ascontiguousarray(xs.reshape(TC, 128, D)),
            wq=Wq_sw, wo=Wo_sw, w1=w1_sw, w2=w2_sw, w3=w3_sw, wmod=wmod_sw,
            cosK=cK, sinK=sK, cosQ=cQ, sinQ=sQ,
            masks=masks, **vecs))
    return in_maps


_TBL = {}


def _rope_tables():
    if "c" in _TBL:
        return _TBL["c"], _TBL["s"]
    try:
        import jax
        import jax.numpy as jnp
        with jax.default_device(jax.devices("cpu")[0]):
            inv_freq = 1.0 / THETA ** (jnp.arange(0, HD, 2, dtype=jnp.float32)
                                       / HD)
            tt = jnp.arange(S, dtype=jnp.float32)
            ang = jnp.repeat(tt[:, None] * inv_freq[None, :], 2, axis=-1)
            c, s_ = np.asarray(jnp.cos(ang)), np.asarray(jnp.sin(ang))
    except Exception:
        inv = (np.float32(1.0) / np.power(
            np.float32(THETA),
            np.arange(0, HD, 2).astype(np.float32) / np.float32(HD)))
        ang = np.repeat(np.arange(S, dtype=np.float32)[:, None] * inv[None, :],
                        2, axis=-1)
        c, s_ = np.cos(ang), np.sin(ang)
    _TBL["c"], _TBL["s"] = c, s_
    return c, s_


ENGINES = ("sync", "tensor", "vector", "scalar", "gpsimd")
V, P, A, E, G = "vector", "tensor", "scalar", "sync", "gpsimd"


class K:

    def __init__(self):
        self.q = {e: [] for e in ENGINES}
        self.semv = {}
        self.sems = {}
        self.hi = {e: {} for e in ENGINES}
        self.pend = {e: set() for e in ENGINES}

    def plan(self, name, n=1):
        self.semv[name] = self.semv.get(name, 0) + n
        return self.semv[name]

    def val(self, name):
        return self.semv.get(name, 0)

    def waitt(self, eng, tok):
        if tok is None or tok == 0:
            return
        self.wait(eng, tok[0], tok[1])

    def wait(self, eng, name, value):
        if value is None or value <= 0:
            return
        if self.hi[eng].get(name, -1) >= value:
            return
        self.hi[eng][name] = value
        self.q[eng].append(lambda e, n=name, v=value: e.wait_ge(self.sems[n], v))

    def op(self, eng, fn, reads=(), writes=(), inc=None, incn=1):
        tags = set(reads) | set(writes)
        if tags & self.pend[eng]:
            self.q[eng].append(lambda e: e.drain())
            self.pend[eng].clear()
        tgt = self.plan(inc, incn) if inc else None

        def f(e, fn=fn, inc=inc, incn=incn):
            ins = fn(e)
            if inc is not None:
                ins.then_inc(self.sems[inc], incn)
        self.q[eng].append(f)
        self.pend[eng].update(writes)
        return tgt


def _alloc(nc):
    t = {}

    def din(name, shape, dtype=F32):
        t[name] = nc.dram_tensor(name, shape, dtype, kind="ExternalInput").ap()

    din("x_sh", [TC, 128, D])
    din("wq", [48, 128, D]); din("wo", [KC, 128, D])
    din("w1", [FC, 128, D]); din("w2", [FC, 128, D]); din("w3", [KC, 128, FF])
    din("wmod", [MS, 128, D])
    din("b_out2", [KC, 128]); din("g1_2", [KC, 128]); din("g2_2", [KC, 128])
    din("g3_2", [FC, 128]); din("c2", [KC, 128])
    din("cosK", [128, TH]); din("sinK", [128, TH])
    din("cosQ", [128, T]); din("sinQ", [128, T])
    din("masks", [QB, 128, 256])
    t["out"] = nc.dram_tensor("out", [QB, 128, D], F32, kind="ExternalOutput").ap()
    if DBG_PHASE:
        t["dbg"] = nc.dram_tensor("dbg", [128, 18432], F32, kind="ExternalOutput").ap()
    t["x_park"] = nc.dram_tensor("x_park", [KC, 128, T], F32, kind="Internal").ap()
    t["xm_park"] = nc.dram_tensor("xm_park", [KC, 128, T], F32, kind="Internal").ap()
    t["v_dram"] = nc.dram_tensor("v_dram", [NH, 128, TH], BF16, kind="Internal").ap()
    t["hmid_dr"] = nc.dram_tensor("hmid_dr", [FC, 128, T], F16, kind="Internal").ap()

    t["ident_d"] = nc.inline_tensor(np.eye(128, dtype=np.float32), "ident_d").ap()
    t["identb_d"] = nc.inline_tensor(np.eye(128, dtype=ml_dtypes.bfloat16),
                                     "identb_d").ap()
    Rnp = np.zeros((128, 128), dtype=ml_dtypes.bfloat16)
    for i in range(64):
        Rnp[2 * i + 1, 2 * i] = -1.0
        Rnp[2 * i, 2 * i + 1] = 1.0
    t["R_d"] = nc.inline_tensor(Rnp, "R_d").ap()
    t["ones_d"] = nc.inline_tensor(np.ones((1, 128), dtype=np.float32), "ones_d").ap()
    seln = np.zeros((8, 8 * 128), dtype=np.float32)
    for j in range(8):
        seln[j, j * 128:(j + 1) * 128] = 1.0
    t["sel8_d"] = nc.inline_tensor(seln, "sel8_d").ap()

    arena = nc.alloc_sbuf_tensor("arena", [128, 206848], mybir.dt.uint8)
    A0 = nc.lookup_mloc(arena).addr
    SB = {}

    def at(name, rel, shape, dtype=F32):
        SB[name] = nc.alloc_sbuf_tensor_at(name, shape, dtype, offset=A0 + rel)

    KB = 1024
    at("identf", 0, [128, 128])
    at("Rb", 512, [128, 128], BF16)
    at("onesr", 768, [1, 128])
    at("onesb", 1280, [128, 1], BF16)
    at("scTb", 1312, [128, 16], BF16)
    at("sel8", 1344, [8, 8 * 128])
    at("bout", 5440, [128, KC])
    at("g1t", 5504, [128, KC])
    at("g2t", 5568, [128, KC])
    at("g3t", 5632, [128, FC])
    at("modv", 5888, [128, 8 * KC])
    at("svec", 6400, [128, 24])
    at("tok8", 6496, [128, 64])
    at("c2t", 6752, [16, 128])
    at("cpad", 7264, [128, 128])
    at("acc1", 7776, [128, 256])
    at("acc2", 8800, [128, 256])
    at("identb", 9824, [128, 128], BF16)
    at("onesf", 10080, [128, 1])
    at("rows", 10 * KB, [1, 2304])
    at("xm1", 10 * KB, [128, 1024])
    RT = 19456
    at("cosKt", RT, [128, TH]); at("sinKt", RT + 4608, [128, TH])
    at("cosQt", RT + 9216, [128, T]); at("sinQt", RT + 13312, [128, T])
    at("maskt", RT + 17408, [128, QB * 256])
    at("wgrp", RT, [128, 2048])
    at("remg", RT + 8192, [128, 2048])
    at("pg", RT + 16384, [128, 2048])
    at("hin0", RT, [128, T], F16)
    at("hin1", RT + 2048, [128, T], F16)
    at("u3g", RT + 4096, [128, 1024])
    at("rem3", RT + 8192, [128, 1024])
    at("p3", RT + 12288, [128, 1024])
    at("rs3B", RT + 16384, [128, 1024])
    at("i3B", RT + 20480, [128, 1024])
    at("sqb", RT + 20480, [128, 1024], BF16)
    STb = 45568
    at("st0", STb, [128, 2048]); at("st1", STb + 8192, [128, 2048])
    at("stq0", STb + 16384, [128, 2048], BF16)
    at("stq1", STb + 20480, [128, 2048], BF16)
    at("p1s0", STb + 24576, [128, 512]); at("p1s1", STb + 26624, [128, 512])
    at("scW", STb + 16384, [128, 1024]); at("ivW", STb + 20480, [128, 1024])
    at("m2B", STb + 24576, [128, 1024])
    at("r2B", 19456 + 16384, [128, 1024])
    at("f10", STb, [128, 1024]); at("f11", STb + 4096, [128, 1024])
    at("f20", STb + 8192, [128, 1024]); at("f21", STb + 12288, [128, 1024])
    at("fa1", STb + 16384, [128, 1024], BF16); at("fb1", STb + 18432, [128, 1024], BF16)
    at("fq1", STb + 20480, [128, 2048], BF16)
    at("fa2", STb + 24576, [128, 1024], BF16); at("fb2", STb + 26624, [128, 1024], BF16)
    at("fq2", 74752 + 24576, [128, 2048], BF16)
    at("w30", STb, [128, 1024]); at("w31", STb + 4096, [128, 1024])
    at("os0", STb + 8192, [128, 1024]); at("os1", STb + 12288, [128, 1024])
    at("wa3", STb + 16384, [128, 1024], BF16); at("wb3", STb + 18432, [128, 1024], BF16)
    at("wq30", STb + 20480, [128, 1024], BF16); at("wq31", STb + 22528, [128, 1024], BF16)
    at("xm0", STb + 24576, [128, 1024])
    BG = 74752
    at("xfm", BG, [128, KC * TH])
    at("qbf", BG, [128, NH * T], BF16)
    at("kbf", BG + 32768, [128, NH * TH], BF16)
    at("hbf", BG + 73728, [128, KC * TH], BF16)
    at("vbuf0", BG + 110592, [128, TH], BF16)
    at("vbuf1", BG + 112896, [128, TH], BF16)
    at("vfm0", BG + 110592, [128, TH], BF16)
    at("vfm1", BG + 112896, [128, TH], BF16)
    at("vtm0", BG + 115200, [128, TC * 128], BF16)
    at("vtm1", BG + 117504, [128, TC * 128], BF16)
    at("scf", BG + 119808, [128, 256])
    at("ppf", BG + 120832, [128, 256])
    at("pnf", BG + 121856, [128, 256])
    at("ptb", BG + 122880, [128, 256], BF16)
    at("scfB", BG + 123392, [128, 256])
    at("ppfB", BG + 124416, [128, 256])
    at("pnfB", BG + 125440, [128, 256])
    at("ptbB", BG + 126464, [128, 256], BF16)
    at("lnt0", BG + 110592, [128, 1152])
    at("lnt1", BG + 115200, [128, 1152])
    at("xbp", BG + 119808, [128, 2 * TH], BF16)
    at("mB", BG + 119808, [128, TH])
    at("rB", BG + 124416, [128, TH])
    at("xres0", BG + 32768, [128, 1024])
    at("xres1", BG + 36864, [128, 1024])
    at("xmid", BG + 65536, [128, KC * T])
    at("xmbf", BG + 32768, [128, KC * T], BF16)
    at("sq2", BG, [128, KC * T], BF16)
    at("h2", BG, [128, KC * T])
    at("xq1", BG + 65536, [128, KC * T], BF16)
    at("xq2", BG + 98304, [128, KC * T], BF16)
    at("sil0", BG, [128, 1024]); at("sil1", BG + 4096, [128, 1024])
    at("hm0", BG + 8192, [128, 1024], F16); at("hm1", BG + 10240, [128, 1024], F16)
    at("amx3", BG + 12288, [128, 1024])
    at("i1B", BG + 16384, [128, 1024]); at("i2B", BG + 20480, [128, 1024])
    at("xq3", BG, [128, FC * T], BF16)
    at("sqp", BG + 124416, [128, 2 * TH], BF16)

    PSl = [nc.alloc_psum_tensor("ps%d" % i, [128, 512], F32) for i in range(6)]
    t["SB"] = SB
    t["PS"] = PSl
    t["PT"] = [nc.alloc_psum_tensor("pt%d" % i, [128, 1024], BF16) for i in range(2)]
    return t


def _build(nc, k, t):
    SB, PS = t["SB"], t["PS"]

    def s(name):
        return SB[name][:]

    def dma_s(dst, src, st="sc"):
        return st, k.op(E, lambda e, d=dst, sr=src: e.dma_start(out=d, in_=sr),
                        inc=st, incn=16)

    def dma_g(dst, src, st="gp"):
        return st, k.op(G, lambda e, d=dst, sr=src: e.dma_start(out=d, in_=sr),
                        inc=st, incn=16)

    def mm(out, lhsT, rhs, start, stop, inc=None):
        return k.op(P, (lambda e, o=out, l=lhsT, r=rhs, st=start, sp=stop:
                        e.matmul(o, l, r, start=st, stop=sp)), inc=inc)

    class TP:

        def __init__(self, banks):
            self.banks = banks
            self.pend_ve = [0, 0]
            self.i = 0

        def do(self, src_ap, dst_ap, eng=V):
            b = self.i % 2
            bank = self.banks[b]
            k.wait(P, "ve", self.pend_ve[b])
            tp_pe = k.op(P, (lambda e, o=bank[:, 0:128], i_=src_ap:
                             e.transpose(o, i_, s("identf"))), inc="pe")
            k.wait(eng, "pe", tp_pe)
            self.pend_ve[b] = k.op(
                eng, (lambda e, d=dst_ap, bb=bank:
                      e.tensor_copy(out=d, in_=bb[:, 0:128])), inc="ve")
            self.i += 1

    def bcast(rows_ap, dst_ap, n, bank, name="bc"):
        done = []
        for off in range(0, n, 512):
            ln = min(512, n - off)
            pe_v = mm(bank[:, 0:ln], s("onesr"), rows_ap[:, off:off + ln],
                      start=True, stop=True, inc="pe")
            k.wait(V, "pe", pe_v)
            ve_v = k.op(V, (lambda e, d=dst_ap[:, off:off + ln], b=bank, l=ln:
                            e.tensor_copy(out=d, in_=b[:, 0:l])),
                        writes=(name,), inc="ve")
            k.wait(P, "ve", ve_v)
            done.append(ve_v)
        return done[-1]

    ldc = {}
    ldc["identf"] = dma_s(s("identf"), t["ident_d"])
    ldc["Rb"] = dma_s(s("Rb"), t["R_d"])
    ldc["identb"] = dma_s(s("identb"), t["identb_d"])
    ldc["onesr"] = dma_s(s("onesr"), t["ones_d"])
    ldc["sel8"] = dma_s(s("sel8"), t["sel8_d"])
    ldc["c2t"] = dma_s(s("c2t"), t["c2"])
    for c in range(KC):
        dma_s(SB["bout"][:, c:c + 1], t["b_out2"][c].unsqueeze(1))
        dma_s(SB["g1t"][:, c:c + 1], t["g1_2"][c].unsqueeze(1))
        ldc["g2_%d" % c] = dma_s(SB["g2t"][:, c:c + 1],
                                 t["g2_2"][c].unsqueeze(1))
    for c in range(FC):
        ldc["g3_%d" % c] = dma_s(SB["g3t"][:, c:c + 1],
                                 t["g3_2"][c].unsqueeze(1))
    ldc["cosK"] = dma_s(s("cosKt"), t["cosK"])
    ldc["sinK"] = dma_s(s("sinKt"), t["sinK"])
    ldc["cosQ"] = dma_s(s("cosQt"), t["cosQ"])
    ldc["sinQ"] = dma_s(s("sinQt"), t["sinQ"])
    for qb in range(QB):
        ldc["mask"] = dma_s(SB["maskt"][:, qb * 256:(qb + 1) * 256], t["masks"][qb])
    sc_tok = ("sc", k.val("sc"))

    sv = SB["svec"]
    k.op(V, lambda e: e.memset(sv[:, 0:1], 1.0), writes=("svec",))
    k.op(V, lambda e: e.memset(sv[:, 1:2], -0.5), writes=("svec",))
    k.op(V, lambda e: e.memset(sv[:, 4:5], 0.0), writes=("svec",))
    k.op(V, lambda e: e.memset(sv[:, 13:14], -63.5), writes=("svec",))
    nc.const_aps.register(mybir.dt.float32, -63.5, sv[:, 13:14])
    nc.const_aps.register(mybir.dt.float32, 0.0, sv[:, 4:5])
    nc.const_aps.register(mybir.dt.float32, 1.0, sv[:, 0:1])
    for col, val in ((2, 1e-5), (3, 1e-8), (11, -ATANH_HALF), (12, ATANH_HALF)):
        k.op(V, (lambda e, c_=col, v_=val: e.tensor_scalar(
            out=sv[:, c_:c_ + 1], in0=sv[:, 0:1], scalar1=v_, scalar2=None,
            op0=ALU.mult)), reads=("svec",), writes=("svec",))
    k.op(V, lambda e: e.memset(SB["onesf"][:], 1.0), writes=("onesf",))
    k.op(V, lambda e: e.memset(SB["cpad"][:], 0.0), writes=("cpad",))
    k.op(V, lambda e: e.memset(s("onesb"), 1.0), writes=("onesb",), inc="ve")
    v_consts = k.val("ve")

    xst = [SB["st0"], SB["st1"]]
    xtp = TP([PS[0], PS[1]])
    xld = []
    xtrans_done = []
    for tt in range(TC):
        if tt >= 2:
            k.wait(E, "pe", xtrans_done[tt - 2])
        xld.append(dma_s(xst[tt % 2][:], t["x_sh"][tt], "sx%d" % (tt % 2)))
        k.waitt(P, xld[tt])
        if tt == 0:
            k.waitt(P, sc_tok)
        for c in range(KC):
            xtp.do(xst[tt % 2][:, c * 128:(c + 1) * 128],
                   SB["xfm"][:, c * TH + tt * 128: c * TH + tt * 128 + 128])
        xtrans_done.append(k.val("pe"))
    x_in_v = k.val("ve")

    k.waitt(A, sc_tok)
    k.wait(A, "ve", v_consts)
    k.op(A, lambda e: e.activation(out=SB["cpad"][0:16, :], in_=s("c2t"),
                                   func=AF.Silu), writes=("cpad",), inc="ae")
    k.wait(P, "ae", k.val("ae"))
    pe_v = k.op(P, lambda e: e.transpose(PS[5][:, 0:128], s("cpad"),
                                         s("identf")), inc="pe")
    k.wait(V, "pe", pe_v)
    k.op(V, lambda e: e.tensor_copy(out=s("scTb"), in_=PS[5][:, 0:16]),
         writes=("scTb",))
    k.op(V, lambda e: e.tensor_copy(out=SB["cpad"][:, 64:80],
                                    in_=PS[5][:, 0:16]),
         writes=("cpad",), inc="ve")
    v_sct = k.val("ve")

    wmst = [SB["st0"], SB["st1"]]
    scTf = SB["cpad"][:, 64:80]
    k.wait(E, "pe", xtrans_done[-1])
    k.wait(P, "ve", v_sct)
    mod_pe0 = {}
    for ms in range(MS):
        if ms >= 2:
            k.wait(E, "pe", mod_pe0[ms - 2])
        w_ld = dma_s(wmst[ms % 2][:], t["wmod"][ms], "sx%d" % (ms % 2))
        k.waitt(P, w_ld)
        for kc in range(KC):
            mm(PS[4][:, 0:1], wmst[ms % 2][:, kc * 128:(kc + 1) * 128],
               scTf[:, kc:kc + 1], start=(kc == 0), stop=(kc == KC - 1),
               inc="pe" if kc == KC - 1 else None)
        mod_pe0[ms] = k.val("pe")
        pe_v = k.val("pe")
        k.wait(V, "pe", pe_v)
        vi, ch = ms // KC, ms % KC
        ve_v = k.op(V, (lambda e, v_=vi, c_=ch:
                        e.tensor_copy(out=SB["modv"][:, v_ * KC + c_:
                                                     v_ * KC + c_ + 1],
                                      in_=PS[4][:, 0:1])),
                    writes=("modv",), inc="ve")
        k.wait(P, "ve", ve_v)
    mo = SB["modv"]

    def slot(i):
        return mo[:, i * KC:(i + 1) * KC]
    k.op(V, lambda e: e.tensor_scalar(out=slot(1), in0=slot(1), scalar1=1.0,
                                      scalar2=None, op0=ALU.add),
         reads=("modv",), writes=("modv",))
    k.op(V, lambda e: e.tensor_scalar(out=slot(4), in0=slot(4), scalar1=1.0,
                                      scalar2=None, op0=ALU.add),
         reads=("modv",), writes=("modv",))
    for gi in (2, 5):
        k.wait(A, "ve", k.val("ve"))
        ae_v = k.op(A, (lambda e, i=gi: e.activation(out=slot(6), in_=slot(i),
                                                     func=AF.Sign)),
                    inc="ae")
        k.wait(V, "ae", ae_v)
        k.op(V, lambda e: e.scalar_tensor_tensor(out=slot(7), in0=slot(6),
                                                 scalar=-1.0, in1=slot(6),
                                                 op0=ALU.mult, op1=ALU.mult),
             writes=("modv7",))
        k.op(V, lambda e: e.tensor_scalar(out=slot(7), in0=slot(7), scalar1=1.0,
                                          scalar2=None, op0=ALU.add),
             reads=("modv7",), writes=("modv7",))
        k.op(V, lambda e: e.tensor_tensor(out=slot(7), in0=slot(7), in1=slot(6),
                                          op=ALU.add),
             reads=("modv7",), writes=("modv7",))
        k.op(V, lambda e, i=gi: e.scalar_tensor_tensor(out=slot(6), in0=slot(i),
                                                       scalar=-1.0, in1=slot(i),
                                                       op0=ALU.mult, op1=ALU.max),
             writes=("modv6",))
        k.op(V, lambda e: e.tensor_scalar(out=slot(6), in0=slot(6), scalar1=0.25,
                                          scalar2=None, op0=ALU.max),
             reads=("modv6",), writes=("modv6",))
        k.op(V, lambda e, i=gi: e.tensor_tensor(out=slot(i), in0=slot(6),
                                                in1=slot(7), op=ALU.mult),
             reads=("modv6", "modv7"), writes=("modv",), inc="ve")
    mod_v = k.val("ve")

    npieces = [(0, 512), (512, 512), (1024, 128)]
    sum_banks = [PS[0], PS[1], PS[2]]
    sq_banks = [PS[3], PS[4], PS[5]]
    k.wait(V, "ve", x_in_v)
    grp_pe = 0
    for g in range(8):
        k.wait(V, "pe", grp_pe)
        cast_v = k.op(V, (lambda e, gg=g: e.tensor_copy(
            out=s("xbp"), in_=SB["xfm"][:, gg * 2 * TH:(gg + 1) * 2 * TH])),
            writes=("xbp",), inc="ve")
        k.wait(A, "ve", cast_v)
        k.wait(A, "pe", grp_pe)
        sq_v = k.op(A, lambda e: e.activation(out=s("sqp"), in_=s("xbp"),
                                              func=AF.Square),
                    writes=("sqp",), inc="ae")
        k.wait(P, "ve", cast_v)
        k.wait(P, "ae", sq_v)
        for ci in range(2):
            for bi, (off, ln) in enumerate(npieces):
                st = (g == 0 and ci == 0)
                sp = (g == 7 and ci == 1)
                mm(sum_banks[bi][0:1, 0:ln], s("onesb"),
                   SB["xbp"][:, ci * TH + off:ci * TH + off + ln], st, sp)
                mm(sq_banks[bi][0:1, 0:ln], s("onesb"),
                   SB["sqp"][:, ci * TH + off:ci * TH + off + ln], st, sp,
                   inc="pe" if (ci == 1 and bi == 2) else None)
        grp_pe = k.val("pe")
    ln1_pe = k.val("pe")
    rows = SB["rows"]
    rowC = SB["lnt0"][0:1, :]
    k.wait(V, "pe", ln1_pe)
    for bi, (off, ln) in enumerate(npieces):
        k.op(V, (lambda e, b=sum_banks[bi], o=off, l=ln:
                 e.tensor_scalar(out=rows[:, o:o + l], in0=b[0:1, 0:l],
                                 scalar1=1.0 / D, scalar2=None, op0=ALU.mult)),
             writes=("rows",))
        k.op(V, (lambda e, b=sq_banks[bi], o=off, l=ln:
                 e.tensor_scalar(out=rows[:, TH + o:TH + o + l], in0=b[0:1, 0:l],
                                 scalar1=1.0 / D, scalar2=None, op0=ALU.mult)),
             writes=("rows",))
    k.op(V, lambda e: e.tensor_tensor(out=rowC, in0=rows[:, 0:TH],
                                      in1=rows[:, 0:TH], op=ALU.mult),
         reads=("rows",), writes=("rowC",))
    k.op(V, lambda e: e.tensor_tensor(out=rows[:, TH:2 * TH],
                                      in0=rows[:, TH:2 * TH], in1=rowC,
                                      op=ALU.subtract),
         reads=("rows", "rowC"), writes=("rows",), inc="ve")
    lnv_v = k.val("ve")
    k.wait(A, "ve", lnv_v)
    k.op(A, lambda e: e.activation(out=rowC, in_=rows[:, TH:2 * TH],
                                   func=AF.Ln, bias=sv[0:1, 2:3],
                                   scale=sv[0:1, 0:1]),
         writes=("rowC",))
    k.op(A, lambda e: e.activation(out=rows[:, TH:2 * TH], in_=rowC,
                                   func=AF.Exp, bias=sv[0:1, 4:5],
                                   scale=sv[0:1, 1:2]),
         reads=("rowC",), writes=("rows",), inc="ae")
    rstd_a = k.val("ae")
    k.wait(V, "ae", rstd_a)
    k.wait(P, "ae", rstd_a)
    bcast(rows[:, 0:TH], s("mB"), TH, PS[0], "mB")
    bcast(rows[:, TH:2 * TH], s("rB"), TH, PS[1], "rB")
    bc_v = k.val("ve")
    k.wait(G, "ve", max(mod_v, bc_v))
    g_done = {}
    for c in range(KC):
        scr = s("lnt1") if c % 2 == 0 else SB["lnt0"][:]
        tg = "lnt1" if c % 2 == 0 else "lnt0"
        k.op(V, (lambda e, cc=c, sc_=scr: e.tensor_tensor(
            out=sc_, in0=SB["xfm"][:, cc * TH:(cc + 1) * TH], in1=s("mB"),
            op=ALU.subtract)), reads=("mB", "rowC", tg), writes=(tg,))
        k.op(V, (lambda e, sc_=scr: e.tensor_tensor(out=sc_, in0=sc_,
                                                    in1=s("rB"), op=ALU.mult)),
             reads=(tg, "rB"), writes=(tg,), inc="ve")
        g_done[c] = k.op(V, (lambda e, cc=c, sc_=scr: e.tensor_scalar(
            out=SB["hbf"][:, cc * TH:(cc + 1) * TH], in0=sc_,
            scalar1=SB["modv"][:, KC + cc:KC + cc + 1],
            scalar2=SB["modv"][:, cc:cc + 1],
            op0=ALU.mult, op1=ALU.add)),
            reads=("modv", tg), writes=("hbf",), inc="po")
    h_v, h_po = k.val("ve"), k.val("po")
    return dict(ldc=ldc, h_v=h_v, h_po=h_po, mod_v=mod_v, x_in_v=x_in_v,
                TPcls=TP, bcast=bcast, mm=mm, dma_s=dma_s, dma_g=dma_g, s=s)


def _build_p23(nc, k, t, pp):
    SB, PS, PT = t["SB"], t["PS"], t["PT"]
    s, mm, dma_s, dma_g = pp["s"], pp["mm"], pp["dma_s"], pp["dma_g"]
    h_v, h_po = pp["h_v"], pp["h_po"]

    npieces = [(0, 512), (512, 512), (1024, 128)]
    wst = [SB["st0"], SB["st1"]]
    wsq = [SB["stq0"], SB["stq1"]]
    vbuf = [SB["vbuf0"], SB["vbuf1"]]

    p1_pieces = []
    for wname, base in (("w1", 0), ("w2", 256), ("w3", 512)):
        wd = t[wname]
        nslab = wd.shape[0]
        per = wd.shape[2] // 512
        for sl in range(nslab):
            for pc in range(per):
                i = base + sl * per + pc
                p1_pieces.append((wd[sl][:, pc * 512:(pc + 1) * 512], i))
    assert len(p1_pieces) == 768
    p1s = [SB["p1s0"], SB["p1s1"]]
    p1pair = SB["p1s0"][:].rearrange("p (a b) -> p a b", a=1) if False else None
    p1_iter = [0]
    p1_cnt = {V: 0, A: 0}
    p1_fold = {}

    def pump_pass1(n):
        for _ in range(n):
            j = p1_iter[0]
            if j >= 384:
                return
            p1_iter[0] += 1
            if j >= 1:
                peng = "p1v" if (j - 1) % 2 == 0 else "p1a"
                k.wait(E, peng, k.val(peng))
            dma_s(p1s[0][:], p1_pieces[2 * j][0], "sp")
            ld2 = dma_s(p1s[1][:], p1_pieces[2 * j + 1][0], "sp")
            eng = V if j % 2 == 0 else A
            k.waitt(eng, ld2)
            acc = SB["acc1"] if eng == V else SB["acc2"]
            pi = p1_cnt[eng]
            p1_cnt[eng] += 1
            col = pi % 16
            if pi >= 16:
                k.wait(eng, "p1f", p1_fold[(eng, pi // 16 - 1)])
            if eng == V:
                k.op(V, (lambda e, a=acc, cl=col:
                         e.reduce_sum(out=a[:, cl:cl + 1],
                                      in_=SB["p1s0"][:].rearrange(
                                          "p (a b) -> p a b", a=2,
                                          b=512).flatten_outer_dims()
                                      if False else p1s[0][:, 0:512],
                                      axis=AX.X, apply_absolute_value=True)),
                     reads=("acc",), writes=("acc", "p1b"))
                k.op(V, (lambda e, a=acc, cl=col:
                         e.reduce_sum(out=a[:, 16 + cl:16 + cl + 1],
                                      in_=p1s[1][:], axis=AX.X,
                                      apply_absolute_value=True)),
                     reads=("acc",), writes=("acc", "p1b"), inc="p1v")
            else:
                k.op(A, (lambda e, a=acc, cl=col: e.activation(
                    out=p1s[0][:], in_=p1s[0][:], func=AF.Abs,
                    accum_out=a[:, cl:cl + 1])),
                    reads=("acc",), writes=("acc", "p1b"))
                k.op(A, (lambda e, a=acc, cl=col: e.activation(
                    out=p1s[1][:], in_=p1s[1][:], func=AF.Abs,
                    accum_out=a[:, 16 + cl:16 + cl + 1])),
                    reads=("acc",), writes=("acc", "p1b"), inc="p1a")
            if pi % 16 == 15:
                fcol = 32 + pi // 16
                if eng == A:
                    k.wait(V, "p1a", k.val("p1a"))
                k.op(V, (lambda e, a=acc, fc=fcol:
                         e.reduce_sum(out=a[:, fc:fc + 1], in_=a[:, 0:32],
                                      axis=AX.X)),
                     reads=("acc",), writes=("acc",), inc="p1f")
                p1_fold[(eng, pi // 16)] = k.val("p1f")

    k.wait(G, "ve", h_v)
    k.wait(G, "po", h_po)
    for kc in range(KC):
        park_qg = dma_g(t["x_park"][kc],
                        SB["xfm"][:, kc * TH + 128:(kc + 1) * TH], "gp")
    k.wait(P, "ve", h_v)
    k.wait(P, "po", h_po)
    k.waitt(V, park_qg)
    pp["park_qg"] = park_qg
    ogld = [None] * 48
    ogcast = [None] * 48
    ev_after = {}
    po_after = {}
    qg_after = {}
    for og in range(48):
        if og >= 2:
            k.wait(E, "po", ogcast[og - 2])
        if og < 2:
            k.wait(E, "po", k.val("po"))
        ogld[og] = dma_s(wst[og % 2][:], t["wq"][og], "sx%d" % (og % 2))
        pump_pass1(8)
        k.waitt(G, ogld[og])
        if og >= 2:
            k.wait(G, "pe", pe_after_og)
        else:
            k.wait(G, "pe", k.val("pe"))
        ogcast[og] = k.op(G, (lambda e, o=og: e.tensor_copy(
            out=wsq[o % 2][:], in_=wst[o % 2][:])), inc="po")
        k.wait(P, "po", ogcast[og])
        banks = [PS[0], PS[1], PS[2]] if og % 2 == 0 else [PS[3], PS[4], PS[5]]
        if og >= 2:
            k.wait(P, "ve", ev_after[og - 2])
        for kc in range(KC):
            for bi, (off, ln) in enumerate(npieces):
                mm(banks[bi][:, 0:ln], wsq[og % 2][:, kc * 128:(kc + 1) * 128],
                   SB["hbf"][:, kc * TH + off:kc * TH + off + ln],
                   start=(kc == 0), stop=(kc == KC - 1),
                   inc="pe" if (kc == KC - 1 and bi == 2) else None)
        pe_after_og = k.val("pe")
        k.wait(V, "pe", pe_after_og)
        if og < 16:
            hidx = og
            dsts = [(SB["qbf"][:, hidx * T:hidx * T + 384], banks[0][:, 128:512]),
                    (SB["qbf"][:, hidx * T + 384:hidx * T + 896], banks[1][:, 0:512]),
                    (SB["qbf"][:, hidx * T + 896:hidx * T + 1024], banks[2][:, 0:128])]
            for di, (d, sr) in enumerate(dsts):
                k.op(V, (lambda e, dd=d, ss=sr: e.tensor_copy(out=dd, in_=ss)),
                     writes=("qbf",), inc="ve" if di == 2 else None)
        elif og < 32:
            hidx = og - 16
            for bi, (off, ln) in enumerate(npieces):
                k.op(V, (lambda e, h_=hidx, o=off, l=ln, b=banks[bi]:
                         e.tensor_copy(out=SB["kbf"][:, h_ * TH + o:h_ * TH + o + l],
                                       in_=b[:, 0:l])), writes=("kbf",),
                     inc="ve" if bi == 2 else None)
        else:
            hidx = og - 32
            vb = vbuf[hidx % 2]
            if hidx >= 2:
                k.waitt(V, qg_after[hidx - 2])
            for bi, (off, ln) in enumerate(npieces):
                k.op(V, (lambda e, o=off, l=ln, b=banks[bi], v_=vb:
                         e.tensor_copy(out=v_[:, o:o + l], in_=b[:, 0:l])),
                     writes=("vbuf",), inc="ve" if bi == 2 else None)
            k.wait(G, "ve", k.val("ve"))
            qg_after[hidx] = dma_g(t["v_dram"][hidx], vb[:],
                                   "ga%d" % (hidx % 2))
        ev_after[og] = k.val("ve")
    qkv_done_pe = k.val("pe")
    v_all_stored = qg_after[15]

    k.wait(P, "ve", ev_after[47])
    k.wait(V, "ga0", k.val("ga0"))
    k.wait(V, "ga1", k.val("ga1"))
    for h in range(NH):
        banks = [PS[0], PS[1], PS[2]] if h % 2 == 0 else [PS[3], PS[4], PS[5]]
        for (x_sl, cosT, sinT, n_tok) in (
                (SB["qbf"][:, h * T:(h + 1) * T], s("cosQt"), s("sinQt"), T),
                (SB["kbf"][:, h * TH:(h + 1) * TH], s("cosKt"), s("sinKt"), TH)):
            pieces = [(0, 512), (512, 512)] if n_tok == T else npieces
            k.wait(P, "ve", k.val("ve"))
            pe_rot = None
            for bi, (off, ln) in enumerate(pieces):
                pe_rot = mm(banks[bi][:, 0:ln], s("Rb"), x_sl[:, off:off + ln],
                            start=True, stop=True,
                            inc="pe" if bi == len(pieces) - 1 else None)
            k.wait(G, "pe", pe_rot)
            k.wait(V, "pe", pe_rot)
            for bi, (off, ln) in enumerate(pieces):
                k.op(V, (lambda e, xs=x_sl, ct=cosT, o=off, l=ln:
                         e.tensor_tensor(out=SB["lnt0"][:, o:o + l],
                                         in0=xs[:, o:o + l], in1=ct[:, o:o + l],
                                         op=ALU.mult)),
                     reads=("qk",), writes=("lnt0",))
                k.op(V, (lambda e, b=banks[bi], st_=sinT, o=off, l=ln:
                         e.tensor_tensor(out=SB["lnt1"][:, o:o + l],
                                         in0=b[:, 0:l], in1=st_[:, o:o + l],
                                         op=ALU.mult)),
                     writes=("lnt1",))
                k.op(V, (lambda e, xs=x_sl, o=off, l=ln:
                         e.tensor_tensor(out=xs[:, o:o + l],
                                         in0=SB["lnt0"][:, o:o + l],
                                         in1=SB["lnt1"][:, o:o + l],
                                         op=ALU.add)),
                     reads=("lnt0", "lnt1"), writes=("qk",), inc="ve")
    rope_v = k.val("ve")
    pump_pass1(384)

    vfm = [SB["vfm0"], SB["vfm1"]]
    vtm = [SB["vtm0"], SB["vtm1"]]
    vin_ld = [None] * NH
    vtm_ready = [None] * NH
    k.wait(G, "ga0", k.val("ga0"))
    k.wait(G, "ga1", k.val("ga1"))
    k.wait(P, "ve", rope_v)

    vtp_last = [0, 0]

    def v_prep(h):
        if h >= 2:
            k.wait(G, "pe", unit_pv_pe[h - 2])
        else:
            k.wait(G, "ve", rope_v)
        vin_ld[h] = dma_g(vfm[h % 2][:], t["v_dram"][h], "gb%d" % (h % 2))
        k.waitt(P, vin_ld[h])
        for tc in range(TC):
            b = PT[tc % 2]
            k.wait(P, "ve", vtp_last[tc % 2])
            pe_v = k.op(P, (lambda e, h_=h, tt=tc, bb=b: e.transpose(
                bb[:, 0:128], vfm[h_ % 2][:, tt * 128:(tt + 1) * 128],
                s("identb"))), inc="pe")
            k.wait(V, "pe", pe_v)
            vtp_last[tc % 2] = k.op(V, (lambda e, h_=h, tt=tc, bb=b: e.tensor_copy(
                out=vtm[h_ % 2][:, tt * 128:(tt + 1) * 128], in_=bb[:, 0:128])),
                writes=("vtm",), inc="ve")
        vtm_ready[h] = k.val("ve")

    unit_pv_pe = {}
    v_prep(0)
    v_prep(1)
    scf_t = [s("scf"), s("scfB")]
    ppf_t = [s("ppf"), s("ppfB")]
    pnf_t = [s("pnf"), s("pnfB")]
    ptb_t = [SB["ptb"], SB["ptbB"]]
    scf_ve = [0, 0]
    exp_ae = [0, 0]
    pnf_ve = [0, 0]
    ptT_pe = [0, 0]
    ptb_ve = [0, 0]
    pv_pe = [0, 0]
    ao_ve = [0, 0]
    for h in range(NH):
        for qb in range(QB):
            p_ = (h * QB + qb) % 2
            psA = PS[0] if p_ == 0 else PS[1]
            psO = PS[4] if p_ == 0 else PS[5]
            mxc = 32 + 3 * p_
            k.wait(P, "ve", scf_ve[p_])
            pe_s = mm(psA[:, 0:256], SB["qbf"][:, h * T + qb * 128:
                                              h * T + (qb + 1) * 128],
                      SB["kbf"][:, h * TH + qb * 128:h * TH + qb * 128 + 256],
                      start=True, stop=True, inc="pe")
            k.wait(V, "pe", pe_s)
            k.wait(V, "ae", exp_ae[p_])
            scf_ve[p_] = k.op(V, (lambda e, b=psA, q_=qb, sc_=scf_t[p_]:
                              e.scalar_tensor_tensor(
                out=sc_, in0=b[:, 0:256], scalar=1.0 / math.sqrt(HD),
                in1=SB["maskt"][:, q_ * 256:(q_ + 1) * 256],
                op0=ALU.mult, op1=ALU.add)),
                writes=("scf",), inc="ve")
            k.op(V, (lambda e, sc_=scf_t[p_], c_=mxc: e.reduce_max(
                out=SB["tok8"][:, c_:c_ + 1], in_=sc_,
                axis=AX.X, negate=True)),
                reads=("scf",), writes=("mx",), inc="ve")
            ve_mx = k.val("ve")
            k.wait(A, "ve", ve_mx)
            k.wait(A, "ve", pnf_ve[p_])
            exp_ae[p_] = k.op(A, (lambda e, sc_=scf_t[p_], pp_=ppf_t[p_],
                              c_=mxc: e.activation(
                out=pp_, in_=sc_, func=AF.Exp,
                bias=SB["tok8"][:, c_:c_ + 1], scale=SB["svec"][:, 0:1])),
                inc="ae")
            k.wait(V, "ae", exp_ae[p_])
            k.op(V, (lambda e, pp_=ppf_t[p_], c_=mxc: e.reduce_sum(
                out=SB["tok8"][:, c_ + 1:c_ + 2], in_=pp_, axis=AX.X)),
                writes=("rs",))
            k.op(V, (lambda e, c_=mxc: e.reciprocal(
                out=SB["tok8"][:, c_ + 2:c_ + 3],
                in_=SB["tok8"][:, c_ + 1:c_ + 2])),
                reads=("rs",), writes=("rs",))
            k.wait(V, "pe", ptT_pe[p_])
            pnf_ve[p_] = k.op(V, (lambda e, pp_=ppf_t[p_], pn_=pnf_t[p_],
                              c_=mxc: e.tensor_scalar(
                out=pn_, in0=pp_, scalar1=SB["tok8"][:, c_ + 2:c_ + 3],
                scalar2=None, op0=ALU.mult)),
                reads=("rs",), writes=("pnf",), inc="ve")
            ve_pn = k.val("ve")
            k.wait(P, "ve", ve_pn)
            for j in range(2):
                k.wait(P, "ve", ptb_ve[j])
                pe_t = k.op(P, (lambda e, jj=j, b=PS[2 + j], pn_=pnf_t[p_]:
                            e.transpose(b[:, 0:128],
                                        pn_[:, jj * 128:(jj + 1) * 128],
                                        s("identf"))), inc="pe")
                ptT_pe[p_] = pe_t
                k.wait(V, "pe", pe_t)
                k.wait(V, "pe", pv_pe[p_])
                ptb_ve[j] = k.op(V, (lambda e, jj=j, b=PS[2 + j],
                                 pt_=ptb_t[p_]: e.tensor_copy(
                    out=pt_[:, jj * 128:(jj + 1) * 128], in_=b[:, 0:128])),
                    writes=("ptb",), inc="ve")
            ve_pt = k.val("ve")
            k.wait(P, "ve", ve_pt)
            k.wait(P, "ve", vtm_ready[h])
            k.wait(P, "ve", ao_ve[p_])
            for j in range(2):
                mm(psO[:, 0:128],
                   vtm[h % 2][:, (qb + j) * 128:(qb + j + 1) * 128],
                   ptb_t[p_][:, j * 128:(j + 1) * 128],
                   start=(j == 0), stop=(j == 1), inc="pe" if j == 1 else None)
            pv_pe[p_] = pe_pv = k.val("pe")
            k.wait(V, "pe", pe_pv)
            ao_ve[p_] = k.op(V, (lambda e, h_=h, q_=qb, b=psO: e.tensor_copy(
                out=SB["qbf"][:, h_ * T + q_ * 128:h_ * T + (q_ + 1) * 128],
                in_=b[:, 0:128])), writes=("ao",), inc="ve")
        unit_pv_pe[h] = k.val("pe")
        if h + 2 < NH:
            v_prep(h + 2)
    attn_v = k.val("ve")
    pp["attn_v"] = pp_attn = attn_v
    pp["qkv_done_pe"] = qkv_done_pe
    return pp


def nc_sq1(SB):
    return SB["xq1"][:]


def _build_p45(nc, k, t, pp):
    SB, PS, PT = t["SB"], t["PS"], t["PT"]
    s, mm, dma_s, dma_g = pp["s"], pp["mm"], pp["dma_s"], pp["dma_g"]
    sv = SB["svec"]
    rows = SB["rows"]
    attn_v = pp["attn_v"]

    k.wait(V, "p1v", 192)
    k.wait(V, "p1a", 192)
    k.wait(V, "p1f", k.val("p1f"))
    k.wait(V, "ae", k.val("ae"))
    scal = {}
    CELL0 = 2176

    def cell(name):
        if name not in scal:
            scal[name] = CELL0 + len(scal)
        return rows[:, scal[name]:scal[name] + 1]

    for wi, nm in enumerate(("s1", "s2", "s3")):
        f0 = 32 + wi * 4
        k.op(V, (lambda e, f=f0: e.reduce_sum(out=SB["acc1"][:, 44:45],
                                              in_=SB["acc1"][:, f:f + 4],
                                              axis=AX.X)),
             reads=("acc",), writes=("accV",))
        k.op(V, (lambda e, f=f0: e.reduce_sum(out=SB["acc2"][:, 44:45],
                                              in_=SB["acc2"][:, f:f + 4],
                                              axis=AX.X)),
             reads=("accG",), writes=("accG",))
        k.op(V, lambda e: e.tensor_tensor(out=SB["acc1"][:, 45:46],
                                          in0=SB["acc1"][:, 44:45],
                                          in1=SB["acc2"][:, 44:45], op=ALU.add),
             reads=("accV", "accG"), writes=("accS",), inc="ve")
        k.wait(P, "ve", k.val("ve"))
        pe_v = mm(PS[5][0:1, 0:1], s("onesf"), SB["acc1"][:, 45:46],
                  start=True, stop=True, inc="pe")
        k.wait(V, "pe", pe_v)
        k.op(V, (lambda e, c=cell(nm): e.tensor_scalar(
            out=c, in0=PS[5][0:1, 0:1], scalar1=1.0 / (FF * D), scalar2=None,
            op0=ALU.mult)), writes=("rows",), inc="ve")
        k.wait(P, "ve", k.val("ve"))
        k.op(V, (lambda e, c=cell(nm), o=cell(nm + "_pe"): e.tensor_scalar(
            out=o, in0=c, scalar1=1e-8, scalar2=None, op0=ALU.add)),
            reads=("rows",), writes=("rows",))
        k.op(V, (lambda e, i_=cell(nm + "_pe"), o=cell(nm + "_r"):
                 e.reciprocal(out=o, in_=i_)),
             reads=("rows",), writes=("rows",))
        k.op(V, (lambda e, c=cell(nm), o=cell(nm + "_1p"): e.tensor_scalar(
            out=o, in0=c, scalar1=1.0, scalar2=None, op0=ALU.add)),
            reads=("rows",), writes=("rows",))
        k.op(V, (lambda e, c=cell(nm), o=cell(nm + "_1m"): e.tensor_scalar(
            out=o, in0=c, scalar1=-1.0, scalar2=1.0, op0=ALU.mult, op1=ALU.add)),
            reads=("rows",), writes=("rows",))
        k.op(V, (lambda e, i_=cell(nm + "_1m"), o=cell(nm + "_1mr"):
                 e.reciprocal(out=o, in_=i_)),
             reads=("rows",), writes=("rows",))
        k.op(V, (lambda e, a=cell(nm + "_1p"), b=cell(nm + "_1mr"),
                 o=cell(nm + "_q"): e.tensor_tensor(out=o, in0=a, in1=b,
                                                    op=ALU.mult)),
             reads=("rows",), writes=("rows",), inc="ve")
        k.wait(A, "ve", k.val("ve"))
        k.op(A, (lambda e, i_=cell(nm + "_q"), o=cell(nm + "_ln"):
                 e.activation(out=o, in_=i_, func=AF.Ln, bias=sv[0:1, 4:5],
                              scale=sv[0:1, 0:1])),
             writes=("rowsA",), inc="ae")
        k.wait(V, "ae", k.val("ae"))
        k.op(V, (lambda e, i_=cell(nm + "_ln"), o=cell(nm + "_lam"):
                 e.tensor_scalar(out=o, in0=i_, scalar1=0.25, scalar2=None,
                                 op0=ALU.mult)),
             reads=("rows", "rowsA"), writes=("rows",), inc="ve")
    k.wait(P, "ve", k.val("ve"))
    for wi, nm in enumerate(("s1", "s2", "s3")):
        pe_v = mm(PS[5][:, 0:1], s("onesr"), cell(nm + "_r"),
                  start=True, stop=True, inc="pe")
        k.wait(V, "pe", pe_v)
        ve_v = k.op(V, (lambda e, w=wi: e.tensor_copy(out=sv[:, 5 + w:6 + w],
                                                      in_=PS[5][:, 0:1])),
                    writes=("svec",), inc="ve")
        k.wait(P, "ve", ve_v)
    sgate_v = k.val("ve")

    wst = [SB["st0"], SB["st1"]]
    wsq = [SB["stq0"], SB["stq1"]]
    xres = [SB["xres0"], SB["xres1"]]
    npc2 = [(0, 512), (512, 512)]
    k.wait(P, "ve", attn_v)
    k.wait(E, "ve", attn_v)
    k.waitt(E, pp["park_qg"])
    ogcast = [None] * KC
    evac_after = {}
    xres_ld = [None] * KC
    pe_after = 0
    for og in range(KC):
        if og >= 2:
            k.wait(E, "po", ogcast[og - 2])
            k.wait(E, "ve", evac_after[og - 2])
        else:
            k.wait(E, "po", k.val("po"))
        ld = dma_s(wst[og % 2][:], t["wo"][og], "sx%d" % (og % 2))
        xres_ld[og] = dma_s(xres[og % 2][:], t["x_park"][og], "sr%d" % (og % 2))
        k.waitt(G, ld)
        if og >= 2:
            k.wait(G, "pe", pe_after)
        else:
            k.wait(G, "pe", k.val("pe"))
        ogcast[og] = k.op(G, (lambda e, o=og: e.tensor_copy(
            out=wsq[o % 2][:], in_=wst[o % 2][:])), inc="po")
        k.wait(P, "po", ogcast[og])
        banks = [PS[0], PS[1]] if og % 2 == 0 else [PS[2], PS[3]]
        if og >= 2:
            k.wait(P, "ve", evac_after[og - 2])
        for kc in range(KC):
            for bi, (off, ln) in enumerate(npc2):
                mm(banks[bi][:, 0:ln], wsq[og % 2][:, kc * 128:(kc + 1) * 128],
                   SB["qbf"][:, kc * T + off:kc * T + off + ln],
                   start=(kc == 0), stop=(kc == KC - 1),
                   inc="pe" if (kc == KC - 1 and bi == 1) else None)
        if og == KC - 1:
            mm(PS[4][0:1, 0:1], s("onesb"), SB["qbf"][:, 0:1],
               start=True, stop=True)
            mm(PS[4][0:1, 0:1], s("onesb"), SB["qbf"][:, 0:1],
               start=True, stop=True, inc="pe")
        pe_after = k.val("pe")
        k.wait(V, "pe", pe_after)
        k.waitt(V, xres_ld[og])
        for bi, (off, ln) in enumerate(npc2):
            k.op(V, (lambda e, b=banks[bi], o=off, l=ln, c=og: e.tensor_scalar(
                out=SB["wgrp"][:, o:o + l],
                in0=b[:, 0:l], scalar1=SB["bout"][:, c:c + 1], scalar2=None,
                op0=ALU.add)), writes=("wgrp",))
            k.op(V, (lambda e, o=off, l=ln, c=og: e.scalar_tensor_tensor(
                out=SB["xmid"][:, c * T + o:c * T + o + l],
                in0=SB["wgrp"][:, o:o + l],
                scalar=SB["modv"][:, 2 * KC + c:2 * KC + c + 1],
                in1=xres[c % 2][:, o:o + l], op0=ALU.mult, op1=ALU.add)),
                reads=("wgrp", "modv"), writes=("xmid",),
                inc="ve" if bi == 1 else None)
        evac_after[og] = k.val("ve")
    xmid_v = k.val("ve")
    pp["wo_cast_po"] = ogcast[KC - 1]

    k.wait(V, "ve", xmid_v)
    cast_v = k.op(V, lambda e: e.tensor_copy(out=s("xmbf"), in_=s("xmid")),
                  writes=("xmbf",), inc="ve")
    k.wait(A, "ve", cast_v)
    sq_a = k.op(A, lambda e: e.activation(out=s("sq2"), in_=s("xmbf"),
                                          func=AF.Square),
                writes=("sq2",), inc="ae")
    k.wait(P, "ve", cast_v)
    k.wait(P, "ae", sq_a)
    for kc in range(KC):
        for bi, (off, ln) in enumerate(npc2):
            mm(PS[bi][0:1, 0:ln], s("onesb"),
               SB["xmbf"][:, kc * T + off:kc * T + off + ln],
               start=(kc == 0), stop=(kc == KC - 1))
            mm(PS[2 + bi][0:1, 0:ln], s("onesb"),
               SB["sq2"][:, kc * T + off:kc * T + off + ln],
               start=(kc == 0), stop=(kc == KC - 1),
               inc="pe" if (kc == KC - 1 and bi == 1) else None)
    ln2_pe = k.val("pe")
    k.wait(V, "pe", ln2_pe)
    for bi, (off, ln) in enumerate(npc2):
        k.op(V, (lambda e, b=PS[bi], o=off, l=ln: e.tensor_scalar(
            out=rows[:, o:o + l], in0=b[0:1, 0:l], scalar1=1.0 / D,
            scalar2=None, op0=ALU.mult)), writes=("rows",))
        k.op(V, (lambda e, b=PS[2 + bi], o=off, l=ln: e.tensor_scalar(
            out=rows[:, TH + o:TH + o + l], in0=b[0:1, 0:l], scalar1=1.0 / D,
            scalar2=None, op0=ALU.mult)), writes=("rows",))
    rowC = rows[:, 1152 + 1024:1152 + 1024 + 1]
    rowM2 = SB["acc1"][0:1, 64:64 + 64]
    k.op(V, lambda e: e.tensor_tensor(out=rows[:, 2 * TH:2 * TH + T] if False
                                      else SB["m2B"][0:1, 0:T],
                                      in0=rows[:, 0:T], in1=rows[:, 0:T],
                                      op=ALU.mult),
         reads=("rows",), writes=("m2Brow",))
    k.op(V, lambda e: e.tensor_tensor(out=rows[:, TH:TH + T],
                                      in0=rows[:, TH:TH + T],
                                      in1=SB["m2B"][0:1, 0:T], op=ALU.subtract),
         reads=("rows", "m2Brow"), writes=("rows",), inc="ve")
    v_var2 = k.val("ve")
    k.wait(A, "ve", v_var2)
    k.op(A, lambda e: e.activation(out=SB["m2B"][0:1, 0:T],
                                   in_=rows[:, TH:TH + T], func=AF.Ln,
                                   bias=sv[0:1, 2:3], scale=sv[0:1, 0:1]),
         writes=("m2Brow",))
    k.op(A, lambda e: e.activation(out=rows[:, TH:TH + T],
                                   in_=SB["m2B"][0:1, 0:T], func=AF.Exp,
                                   bias=sv[0:1, 4:5], scale=sv[0:1, 1:2]),
         reads=("m2Brow",), writes=("rowsX",), inc="ae")
    rstd2_a = k.val("ae")
    k.wait(V, "ae", rstd2_a)
    k.wait(P, "ae", rstd2_a)
    pp["bcast"](rows[:, 0:T], s("m2B"), T, PS[0], "m2B")
    pp["bcast"](rows[:, TH:TH + T], s("r2B"), T, PS[1], "r2B")
    bc2_v = k.val("ve")
    k.wait(G, "ve", bc2_v)
    g2_done = {}
    for c in range(KC):
        scr = SB["scW"][:] if c % 2 == 0 else SB["ivW"][:]
        tg = "scW" if c % 2 == 0 else "ivW"
        k.op(V, (lambda e, cc=c, sc_=scr: e.tensor_tensor(
            out=sc_, in0=SB["xmid"][:, cc * T:(cc + 1) * T], in1=s("m2B"),
            op=ALU.subtract)), reads=("m2B", tg), writes=(tg,))
        k.op(V, (lambda e, sc_=scr: e.tensor_tensor(
            out=sc_, in0=sc_, in1=s("r2B"), op=ALU.mult)),
            reads=(tg, "r2B"), writes=(tg,), inc="ve")
        g2_done[c] = k.op(V, (lambda e, cc=c, sc_=scr: e.tensor_scalar(
            out=SB["h2"][:, cc * T:(cc + 1) * T], in0=sc_,
            scalar1=SB["modv"][:, 4 * KC + cc:4 * KC + cc + 1],
            scalar2=SB["modv"][:, 3 * KC + cc:3 * KC + cc + 1],
            op0=ALU.mult, op1=ALU.add)),
            reads=("modv", tg), writes=("h2",), inc="po")
    h2_v, h2_po = k.val("ve"), k.val("po")

    k.wait(G, "ve", h2_v)
    k.wait(G, "po", h2_po)
    for kc in range(KC):
        park2 = dma_g(t["xm_park"][kc], SB["xmid"][:, kc * T:(kc + 1) * T],
                      "gp2")
    pp["park2"] = park2

    k.wait(A, "ve", h2_v)
    k.wait(A, "po", h2_po)
    k.waitt(A, pp["park2"])
    sqa2 = k.op(A, lambda e: e.activation(
        out=SB["xq1"][:].bitcast(BF16) if False else
        nc_sq1(SB), in_=s("h2"), func=AF.Square),
        writes=("sqh",), inc="ae")
    k.wait(P, "ae", sqa2)
    for kc in range(KC):
        for bi, (off, ln) in enumerate(npc2):
            mm(PS[2 + bi][0:1, 0:ln], s("onesb"),
               nc_sq1(SB)[:, kc * T + off:kc * T + off + ln],
               start=(kc == 0), stop=(kc == KC - 1),
               inc="pe" if (kc == KC - 1 and bi == 1) else None)
    rms1_pe = k.val("pe")
    k.wait(V, "pe", rms1_pe)
    for bi, (off, ln) in enumerate(npc2):
        k.op(V, (lambda e, b=PS[2 + bi], o=off, l=ln: e.tensor_scalar(
            out=rows[:, o:o + l], in0=b[0:1, 0:l], scalar1=1.0 / D,
            scalar2=None, op0=ALU.mult)), writes=("rows",), inc="ve")
    v_ms1 = k.val("ve")
    k.wait(A, "ve", v_ms1)
    k.op(A, lambda e: e.activation(out=rows[:, TH:TH + T], in_=rows[:, 0:T],
                                   func=AF.Ln, bias=sv[0:1, 3:4],
                                   scale=sv[0:1, 0:1]),
         writes=("rowsY",))
    k.op(A, lambda e: e.activation(out=rows[:, 0:T], in_=rows[:, TH:TH + T],
                                   func=AF.Exp, bias=sv[0:1, 4:5],
                                   scale=sv[0:1, 1:2]),
         reads=("rowsY",), writes=("rowsZ",), inc="ae")
    rstd1_a = k.val("ae")
    k.wait(V, "ae", rstd1_a)
    k.wait(P, "ae", rstd1_a)
    pp["bcast"](rows[:, 0:T], s("m2B"), T, PS[0], "r1B")
    r1b_v = k.val("ve")
    pp["h2_v"] = h2_v
    pp["r1b_v"] = r1b_v
    pp["sgate_v"] = sgate_v
    pp["cell"] = cell
    return pp


def _build_p5b(nc, k, t, pp):
    SB, PS = t["SB"], t["PS"]
    s, mm = pp["s"], pp["mm"]
    sv = SB["svec"]
    cell = pp["cell"]

    def r1bc(n2):
        return SB["m2B"][:].unsqueeze(1).broadcast_to([128, n2, T])

    def gbc(gt, c0, n2):
        return SB[gt][:, c0:c0 + n2].unsqueeze(2).broadcast_to([128, n2, T])

    def scbc(n2):
        return SB["scW"][:].unsqueeze(1).broadcast_to([128, n2, T])

    def quant(gt, pb, xq_name, lam_cell, wait_v):
        wg, rg, pg = SB["wgrp"], SB["remg"], SB["pg"]
        amax = [SB["scW"], SB["ivW"]]
        k.wait(V, "ve", wait_v)
        k.wait(G, "ve", wait_v)
        k.waitt(V, pp["park2"])
        k.waitt(G, pp["park2"])
        for grp in range(8):
            c0 = grp * 2
            k.op(V, (lambda e, c=c0: e.tensor_tensor(
                out=pg[:].rearrange("p (a b) -> p a b", a=2),
                in0=SB["h2"][:, c * T:(c + 2) * T].rearrange("p (a b) -> p a b", a=2),
                in1=r1bc(2), op=ALU.mult)),
                reads=("m2B",), writes=("pg",))
            k.op(V, (lambda e, c=c0: e.tensor_tensor(
                out=wg[:].rearrange("p (a b) -> p a b", a=2),
                in0=pg[:].rearrange("p (a b) -> p a b", a=2),
                in1=gbc(gt, c, 2), op=ALU.mult)),
                reads=("pg",), writes=("wgrp",))
            k.op(V, lambda e: e.scalar_tensor_tensor(
                out=pg[:], in0=wg[:], scalar=-1.0, in1=wg[:],
                op0=ALU.mult, op1=ALU.max),
                reads=("wgrp",), writes=("pg",))
            k.op(V, lambda e: e.tensor_tensor(
                out=rg[:, 0:T], in0=pg[:, 0:T], in1=pg[:, T:2 * T], op=ALU.max),
                reads=("pg",), writes=("remg",))
            if grp == 0:
                k.op(V, lambda e: e.tensor_copy(out=amax[0][:], in_=rg[:, 0:T]),
                     reads=("remg",), writes=("amax",))
            else:
                src_t, dst_t = amax[(grp + 1) % 2], amax[grp % 2]
                k.op(V, (lambda e, a=src_t, b=dst_t: e.tensor_tensor(
                    out=b[:], in0=rg[:, 0:T], in1=a[:], op=ALU.max)),
                    reads=("remg", "amax"), writes=("amax",),
                    inc="ve" if grp == 7 else None)
        amax_f = amax[7 % 2]
        k.wait(P, "ve", k.val("ve"))
        bank_ve = [0, 0]
        for j in range(8):
            b = PS[4 + j % 2]
            k.wait(P, "ve", bank_ve[j % 2])
            pe_v = k.op(P, (lambda e, jj=j, bb=b, af=amax_f: e.transpose(
                bb[:, 0:128], af[:, jj * 128:(jj + 1) * 128], s("identf"))),
                inc="pe")
            k.wait(V, "pe", pe_v)
            bank_ve[j % 2] = k.op(V, (lambda e, jj=j, bb=b: e.reduce_max(
                out=SB["tok8"][:, pb * 8 + jj:pb * 8 + jj + 1],
                in_=bb[:, 0:128], axis=AX.X)),
                writes=("tok8",), inc="ve")
        pl = lambda i: SB["tok8"][:, (pb + i) * 8:(pb + i) * 8 + 8]
        k.op(V, lambda e: e.tensor_scalar(out=pl(1), in0=pl(0), scalar1=1e-4,
                                          scalar2=None, op0=ALU.max),
             reads=("tok8",), writes=("tok8",))
        k.op(V, lambda e: e.reciprocal(out=pl(0), in_=pl(1)),
             reads=("tok8",), writes=("tok8",))
        k.op(V, lambda e: e.tensor_scalar(out=pl(2), in0=pl(0), scalar1=127.0,
                                          scalar2=None, op0=ALU.mult),
             reads=("tok8",), writes=("tok8",))
        svcol = 8 if pb == 0 else 9
        k.wait(P, "ve", k.val("ve"))
        pe_v = mm(PS[4][:, 0:1], s("onesr"), lam_cell, start=True, stop=True,
                  inc="pe")
        k.wait(V, "pe", pe_v)
        k.op(V, (lambda e: e.tensor_copy(out=sv[:, svcol:svcol + 1],
                                         in_=PS[4][:, 0:1])),
             writes=("svec",), inc="ve")
        k.op(V, (lambda e: e.tensor_scalar(out=pl(0), in0=pl(1),
                                           scalar1=1.0 / 127.0,
                                           scalar2=sv[:, svcol:svcol + 1],
                                           op0=ALU.mult, op1=ALU.mult)),
             reads=("tok8", "svec"), writes=("tok8",), inc="ve")
        k.wait(P, "ve", k.val("ve"))
        pe_v = k.op(P, (lambda e: e.transpose(PS[4][0:8, 0:128], pl(2),
                                              s("identf"))), inc="pe")
        k.wait(V, "pe", pe_v)
        ve_v = k.op(V, lambda e: e.tensor_copy(out=SB["cpad"][0:8, 0:128],
                                               in_=PS[4][0:8, 0:128]),
                    writes=("cpad",), inc="ve")
        k.wait(P, "ve", ve_v)
        for j in range(8):
            b = PS[4 + j % 2]
            k.wait(P, "ve", bank_ve[j % 2])
            pe_v = mm(b[:, 0:128], SB["sel8"][:, j * 128:(j + 1) * 128],
                      SB["cpad"][0:8, 0:128], start=True, stop=True, inc="pe")
            k.wait(V, "pe", pe_v)
            bank_ve[j % 2] = k.op(V, (lambda e, jj=j, bb=b: e.tensor_copy(
                out=SB["scW"][:, jj * 128:(jj + 1) * 128], in_=bb[:, 0:128])),
                writes=("scW",), inc="ve")
        scb_v = k.val("ve")
        k.wait(G, "ve", scb_v)
        for grp in range(8):
            c0 = grp * 2
            ve_h = [0, 0]
            ae_h = [0, 0]
            for hf in range(2):
                c = c0 + hf
                o = hf * T
                k.op(V, (lambda e, cc=c, o_=o: e.tensor_tensor(
                    out=pg[:, o_:o_ + T], in0=SB["h2"][:, cc * T:(cc + 1) * T],
                    in1=SB["m2B"][:], op=ALU.mult)),
                    reads=("m2B", "pgh%d" % hf), writes=("pgh%d" % hf,))
                k.op(V, (lambda e, cc=c, o_=o, g_=gt: e.tensor_scalar(
                    out=rg[:, o_:o_ + T], in0=pg[:, o_:o_ + T],
                    scalar1=SB[g_][:, cc:cc + 1], scalar2=None, op0=ALU.mult)),
                    reads=("pgh%d" % hf,), writes=("rgh%d" % hf,))
                k.op(V, (lambda e, o_=o: e.tensor_tensor(
                    out=wg[:, o_:o_ + T], in0=rg[:, o_:o_ + T],
                    in1=SB["scW"][:], op=ALU.mult)),
                    reads=("rgh%d" % hf, "scW"), writes=("wgh%d" % hf,))
                ve_h[hf] = k.op(V, (lambda e, o_=o: e.scalar_tensor_tensor(
                    out=rg[:, o_:o_ + T], in0=wg[:, o_:o_ + T], scalar=-1.0,
                    in1=wg[:, o_:o_ + T], op0=ALU.mult, op1=ALU.max)),
                    reads=("wgh%d" % hf,), writes=("rgh%d" % hf,), inc="ve")
            for d in range(6, -1, -1):
                for hf in range(2):
                    o = hf * T
                    k.wait(A, "ve", ve_h[hf])
                    ae_h[hf] = k.op(A, (lambda e, o_=o: e.activation(
                        out=pg[:, o_:o_ + T], in_=rg[:, o_:o_ + T],
                        func=AF.Sign, bias=-63.5, scale=1.0)),
                        reads=("rgh%d" % hf,), writes=("pgh%d" % hf,),
                        inc="ae")
                    k.wait(V, "ae", ae_h[hf])
                    ve_h[hf] = k.op(V, (lambda e, dd=d, o_=o:
                                        e.scalar_tensor_tensor(
                        out=rg[:, o_:o_ + T], in0=pg[:, o_:o_ + T],
                        scalar=-float(1 << dd) / 2.0, in1=rg[:, o_:o_ + T],
                        op0=ALU.mult, op1=ALU.add)),
                        reads=("rgh%d" % hf, "pgh%d" % hf),
                        writes=("rgh%d" % hf,), inc="ve")
            for hf in range(2):
                c = c0 + hf
                o = hf * T
                k.wait(A, "ve", ve_h[hf])
                ae_h[hf] = k.op(A, (lambda e, o_=o: e.activation(
                    out=pg[:, o_:o_ + T], in_=wg[:, o_:o_ + T], func=AF.Sign,
                    bias=0.0, scale=1.0)),
                    reads=("wgh%d" % hf,), writes=("pgh%d" % hf,), inc="ae")
                k.op(V, (lambda e, o_=o: e.tensor_scalar(
                    out=rg[:, o_:o_ + T], in0=rg[:, o_:o_ + T], scalar1=-63.5,
                    scalar2=None, op0=ALU.add)),
                    reads=("rgh%d" % hf,), writes=("rgh%d" % hf,))
                k.wait(V, "ae", ae_h[hf])
                k.op(V, (lambda e, o_=o: e.tensor_tensor(
                    out=pg[:, o_:o_ + T], in0=pg[:, o_:o_ + T],
                    in1=rg[:, o_:o_ + T], op=ALU.mult)),
                    reads=("rgh%d" % hf, "pgh%d" % hf),
                    writes=("pgh%d" % hf,))
                k.op(V, (lambda e, cc=c, o_=o, xq=xq_name: e.tensor_tensor(
                    out=SB[xq][:, cc * T:(cc + 1) * T], in0=wg[:, o_:o_ + T],
                    in1=pg[:, o_:o_ + T], op=ALU.subtract)),
                    reads=("pgh%d" % hf, "wgh%d" % hf), writes=(xq_name,),
                    inc="ve")
        return k.val("ve")

    q1_v = quant("g1t", 0, "xq1", pp["cell"]("s1_lam"), pp["r1b_v"])
    q2_v = quant("g2t", 3, "xq2", pp["cell"]("s2_lam"), q1_v)
    pp["q12_v"] = q2_v
    return pp


def _build_p6(nc, k, t, pp):
    SB, PS = t["SB"], t["PS"]
    s, mm, dma_s, dma_g = pp["s"], pp["mm"], pp["dma_s"], pp["dma_g"]
    sv = SB["svec"]

    k.wait(P, "ve", pp["q12_v"])
    for pb, dst in ((0, "i1B"), (3, "i2B")):
        pl0 = SB["tok8"][:, pb * 8:pb * 8 + 8]
        pe_v = k.op(P, (lambda e, p_=pl0: e.transpose(PS[4][0:8, 0:128], p_,
                                                      s("identf"))), inc="pe")
        k.wait(V, "pe", pe_v)
        ve_v = k.op(V, lambda e: e.tensor_copy(out=SB["cpad"][0:8, 0:128],
                                               in_=PS[4][0:8, 0:128]),
                    writes=("cpad",), inc="ve")
        k.wait(P, "ve", ve_v)
        for j in range(8):
            pe_v = mm(PS[4][:, 0:128], SB["sel8"][:, j * 128:(j + 1) * 128],
                      SB["cpad"][0:8, 0:128], start=True, stop=True, inc="pe")
            k.wait(V, "pe", pe_v)
            ve_v = k.op(V, (lambda e, jj=j, d=dst: e.tensor_copy(
                out=SB[d][:, jj * 128:(jj + 1) * 128], in_=PS[4][:, 0:128])),
                writes=(dst,), inc="ve")
            k.wait(P, "ve", ve_v)
    iB_v = k.val("ve")

    fs1 = [SB["f10"], SB["f11"]]
    fs2 = [SB["f20"], SB["f21"]]
    hm = [SB["hm0"], SB["hm1"]]
    amx_half = [SB["remg"][:, 0:1024], SB["remg"][:, 1024:2048]]
    k.op(V, lambda e: e.memset(SB["remg"][:, 0:1024], 0.0), writes=("remg",))
    k.op(V, lambda e: e.tensor_tensor(out=SB["acc1"][:, 0:FC],
                                      in0=SB["g3t"][:], in1=SB["g3t"][:],
                                      op=ALU.mult),
         reads=("acc1",), writes=("acc1",), inc="ve")
    k.wait(V, "ve", iB_v)
    k.wait(A, "ve", iB_v)
    k.wait(E, "po", pp["wo_cast_po"])
    hm_qg = {}
    evac_ve = {}
    pe_og = {}
    half_pe = {}
    add_ve = {}
    amx_po = {}
    sq_ae = {}
    ssq_pe = {}
    sign_ae = {}
    sqmax_ve = {}
    for og in range(FC):
        b1 = [PS[0], PS[1]] if og % 2 == 0 else [PS[2], PS[3]]
        b2 = [PS[2], PS[3]] if og % 2 == 0 else [PS[0], PS[1]]
        for wi, (wd, fs, fa, fb, fq, svc) in enumerate((
                (t["w1"], fs1, "fa1", "fb1", "fq1", 5),
                (t["w2"], fs2, "fa2", "fb2", "fq2", 6))):
            for hf in range(2):
                k.wait(E, "ae", sign_ae.get((og - 1, wi, hf), 0))
                ld = dma_s(fs[hf][:], wd[og][:, hf * 1024:(hf + 1) * 1024],
                           "sf%d" % (wi * 2 + hf))
                k.waitt(A, ld)
                k.wait(A, "ve", add_ve.get((og, wi, hf - 1),
                                           add_ve.get((og - 1, wi, 1), 0)))
                k.op(A, (lambda e, f_=fs[hf], d=fa, sc_=svc: e.activation(
                    out=SB[d][:], in_=f_[:], func=AF.Sign,
                    scale=sv[:, sc_:sc_ + 1], bias=sv[:, 11:12])),
                    writes=(fa,))
                ae_v = k.op(A, (lambda e, f_=fs[hf], d=fb, sc_=svc: e.activation(
                    out=SB[d][:], in_=f_[:], func=AF.Sign,
                    scale=sv[:, sc_:sc_ + 1], bias=sv[:, 12:13])),
                    writes=(fb,), inc="ae")
                sign_ae[(og, wi, hf)] = ae_v
                k.wait(V, "ae", ae_v)
                k.wait(V, "pe", half_pe.get((og - 1, wi, hf), 0))
                ve_v = k.op(V, (lambda e, a=fa, b_=fb, q=fq, h_=hf:
                                e.tensor_tensor(
                    out=SB[q][:, h_ * 1024:(h_ + 1) * 1024], in0=SB[a][:],
                    in1=SB[b_][:], op=ALU.add)),
                    reads=(fa, fb), writes=(fq,), inc="ve")
                add_ve[(og, wi, hf)] = ve_v
                k.wait(P, "ve", ve_v)
                if og >= 2:
                    k.wait(P, "ve", evac_ve.get(og - 2, 0))
                banks = b1 if wi == 0 else b2
                fqt = SB[fq]
                for kc in range(8):
                    kcg = hf * 8 + kc
                    xq = SB["xq1"] if wi == 0 else SB["xq2"]
                    for th in range(2):
                        mm(banks[th][:, 0:512],
                           fqt[:, kcg * 128:(kcg + 1) * 128],
                           xq[:, kcg * T + th * 512:kcg * T + th * 512 + 512],
                           start=(kcg == 0), stop=(kcg == KC - 1),
                           inc="pe" if (kc == 7 and th == 1) else None)
                half_pe[(og, wi, hf)] = k.val("pe")
        pe_og[og] = k.val("pe")
        k.wait(V, "pe", pe_og[og])
        for th in range(2):
            k.op(V, (lambda e, b=b1[th], t_=th: e.tensor_tensor(
                out=SB["sil0"][:, t_ * 512:(t_ + 1) * 512], in0=b[:, 0:512],
                in1=SB["i1B"][:, t_ * 512:(t_ + 1) * 512], op=ALU.mult)),
                reads=("i1B",), writes=("sil0",), inc="ve" if th == 1 else None)
        ve_r1 = k.val("ve")
        k.wait(A, "ve", ve_r1)
        ae_si = k.op(A, lambda e: e.activation(out=s("sil1"), in_=s("sil0"),
                                               func=AF.Silu),
                     writes=("sil1",), inc="ae")
        if og >= 2:
            k.waitt(V, hm_qg[og - 2])
            k.wait(V, "ae", sq_ae[og - 2])
        for th in range(2):
            k.op(V, (lambda e, b=b2[th], t_=th, h_=hm[og % 2]: e.tensor_tensor(
                out=h_[:, t_ * 512:(t_ + 1) * 512], in0=b[:, 0:512],
                in1=SB["i2B"][:, t_ * 512:(t_ + 1) * 512], op=ALU.mult)),
                reads=("i2B",), writes=("hm",),
                inc="ve" if th == 1 else None)
        k.wait(V, "ae", ae_si)
        ve_hm = k.op(V, (lambda e, h_=hm[og % 2]: e.tensor_tensor(
            out=h_[:], in0=s("sil1"), in1=h_[:], op=ALU.mult)),
            reads=("hm", "sil1"), writes=("hm",), inc="ve")
        evac_ve[og] = k.val("ve")
        k.wait(G, "ve", ve_hm)
        hm_qg[og] = dma_g(t["hmid_dr"][og], hm[og % 2][:],
                          "gh%d" % (og % 2))
        amx_po[og] = k.val("po")
        k.wait(A, "ve", ve_hm)
        k.wait(A, "pe", ssq_pe.get(og - 1, 0))
        k.wait(A, "ve", sqmax_ve.get(og - 1, 0))
        ae_sq = k.op(A, (lambda e, h_=hm[og % 2]: e.activation(
            out=SB["sqb"][:], in_=h_[:], func=AF.Square)),
            writes=("sqb",), inc="ae")
        sq_ae[og] = ae_sq
        k.wait(V, "ae", ae_sq)
        k.op(V, (lambda e, a=amx_half[og % 2], b=amx_half[(og + 1) % 2],
                 o_=og: e.scalar_tensor_tensor(
                 out=b[:], in0=SB["sqb"][:], scalar=SB["acc1"][:, o_:o_ + 1],
                 in1=a[:], op0=ALU.mult, op1=ALU.max)),
             reads=("sqb", "remg", "acc1"), writes=("remg",), inc="ve")
        sqmax_ve[og] = k.val("ve")
        k.wait(P, "ae", ae_sq)
        for th in range(2):
            mm(PS[4 + th][0:1, 0:512], s("onesb"),
               SB["sqb"][:, th * 512:(th + 1) * 512],
               start=(og == 0), stop=(og == FC - 1),
               inc="pe" if th == 1 else None)
        ssq_pe[og] = k.val("pe")
    ss3_pe = k.val("pe")
    amx3_ve = k.val("ve")
    hm_last = hm_qg[FC - 1]
    pp["ss3_pe"] = ss3_pe
    pp["amx3_ve"] = amx3_ve
    pp["amx3_final"] = amx_half[FC % 2]
    pp["hm_last"] = hm_last
    pp["p6_ae"] = k.val("ae")
    pp["p6_pe"] = k.val("pe")
    pp["p6_ve"] = k.val("ve")
    return pp


def _build_p7(nc, k, t, pp):
    SB, PS = t["SB"], t["PS"]
    s, mm, dma_s, dma_g = pp["s"], pp["mm"], pp["dma_s"], pp["dma_g"]
    sv = SB["svec"]
    rows = SB["rows"]
    cell = pp["cell"]

    k.wait(V, "pe", pp["ss3_pe"])
    for th in range(2):
        k.op(V, (lambda e, t_=th: e.tensor_scalar(
            out=rows[:, t_ * 512:(t_ + 1) * 512], in0=PS[4 + t_][0:1, 0:512],
            scalar1=1.0 / FF, scalar2=None, op0=ALU.mult)),
            writes=("rows",), inc="ve" if th == 1 else None)
    v_ms3 = k.val("ve")
    k.wait(A, "ve", v_ms3)
    k.op(A, lambda e: e.activation(out=rows[:, TH:TH + T], in_=rows[:, 0:T],
                                   func=AF.Ln, bias=sv[0:1, 3:4],
                                   scale=sv[0:1, 0:1]),
         writes=("rows3",))
    k.op(A, lambda e: e.activation(out=rows[:, 0:T], in_=rows[:, TH:TH + T],
                                   func=AF.Exp, bias=sv[0:1, 4:5],
                                   scale=sv[0:1, 1:2]),
         reads=("rows3",), writes=("rows",), inc="ae")
    rstd3_a = k.val("ae")
    k.wait(V, "ae", rstd3_a)
    k.wait(P, "ae", rstd3_a)
    pp["bcast"](rows[:, 0:T], s("rs3B"), T, PS[0], "rs3B")
    amxf = pp["amx3_final"]
    k.wait(P, "ve", pp["amx3_ve"])
    bank_ve = [0, 0]
    for j in range(8):
        b = PS[2 + j % 2]
        k.wait(P, "ve", bank_ve[j % 2])
        pe_v = k.op(P, (lambda e, jj=j, bb=b, af=amxf: e.transpose(
            bb[:, 0:128], af[:, jj * 128:(jj + 1) * 128], s("identf"))),
            inc="pe")
        k.wait(V, "pe", pe_v)
        bank_ve[j % 2] = k.op(V, (lambda e, jj=j, bb=b: e.reduce_max(
            out=SB["tok8"][:, 48 + jj:48 + jj + 1], in_=bb[:, 0:128],
            axis=AX.X)), writes=("tok8",), inc="ve")
    pl6 = SB["tok8"][:, 48:56]
    pl7 = SB["tok8"][:, 56:64]
    k.op(V, lambda e: e.nop(), inc="ve")
    k.wait(A, "ve", k.val("ve"))
    k.op(A, lambda e: e.activation(out=pl6, in_=pl6, func=AF.Sqrt,
                                   bias=sv[:, 4:5], scale=sv[:, 0:1]),
         writes=("tok8A",), inc="ae")
    k.wait(V, "ae", k.val("ae"))
    rpl = SB["cpad"][:, 96:104]
    for j in range(8):
        b = PS[2 + j % 2]
        k.wait(P, "ve", bank_ve[j % 2])
        pe_v = k.op(P, (lambda e, jj=j, bb=b: e.transpose(
            bb[:, 0:128], s("rs3B")[:, jj * 128:(jj + 1) * 128],
            s("identf"))), reads=("rs3B",), inc="pe")
        k.wait(V, "pe", pe_v)
        bank_ve[j % 2] = k.op(V, (lambda e, jj=j, bb=b, r_=rpl: e.tensor_copy(
            out=r_[:, jj:jj + 1], in_=bb[:, 0:1])),
            writes=("cpad",), inc="ve")
    k.op(V, lambda e: e.tensor_tensor(out=pl6, in0=pl6, in1=rpl,
                                      op=ALU.mult),
         reads=("tok8", "tok8A", "cpad"), writes=("tok8",), inc="ve")
    k.op(V, lambda e: e.tensor_scalar(out=pl7, in0=pl6, scalar1=1e-4,
                                      scalar2=None, op0=ALU.max),
         reads=("tok8", "tok8A"), writes=("tok8",))
    k.op(V, lambda e: e.reciprocal(out=pl6, in_=pl7),
         reads=("tok8",), writes=("tok8",))
    k.op(V, lambda e: e.tensor_scalar(out=pl6, in0=pl6, scalar1=127.0,
                                      scalar2=None, op0=ALU.mult),
         reads=("tok8",), writes=("tok8",), inc="ve")
    k.wait(P, "ve", k.val("ve"))
    pe_v = mm(PS[0][:, 0:1], s("onesr"), cell("s3_lam"), start=True, stop=True,
              inc="pe")
    k.wait(V, "pe", pe_v)
    k.op(V, lambda e: e.tensor_copy(out=sv[:, 10:11], in_=PS[0][:, 0:1]),
         writes=("svec",))
    k.op(V, lambda e: e.tensor_scalar(out=pl7, in0=pl7, scalar1=1.0 / 127.0,
                                      scalar2=sv[:, 10:11], op0=ALU.mult,
                                      op1=ALU.mult),
         reads=("tok8", "svec"), writes=("tok8",), inc="ve")
    k.wait(P, "ve", k.val("ve"))
    for pl, dst in ((pl6, "u3g"), (pl7, "i3B")):
        pe_v = k.op(P, (lambda e, p_=pl: e.transpose(PS[0][0:8, 0:128], p_,
                                                     s("identf"))), inc="pe")
        k.wait(V, "pe", pe_v)
        ve_v = k.op(V, lambda e: e.tensor_copy(out=SB["cpad"][0:8, 0:128],
                                               in_=PS[0][0:8, 0:128]),
                    writes=("cpad",), inc="ve")
        k.wait(P, "ve", ve_v)
        for j in range(8):
            b = PS[2 + j % 2]
            k.wait(P, "ve", bank_ve[j % 2])
            pe_v = mm(b[:, 0:128], SB["sel8"][:, j * 128:(j + 1) * 128],
                      SB["cpad"][0:8, 0:128], start=True, stop=True, inc="pe")
            k.wait(V, "pe", pe_v)
            bank_ve[j % 2] = k.op(V, (lambda e, jj=j, bb=b, d=dst: e.tensor_copy(
                out=SB[d][:, jj * 128:(jj + 1) * 128], in_=bb[:, 0:128])),
                writes=(dst,), inc="ve")
    k.op(V, lambda e: e.tensor_tensor(out=s("rs3B"), in0=s("rs3B"),
                                      in1=s("u3g"), op=ALU.mult),
         reads=("rs3B", "u3g"), writes=("rs3B",), inc="ve")
    rs3_v = k.val("ve")

    hin = [SB["hin0"], SB["hin1"]]
    k.wait(V, "ve", rs3_v)
    k.wait(V, "gh0", k.val("gh0"))
    k.wait(V, "gh1", k.val("gh1"))
    k.wait(E, "gh0", k.val("gh0"))
    k.wait(E, "gh1", k.val("gh1"))
    casc_ve = {}
    for kc in range(FC):
        if kc >= 2:
            k.wait(E, "ve", casc_ve[kc - 2])
        ld = dma_s(hin[kc % 2][:], t["hmid_dr"][kc], "sh%d" % (kc % 2))
        k.waitt(V, ld)
        u3, r3, p3 = SB["u3g"], SB["rem3"], SB["p3"]
        HF = ((0, 512), (512, 512))
        ve_h = [0, 0]
        ae_h = [0, 0]
        for hf, (o, l) in enumerate(HF):
            k.op(V, (lambda e, c=kc, h_=hin[kc % 2], o_=o, l_=l:
                     e.scalar_tensor_tensor(
                out=u3[:, o_:o_ + l_], in0=h_[:, o_:o_ + l_],
                scalar=SB["g3t"][:, c:c + 1], in1=SB["rs3B"][:, o_:o_ + l_],
                op0=ALU.mult, op1=ALU.mult)),
                reads=("u3g%d" % hf, "rs3B"), writes=("u3g%d" % hf,))
            ve_h[hf] = k.op(V, (lambda e, o_=o, l_=l: e.scalar_tensor_tensor(
                out=r3[:, o_:o_ + l_], in0=u3[:, o_:o_ + l_], scalar=-1.0,
                in1=u3[:, o_:o_ + l_], op0=ALU.mult, op1=ALU.max)),
                reads=("u3g%d" % hf,), writes=("rem3%d" % hf,), inc="ve")
        for d in range(6, -1, -1):
            for hf, (o, l) in enumerate(HF):
                k.wait(A, "ve", ve_h[hf])
                ae_h[hf] = k.op(A, (lambda e, o_=o, l_=l: e.activation(
                    out=p3[:, o_:o_ + l_], in_=r3[:, o_:o_ + l_],
                    func=AF.Sign, bias=-63.5, scale=1.0)),
                    reads=("rem3%d" % hf,), writes=("p3%d" % hf,), inc="ae")
                k.wait(V, "ae", ae_h[hf])
                ve_h[hf] = k.op(V, (lambda e, dd=d, o_=o, l_=l:
                                    e.scalar_tensor_tensor(
                    out=r3[:, o_:o_ + l_], in0=p3[:, o_:o_ + l_],
                    scalar=-float(1 << dd) / 2.0, in1=r3[:, o_:o_ + l_],
                    op0=ALU.mult, op1=ALU.add)),
                    reads=("rem3%d" % hf, "p3%d" % hf),
                    writes=("rem3%d" % hf,), inc="ve")
        for hf, (o, l) in enumerate(HF):
            k.wait(A, "ve", ve_h[hf])
            ae_h[hf] = k.op(A, (lambda e, o_=o, l_=l: e.activation(
                out=p3[:, o_:o_ + l_], in_=u3[:, o_:o_ + l_], func=AF.Sign,
                bias=0.0, scale=1.0)),
                reads=("u3g%d" % hf,), writes=("p3%d" % hf,), inc="ae")
            k.op(V, (lambda e, o_=o, l_=l: e.tensor_scalar(
                out=r3[:, o_:o_ + l_], in0=r3[:, o_:o_ + l_], scalar1=-63.5,
                scalar2=None, op0=ALU.add)),
                reads=("rem3%d" % hf,), writes=("rem3%d" % hf,))
            k.wait(V, "ae", ae_h[hf])
            k.op(V, (lambda e, o_=o, l_=l: e.tensor_tensor(
                out=p3[:, o_:o_ + l_], in0=p3[:, o_:o_ + l_],
                in1=r3[:, o_:o_ + l_], op=ALU.mult)),
                reads=("rem3%d" % hf, "p3%d" % hf), writes=("p3%d" % hf,))
            k.op(V, (lambda e, c=kc, o_=o, l_=l: e.tensor_tensor(
                out=SB["xq3"][:, c * T + o_:c * T + o_ + l_],
                in0=u3[:, o_:o_ + l_], in1=p3[:, o_:o_ + l_],
                op=ALU.subtract)),
                reads=("p3%d" % hf, "u3g%d" % hf), writes=("xq3",), inc="ve")
        casc_ve[kc] = k.val("ve")
    xq3_v = k.val("ve")

    w3s = [SB["w30"], SB["w31"]]
    wq3 = [SB["wq30"], SB["wq31"]]
    xm = [SB["xm0"], SB["xm1"]]
    k.wait(E, "ae", pp["p6_ae"])
    k.wait(E, "pe", pp["p6_pe"])
    k.wait(E, "ve", pp["p6_ve"])
    k.waitt(E, pp["park2"])
    sgn3_ae = {}
    add3_ve = {}
    pe_g = {}
    evac3_ve = {}
    pe_oc = {}
    tr_ve = [0, 0]
    os1_qg = [0, 0]
    for oc in range(KC):
        banks = [PS[0], PS[1]] if oc % 2 == 0 else [PS[2], PS[3]]
        if oc >= 2:
            k.wait(E, "ve", evac3_ve[oc - 2])
        xm_ld = dma_s(xm[oc % 2][:], t["xm_park"][oc], "sr%d" % (oc % 2))
        if oc >= 2:
            k.wait(P, "ve", evac3_ve[oc - 2])
        for pc in range(8):
            g = oc * 8 + pc
            if g < 2:
                k.wait(E, "po", k.val("po"))
            k.wait(E, "ae", sgn3_ae.get(g - 2, 0))
            ld = dma_s(w3s[g % 2][:], t["w3"][oc][:, pc * 1024:(pc + 1) * 1024],
                       "sx%d" % (g % 2))
            k.waitt(A, ld)
            k.wait(A, "ve", add3_ve.get(g - 1, 0))
            k.op(A, (lambda e, f_=w3s[g % 2]: e.activation(
                out=s("wa3"), in_=f_[:], func=AF.Sign, scale=sv[:, 7:8],
                bias=sv[:, 11:12])), writes=("wa3",))
            ae_v = k.op(A, (lambda e, f_=w3s[g % 2]: e.activation(
                out=s("wb3"), in_=f_[:], func=AF.Sign, scale=sv[:, 7:8],
                bias=sv[:, 12:13])), writes=("wb3",), inc="ae")
            sgn3_ae[g] = ae_v
            k.wait(V, "ae", ae_v)
            k.wait(V, "pe", pe_g.get(g - 2, 0))
            ve_v = k.op(V, (lambda e, q=wq3[g % 2]: e.tensor_tensor(
                out=q[:], in0=s("wa3"), in1=s("wb3"), op=ALU.add)),
                reads=("wa3", "wb3"), writes=("wq3",), inc="ve")
            add3_ve[g] = ve_v
            k.wait(P, "ve", ve_v)
            k.wait(P, "ve", casc_ve[pc * 8 + 7])
            for kc in range(8):
                kcg = pc * 8 + kc
                for th in range(2):
                    mm(banks[th][:, 0:512],
                       wq3[g % 2][:, kc * 128:(kc + 1) * 128],
                       SB["xq3"][:, kcg * T + th * 512:kcg * T + th * 512 + 512],
                       start=(kcg == 0), stop=(kcg == FC - 1),
                       inc="pe" if (kc == 7 and th == 1) else None)
            pe_g[g] = k.val("pe")
        pe_oc[oc] = k.val("pe")
        k.wait(V, "pe", pe_oc[oc])
        k.waitt(V, xm_ld)
        for th in range(2):
            k.op(V, (lambda e, b=banks[th], t_=th, c=oc: e.scalar_tensor_tensor(
                out=SB["os0"][:, t_ * 512:(t_ + 1) * 512], in0=b[:, 0:512],
                scalar=SB["modv"][:, 5 * KC + c:5 * KC + c + 1],
                in1=SB["i3B"][:, t_ * 512:(t_ + 1) * 512],
                op0=ALU.mult, op1=ALU.mult)),
                reads=("i3B", "modv"), writes=("os0",))
            k.op(V, (lambda e, t_=th, c=oc: e.tensor_tensor(
                out=SB["os0"][:, t_ * 512:(t_ + 1) * 512],
                in0=SB["os0"][:, t_ * 512:(t_ + 1) * 512],
                in1=xm[c % 2][:, t_ * 512:(t_ + 1) * 512], op=ALU.add)),
                reads=("os0",), writes=("os0",), inc="ve" if th == 1 else None)
        evac3_ve[oc] = k.val("ve")
        k.wait(P, "ve", evac3_ve[oc])
        for j in range(QB):
            b = PS[4 + j % 2]
            k.wait(P, "ve", tr_ve[j % 2])
            pe_v = k.op(P, (lambda e, jj=j, bb=b: e.transpose(
                bb[:, 0:128], SB["os0"][:, jj * 128:(jj + 1) * 128],
                s("identf"))), inc="pe")
            k.wait(V, "pe", pe_v)
            k.waitt(V, os1_qg[j % 2])
            tr_ve[j % 2] = k.op(V, (lambda e, jj=j, bb=b: e.tensor_copy(
                out=SB["os1"][:, (jj % 2) * 128:(jj % 2) * 128 + 128],
                in_=bb[:, 0:128])), writes=("os1",), inc="ve")
            k.wait(G, "ve", tr_ve[j % 2])
            os1_qg[j % 2] = dma_g(
                t["out"][j][:, oc * 128:(oc + 1) * 128],
                SB["os1"][:, (j % 2) * 128:(j % 2) * 128 + 128],
                "go%d" % (j % 2))
    k.wait(G, "go0", k.val("go0"))
    k.wait(G, "go1", k.val("go1"))
    return pp


_CACHE = {}


def _get_program():
    if "nc" in _CACHE:
        return _CACHE["nc"]
    nc = bass.Bass("TRN2", target_bir_lowering=False,
                   detect_race_conditions=False)
    k = K()
    t = _alloc(nc)
    import contextlib
    stack = contextlib.ExitStack()
    pp = _build(nc, k, t)
    pp = _build_p23(nc, k, t, pp)
    pp = _build_p45(nc, k, t, pp)
    pp = _build_p5b(nc, k, t, pp)
    pp = _build_p6(nc, k, t, pp)
    pp = _build_p7(nc, k, t, pp)
    for name in sorted(k.semv):
        k.sems[name] = stack.enter_context(nc.semaphore(name))
    with nc.Block() as block:
        @block.sync
        def _(eng):
            for f in k.q["sync"]:
                f(eng)

        @block.tensor
        def _(eng):
            for f in k.q["tensor"]:
                f(eng)

        @block.vector
        def _(eng):
            for f in k.q["vector"]:
                f(eng)

        @block.scalar
        def _(eng):
            for f in k.q["scalar"]:
                f(eng)

        @block.gpsimd
        def _(eng):
            for f in k.q["gpsimd"]:
                f(eng)
    stack.close()
    _CACHE["nc"] = nc
    return nc


def kernel(**inputs):
    in_maps = _host_prepare(
        inputs["L0_attn_x"], inputs["L0_attn_W_qkv"], inputs["L0_attn_W_out"],
        inputs["L0_attn_b_out"], inputs["L0_ffn_w1"], inputs["L0_ffn_g1"],
        inputs["L0_ffn_w2"], inputs["L0_ffn_g2"], inputs["L0_ffn_w3"],
        inputs["L0_ffn_g3"], inputs["L0_c"], inputs["L0_wmod"])
    nc = _get_program()
    res = run_bass_kernel_spmd(nc, in_maps, core_ids=list(range(NC)))
    outs = [np.asarray(r["out"], dtype=np.float32).reshape(T, D)
            for r in res.results]
    return np.concatenate(outs, axis=0).reshape(1, S, D)
